# version 02
# baseline (speedup 1.0000x reference)
import numpy as np
import ml_dtypes

import concourse.bass as bass
import concourse.mybir as mybir
from concourse.bass_utils import run_bass_kernel_spmd

F32 = mybir.dt.float32
BF16 = mybir.dt.bfloat16
AF = mybir.ActivationFunctionType
ALU = mybir.AluOpType
AX = mybir.AxisListType

T = 2048
D = 1024
NDT = D // 128
SL = 512
NTC = T // SL
MAGIC = 8388608.0

FUSE_GROUPS = [[0], [1], [2], [3], [4], [5], [6], [7]]
DEBUG_TAPS = False



class Seq:
    ENGINES = ("sync", "scalar", "vector", "tensor", "gpsimd")

    def __init__(self, nc):
        self.nc = nc
        self.steps = []
        self.n = 0

    def _add(self, eng, inc, fn):
        self.steps.append((eng, self.n, inc, fn))
        self.n += inc

    def dma(self, out, in_, q="sync"):
        self._add(q, 16, lambda E, o=out, i=in_: E.dma_start(out=o, in_=i))

    def dmaT(self, out, in_, q="sync"):
        self._add(q, 16, lambda E, o=out, i=in_: E.dma_start_transpose(o, i))

    def mm(self, out, lhsT, rhs, start=True, stop=True, tile_position=None):
        def f(E, o=out, l=lhsT, r=rhs, st=start, sp=stop, tp=tile_position):
            return E.matmul(o, l, r, start=st, stop=sp, tile_position=tp)
        self._add("tensor", 1, f)

    def trans(self, out, in_, ident):
        self._add("tensor", 1, lambda E, o=out, i=in_, d=ident: E.transpose(o, i, d))

    def act(self, out, in_, func, bias=0.0, scale=1.0, accum_out=None):
        def f(E, o=out, i=in_, fu=func, b=bias, s=scale, a=accum_out):
            return E.activation(o, i, fu, bias=b, scale=s, accum_out=a)
        self._add("scalar", 1, f)

    def tt(self, out, a, b, op, eng="vector"):
        self._add(eng, 1, lambda E, o=out, x=a, y=b, p=op: E.tensor_tensor(o, x, y, p))

    def ts(self, out, in0, s1, s2, op0, op1=ALU.bypass, accum_out=None, eng="vector"):
        def f(E, o=out, i=in0, a=s1, b=s2, p0=op0, p1=op1, ac=accum_out):
            return E.tensor_scalar(o, i, a, b, p0, p1, accum_out=ac)
        self._add(eng, 1, f)

    def tss(self, out, in0, s1, op, eng="vector"):
        self._add(eng, 1, lambda E, o=out, i=in0, a=s1, p=op: E.tensor_single_scalar(o, i, a, p))

    def stt(self, out, in0, scalar, in1, op0, op1, eng="vector"):
        def f(E, o=out, i=in0, s=scalar, j=in1, p0=op0, p1=op1):
            return E.scalar_tensor_tensor(o, i, s, j, p0, p1)
        self._add(eng, 1, f)

    def red(self, out, in_, op, axis=AX.X, absval=None, negate=None, eng="vector"):
        def f(E, o=out, i=in_, p=op, a=axis, ab=absval, ng=negate):
            return E.tensor_reduce(o, i, a, p, apply_absolute_value=ab, negate=ng)
        self._add(eng, 1, f)

    def copy(self, out, in_, eng="vector"):
        self._add(eng, 1, lambda E, o=out, i=in_: E.tensor_copy(o, i))

    def acopy(self, out, in_, bias=0.0, scale=1.0):
        if isinstance(bias, (int, float)):
            self.act(out, in_, AF.Copy, bias=bias, scale=scale)
        else:
            assert scale == 1.0
            self.ts(out, in_, bias, None, ALU.add)

    def recip(self, out, in_):
        self._add("vector", 1, lambda E, o=out, i=in_: E.reciprocal(o, i))

    def scan(self, out, d0, d1, initial, op0, op1, eng="vector"):
        def f(E, o=out, a=d0, b=d1, ini=initial, p0=op0, p1=op1):
            return E.tensor_tensor_scan(o, a, b, ini, p0, p1)
        self._add(eng, 1, f)

    def memset(self, ap, val, eng="vector"):
        self._add(eng, 1, lambda E, a=ap, v=val: E.memset(a, v))

    def affine_sel(self, out, in_, pattern, cmp, fill, base=0, cm=1):
        def f(E, o=out, i=in_, p=pattern, c=cmp, fl=fill, b=base, m=cm):
            return E.affine_select(o, i, p, c, fl, base=b, channel_multiplier=m)
        self._add("gpsimd", 1, f)

    def build(self, sem):
        nc = self.nc
        with nc.Block() as block:
            for engname in self.ENGINES:
                def section(E, engname=engname):
                    for (e, base, inc, fn) in self.steps:
                        if e == engname:
                            E.wait_ge(sem, base)
                            fn(E).then_inc(sem, inc)
                    E.wait_ge(sem, self.n)
                getattr(block, engname)(section)



def _rope_tables(base):
    inv = 1.0 / (base ** (np.arange(0, 32, dtype=np.float64) / 32.0))
    t = np.arange(T, dtype=np.float64)
    ang = t[None, :] * inv[:, None]
    cos = np.concatenate([np.cos(ang)] * 2, 0)
    sin = np.concatenate([np.sin(ang)] * 2, 0)
    s = 8.0 ** -0.5
    cosf = (np.concatenate([cos, cos], 0) * s).astype(ml_dtypes.bfloat16)
    sinf = (np.concatenate([sin, sin], 0) * s).astype(ml_dtypes.bfloat16)
    return cosf, sinf


def _cols(a, n=None):
    a = np.asarray(a, dtype=np.float32).reshape(-1)
    n = n or (a.size // 128)
    return np.ascontiguousarray(a.reshape(n, 128).T)


def make_consts():
    c = {}
    c["c_id_bf"] = np.eye(128, dtype=ml_dtypes.bfloat16)
    c["c_id_f32"] = np.eye(128, dtype=np.float32)
    c["c_cols_bf"] = np.concatenate(
        [np.ones((128, 1), np.float32), np.full((128, 1), 1.0 / 1024.0, np.float32)],
        axis=1).astype(ml_dtypes.bfloat16)
    c["c_ones_row_f32"] = np.ones((1, 128), dtype=np.float32)
    c["c_misc_cols"] = np.ascontiguousarray(np.broadcast_to(
        np.array([1e-5, 1e-6, 1.0, 0.0], np.float32)[None, :], (128, 4)))
    c["c_ones_row_bf"] = np.ones((1, 128), dtype=ml_dtypes.bfloat16)
    c["c_cos_l0"], c["c_sin_l0"] = _rope_tables(10000.0)
    c["c_cos_l3"], c["c_sin_l3"] = _rope_tables(1000000.0)
    p = np.arange(128)[:, None]
    qt = np.arange(16)[None, :]
    c["c_rowrecip"] = (1.0 / (qt * 128 + p + 1)).astype(np.float32)
    m = np.arange(640)[None, :]
    c["c_e3shift_bf"] = np.where(p - m + 511 >= 0, 0.0, -2.0e30).astype(ml_dtypes.bfloat16)
    return c


def make_derived(inp):
    d = {}
    tr = lambda a: np.ascontiguousarray(np.asarray(a, dtype=np.float32).T)
    asf = lambda a: np.ascontiguousarray(np.asarray(a, dtype=np.float32))
    for w in ("Wq", "Wk", "Wv", "Wo"):
        d[f"d_L0_{w}T"] = tr(inp[f"L0_attn_{w}"])
    biases = np.asarray(inp["L0_attn_biases"], dtype=np.float32)
    W = 1536
    pp = np.arange(128)[:, None]
    mm_ = np.arange(W)[None, :]
    v = pp - mm_ + 1023
    es = biases[:, np.clip(v, 0, 512)]
    es = np.where(v[None] < 0, -2.0e30, np.where(v[None] > 512, 0.0, es))
    d["d_eshift_l0"] = np.ascontiguousarray(es.astype(np.float32))
    d["d_tau_bc"] = np.ascontiguousarray(
        np.broadcast_to(np.asarray(inp["L0_attn_tau"], np.float32)[None, :], (128, 16)))
    d["d_L0f_wr_cols"] = _cols(inp["L0_ffn_w_router"])
    d["d_L0f_b1_cols"] = _cols(inp["L0_ffn_b1"])
    d["d_L0f_b2_cols"] = _cols(inp["L0_ffn_b2"])
    d["d_L1_conv_cols"] = np.ascontiguousarray(
        np.asarray(inp["L1_attn_conv_w"], np.float32)[:, 0, :]
        .reshape(16, 128, 4).transpose(1, 0, 2).reshape(128, 64))
    d["d_L1_convb_cols"] = _cols(inp["L1_attn_conv_b"])
    d["d_L1_bdt_cols"] = _cols(inp["L1_attn_b_dt"])
    d["d_L1_D_cols"] = _cols(inp["L1_attn_D"])
    d["d_L1_alog_cols"] = np.ascontiguousarray(
        np.asarray(inp["L1_attn_A_log"], np.float32)
        .reshape(16, 128, 16).transpose(1, 0, 2).reshape(128, 256))
    upw = np.asarray(inp["L1_ffn_up_weights"], dtype=np.float32)
    dnw = np.asarray(inp["L1_ffn_down_weights"], dtype=np.float32)
    d["d_bf_up1"] = _bf_factors_g1(upw)
    d["d_bf_up2"] = _bf_factors_g2(upw)
    d["d_bf_dn1"] = _bf_factors_g1(dnw)
    d["d_bf_dn2"] = _bf_factors_g2(dnw)
    fp = np.arange(4096)
    f_of_fp = (fp % 64) * 64 + (fp // 64)
    d["d_bexp_cols"] = _cols(inp["L1_ffn_b_exp"])
    d["d_upbiasP_cols"] = _cols(np.asarray(inp["L1_ffn_up_bias"], np.float32)[f_of_fp])
    d["d_W_conP"] = np.ascontiguousarray(
        np.asarray(inp["L1_ffn_W_con"], np.float32)[f_of_fp, :])
    d["d_bcon_cols"] = _cols(inp["L1_ffn_b_con"])
    d["d_dbias_cols"] = _cols(inp["L1_ffn_down_bias"])
    Win = np.asarray(inp["L2_attn_W_in"], dtype=np.float32)
    d["d_L2_WinT_half"] = tr(Win[D:2 * D, :])
    d["d_L2_binh_cols"] = _cols(np.asarray(inp["L2_attn_b_in"], np.float32)[D:2 * D])
    d["d_L2_WgateT"] = tr(inp["L2_attn_W_gate"])
    d["d_L2_bg_cols"] = _cols(inp["L2_attn_b_gate"])
    d["d_L2_A"] = asf(inp["L2_attn_A"])
    d["d_L2_AT"] = tr(inp["L2_attn_A"])
    d["d_L2_BmT"] = tr(inp["L2_attn_Bm"])
    d["d_L2_CmT"] = asf(inp["L2_attn_Cm"]).T.copy()
    d["d_L2_WoutT"] = tr(inp["L2_attn_W_out"])
    d["d_L2_bo_cols"] = _cols(inp["L2_attn_b_out"])
    d["d_L3_ns_cols"] = _cols(inp["L3_attn_norm_scale"])
    d["d_L3f_b1_cols"] = np.ascontiguousarray(
        np.asarray(inp["L3_ffn_b1"], np.float32).reshape(8, 4, 128).transpose(0, 2, 1))
    return d


def _dev_layout(out):
    dev = np.empty((128, 6 * 32 * 64), dtype=np.float32)
    for s in range(6):
        for k in range(64):
            kb, kh = k % 2, k // 2
            dev[kb * 64:(kb + 1) * 64, s * 2048 + kh * 64:s * 2048 + (kh + 1) * 64] = out[s, k]
    return np.ascontiguousarray(dev)


def _bf_factors_g1(wts):
    out = np.empty((6, 64, 64, 64), dtype=np.float32)
    i = np.arange(64)[None, :]
    j = np.arange(64)[:, None]
    for s in range(6):
        sig = 1 << s
        w = np.asarray(wts[s]).reshape(2048 // sig, sig, 2, 2)
        for k in range(64):
            nb = k * (1 << (5 - s)) + (j >> (s + 1))
            low = np.broadcast_to(i & (sig - 1), (64, 64))
            r = np.broadcast_to((i >> s) & 1, (64, 64))
            c = np.broadcast_to((j >> s) & 1, (64, 64))
            nbb = np.broadcast_to(nb, (64, 64))
            out[s, k] = w[nbb, low, r, c]
    return _dev_layout(out)


def _bf_factors_g2(wts):
    out = np.empty((6, 64, 64, 64), dtype=np.float32)
    i = np.arange(64)[None, :]
    j = np.arange(64)[:, None]
    for sp in range(6):
        s = sp + 6
        sig = 1 << s
        w = np.asarray(wts[s]).reshape(4096 // (2 * sig), sig, 2, 2)
        for kp in range(64):
            nb = np.broadcast_to(j >> (sp + 1), (64, 64))
            low = np.broadcast_to(kp + 64 * (i & ((1 << sp) - 1)), (64, 64))
            r = np.broadcast_to((i >> sp) & 1, (64, 64))
            c = np.broadcast_to((j >> sp) & 1, (64, 64))
            out[sp, kp] = w[nb, low, r, c]
    return _dev_layout(out)



class Prog:
    def __init__(self, sublayers):
        self.nc = bass.Bass("TRN2", target_bir_lowering=False,
                            dynamic_dma_scratch_size=16384)
        self.S = Seq(self.nc)
        self.sub = sublayers
        self.dram = {}
        self.used = []
        self._scr = 0
        self._alloc()

    def inp(self, name, shape, dtype=F32):
        if name not in self.dram:
            self.dram[name] = self.nc.dram_tensor(name, list(shape), dtype,
                                                  kind="ExternalInput").ap()
            self.used.append(name)
        return self.dram[name]

    def tap(self, name, ap):
        if not DEBUG_TAPS:
            return
        sh = list(ap.shape)
        out = self.nc.dram_tensor("tap_" + name, sh, ap.dtype, kind="ExternalOutput").ap()
        self.S.dma(out, ap)

    def scratch(self, shape, dtype):
        self._scr += 1
        return self.nc.dram_tensor(f"scr{self._scr}", list(shape), dtype,
                                   kind="Internal").ap()

    def _alloc(self):
        nc = self.nc
        a = lambda nm, sh, dt: nc.alloc_sbuf_tensor(nm, sh, dt).ap()
        self.x = a("x", [128, NDT * T], F32)
        self.xn = a("xn", [128, NDT * T], BF16)
        self.sA = a("sA", [128, 24576], BF16)
        self.sB = a("sB", [128, 16384], BF16)
        self.sF = a("sF", [128, 3072], F32)
        self.sW = a("sW", [128, 3072], BF16)
        self.sC = a("sC", [128, 2048], F32)
        self.sR = a("sR", [128, 512], F32)
        self.k_id_bf = a("k_id_bf", [128, 128], BF16)
        self.k_id_f32 = a("k_id_f32", [128, 128], F32)
        self.k_cols = a("k_cols", [128, 2], BF16)
        self.k_row_f32 = a("k_row_f32", [1, 128], F32)
        self.k_misc = a("k_misc", [128, 4], F32)
        self.k_row_bf = a("k_row_bf", [1, 128], BF16)
        self.pA = nc.alloc_psum_tensor("pA", [128, 2048], F32).ap()
        self.pB = nc.alloc_psum_tensor("pB", [128, 1024], F32).ap()
        self.pC = nc.alloc_psum_tensor("pC", [128, 512], F32).ap()
        self.pT = nc.alloc_psum_tensor("pT", [128, 1024], BF16).ap()

    def load_consts(self):
        S = self.S
        S.dma(self.k_id_bf[:], self.inp("c_id_bf", [128, 128], BF16)[:])
        S.dma(self.k_id_f32[:], self.inp("c_id_f32", [128, 128], F32)[:])
        S.dma(self.k_cols[:], self.inp("c_cols_bf", [128, 2], BF16)[:])
        S.dma(self.k_row_f32[:], self.inp("c_ones_row_f32", [1, 128], F32)[:])
        S.dma(self.k_misc[:], self.inp("c_misc_cols", [128, 4], F32)[:])
        S.dma(self.k_row_bf[:], self.inp("c_ones_row_bf", [1, 128], BF16)[:])

    def xt(self, i, lo=0, hi=T):
        return self.x[:, i * T + lo:i * T + hi]

    def xnt(self, i, lo=0, hi=T):
        return self.xn[:, i * T + lo:i * T + hi]

    def load_wchunks(self, dst, dram2d, rows, cols, col0=0, ncols=None):
        w = ncols if ncols is not None else cols
        nch = rows // 128
        src = dram2d.rearrange("(c p) o -> p c o", p=128)[:, :, col0:col0 + w]
        self.S.dma(dst[:, 0:nch * w], src, q="gpsimd")

    def prelude(self, skip_xn=False):
        S = self.S
        xin = self.inp("xin", [D, T], F32)
        S.dma(self.x[:], xin.rearrange("(i p) t -> p i t", p=128))
        if not skip_xn:
            for i in range(NDT):
                S.copy(self.xnt(i), self.xt(i))

    def epilogue(self):
        S = self.S
        xout = self.nc.dram_tensor("xout", [D, T], F32, kind="ExternalOutput").ap()
        S.dma(xout.rearrange("(i p) t -> p i t", p=128), self.x[:])

    def bcast_row(self, psum_dst, row_ap):
        self.S.mm(psum_dst, self.k_row_f32[:], row_ap, start=True, stop=True)

    def postln(self, eps=1e-5):
        S = self.S
        inv = self.k_cols[:, 1:2]
        for i in range(NDT):
            S.copy(self.sB[:, i * T:(i + 1) * T], self.xt(i))
        for tc in range(NTC):
            lo = tc * SL
            for i in range(NDT):
                S.mm(self.pA[0:1, lo:lo + SL], inv,
                     self.sB[:, i * T + lo:i * T + lo + SL],
                     start=(i == 0), stop=(i == NDT - 1))
        mean_row = self.sF[0:1, 0:T]
        S.acopy(mean_row, self.pA[0:1, 0:T])
        for i in range(NDT):
            S.act(self.sB[:, i * T:(i + 1) * T], self.xt(i), AF.Square)
        for tc in range(NTC):
            lo = tc * SL
            for i in range(NDT):
                S.mm(self.pA[0:1, lo:lo + SL], inv,
                     self.sB[:, i * T + lo:i * T + lo + SL],
                     start=(i == 0), stop=(i == NDT - 1))
        for tc in range(NTC):
            lo = tc * SL
            t1 = self.sF[0:1, T + 0:T + SL]
            t2 = self.sF[0:1, T + SL:T + 2 * SL]
            mch = self.sF[0:1, lo:lo + SL]
            S.act(t1, mch, AF.Square)
            S.tt(t1, self.pA[0:1, lo:lo + SL], t1, ALU.subtract)
            S.act(t2, t1, AF.Ln, bias=self.k_misc[0:1, 0:1])
            S.act(t1, t2, AF.Exp, scale=-0.5)
            S.tt(t2, mch, t1, ALU.mult)
            self.bcast_row(self.pB[:, 0:SL], t1)
            self.bcast_row(self.pB[:, SL:2 * SL], t2)
            for i in range(NDT):
                xi = self.xt(i, lo, lo + SL)
                S.tt(xi, xi, self.pB[:, 0:SL], ALU.mult)
                S.tt(xi, xi, self.pB[:, SL:2 * SL], ALU.subtract)
                S.copy(self.xnt(i, lo, lo + SL), xi)

    def l2_attn(self):
        S = self.S
        winT = self.inp("d_L2_WinT_half", [D, D])
        wgT = self.inp("d_L2_WgateT", [D, D])
        binh = self.inp("d_L2_binh_cols", [128, 8])
        bg = self.inp("d_L2_bg_cols", [128, 8])
        S.dma(self.sR[:, 0:8], binh[:])
        S.dma(self.sR[:, 8:16], bg[:])
        for i in range(NDT):
            self.load_wchunks(self.sW[:, 0:1024], winT, D, 128, col0=i * 128, ncols=128)
            self.load_wchunks(self.sW[:, 1024:2048], wgT, D, 128, col0=i * 128, ncols=128)
            for tc in range(NTC):
                lo = tc * SL
                for c in range(NDT):
                    S.mm(self.pA[:, lo:lo + SL], self.sW[:, c * 128:(c + 1) * 128],
                         self.xnt(c, lo, lo + SL), start=(c == 0), stop=(c == NDT - 1))
                for c in range(NDT):
                    S.mm(self.pB[:, 0:SL], self.sW[:, 1024 + c * 128:1024 + (c + 1) * 128],
                         self.xnt(c, lo, lo + SL), start=(c == 0), stop=(c == NDT - 1))
                S.acopy(self.sB[:, 0:SL], self.pA[:, lo:lo + SL], bias=self.sR[:, i:i + 1])
                S.act(self.sB[:, SL:2 * SL], self.pB[:, 0:SL], AF.Sigmoid,
                      bias=self.sR[:, 8 + i:9 + i])
                S.tt(self.sA[:, i * T + lo:i * T + lo + SL],
                     self.sB[:, 0:SL], self.sB[:, SL:2 * SL], ALU.mult)
        self.tap("u", self.sA[:, 0:NDT * T])
        self.tap("x_a", self.x[:])
        bmT = self.inp("d_L2_BmT", [D, 16])
        self.load_wchunks(self.sW[:, 0:128], bmT, D, 16, col0=0, ncols=16)
        for tc in range(NTC):
            lo = tc * SL
            for c in range(NDT):
                S.mm(self.pA[0:16, lo:lo + SL], self.sW[:, c * 16:(c + 1) * 16],
                     self.sA[:, c * T + lo:c * T + lo + SL],
                     start=(c == 0), stop=(c == NDT - 1))
        H = self.sB[0:16, 0:T]
        S.copy(H, self.pA[0:16, 0:T])
        self.tap("uB", H)
        self.tap("x_b", self.x[:])
        S.dma(self.sW[0:16, 2048:2064], self.inp("d_L2_A", [16, 16])[:], q="gpsimd")
        S.dma(self.sW[0:16, 2064:2080], self.inp("d_L2_AT", [16, 16])[:], q="gpsimd")
        Msb = self.sW[0:16, 2048:2064]
        Tsb = self.sW[0:16, 2064:2080]
        for k in range(11):
            sh = 1 << k
            for tc in range(NTC):
                lo = tc * SL
                r0 = max(lo, sh)
                if r0 >= (tc + 1) * SL:
                    continue
                n = (tc + 1) * SL - r0
                S.mm(self.pA[0:16, r0:r0 + n], Tsb, self.sB[0:16, r0 - sh:r0 - sh + n],
                     start=True, stop=True)
            S.tt(self.sB[0:16, sh:T], self.sB[0:16, sh:T], self.pA[0:16, sh:T], ALU.add)
            if k < 10:
                S.mm(self.pC[0:16, 0:16], Tsb, Msb, start=True, stop=True)
                S.copy(Msb, self.pC[0:16, 0:16])
                S.trans(self.pT[0:16, 0:16], Msb, self.k_id_bf[0:16, 0:16])
                S.copy(Tsb, self.pT[0:16, 0:16])
        self.tap("H", H)
        self.tap("x_c", self.x[:])
        cmT = self.inp("d_L2_CmT", [16, D])
        S.dma(self.sW[0:16, 0:1024], cmT[:], q="gpsimd")
        for i in range(NDT):
            for tc in range(NTC):
                lo = tc * SL
                S.mm(self.pA[:, lo:lo + SL], self.sW[0:16, i * 128:(i + 1) * 128],
                     self.sB[0:16, lo:lo + SL], start=True, stop=True)
            S.copy(self.sA[:, i * T:(i + 1) * T], self.pA[:, 0:T])
        self.tap("y", self.sA[:, 0:NDT * T])
        self.tap("xpre", self.x[:])
        woT = self.inp("d_L2_WoutT", [D, D])
        bo = self.inp("d_L2_bo_cols", [128, 8])
        S.dma(self.sR[:, 16:24], bo[:])
        for i in range(NDT):
            self.load_wchunks(self.sW[:, 0:1024], woT, D, 128, col0=i * 128, ncols=128)
            for tc in range(NTC):
                lo = tc * SL
                for c in range(NDT):
                    S.mm(self.pA[:, lo:lo + SL], self.sW[:, c * 128:(c + 1) * 128],
                         self.sA[:, c * T + lo:c * T + lo + SL],
                         start=(c == 0), stop=(c == NDT - 1))
                xi = self.xt(i, lo, lo + SL)
                S.ts(self.sC[:, 0:SL], self.pA[:, lo:lo + SL],
                     self.sR[:, 16 + i:17 + i], None, ALU.add)
                S.tt(xi, xi, self.sC[:, 0:SL], ALU.add)
        self.tap("xres", self.x[:])
        self.postln()
        self.tap("xfin", self.x[:])


    def _rmsnorm_l3(self):
        S = self.S
        ns = self.inp("d_L3_ns_cols", [128, 8])
        S.dma(self.sR[:, 48:56], ns[:])
        inv = self.k_cols[:, 1:2]
        for i in range(NDT):
            S.act(self.sB[:, i * T:(i + 1) * T], self.xt(i), AF.Square)
        for tc in range(NTC):
            lo = tc * SL
            for i in range(NDT):
                S.mm(self.pA[0:1, lo:lo + SL], inv,
                     self.sB[:, i * T + lo:i * T + lo + SL],
                     start=(i == 0), stop=(i == NDT - 1))
        for tc in range(NTC):
            lo = tc * SL
            t1 = self.sF[0:1, 0:SL]
            S.act(t1, self.pA[0:1, lo:lo + SL], AF.Ln, bias=self.k_misc[0:1, 1:2])
            S.act(t1, t1, AF.Exp, scale=-0.5)
            self.bcast_row(self.pB[:, 0:SL], t1)
            for i in range(NDT):
                S.tt(self.sC[:, 0:SL], self.xt(i, lo, lo + SL),
                     self.pB[:, 0:SL], ALU.mult)
                S.ts(self.xnt(i, lo, lo + SL), self.sC[:, 0:SL],
                     self.sR[:, 48 + i:49 + i], None, ALU.mult)

    def _attention(self, layer):
        S = self.S
        l0 = (layer == 0)
        cosd = self.inp("c_cos_l0" if l0 else "c_cos_l3", [128, T], BF16)
        sind = self.inp("c_sin_l0" if l0 else "c_sin_l3", [128, T], BF16)
        S.dma(self.sB[:, 4096:6144], cosd[:])
        S.dma(self.sB[:, 6144:8192], sind[:])
        cos = self.sB[:, 4096:6144]
        sin = self.sB[:, 6144:8192]
        EOFF = 8192
        if l0:
            esh = self.inp("d_eshift_l0", [16, 128, 1536])
            tau = self.inp("d_tau_bc", [128, 16])
            rr = self.inp("c_rowrecip", [128, 16])
            S.dma(self.sR[:, 24:40], tau[:])
            S.dma(self.sR[:, 40:56], rr[:])
        else:
            e3 = self.inp("c_e3shift_bf", [128, 640], BF16)
            S.dma(self.sB[:, EOFF:EOFF + 640], e3[:])
        if l0:
            wq = self.inp("d_L0_WqT", [D, D])
            wk = self.inp("d_L0_WkT", [D, D])
            wv = self.inp("d_L0_WvT", [D, D])
            wo = self.inp("d_L0_WoT", [D, D])
        else:
            wq = self.inp("L3_attn_Wq", [D, D])
            wk = self.inp("L3_attn_Wk", [D, D])
            wv = self.inp("L3_attn_Wv", [D, D])
            wo = self.inp("L3_attn_Wo", [D, D])
        QOF, KOF, VOF = 16384, 18432, 20480
        for hp in range(8):
            if l0:
                for hh in range(2):
                    S.dma(self.sB[:, EOFF + hh * 1536:EOFF + (hh + 1) * 1536],
                          esh[2 * hp + hh], q="gpsimd")
            for (wap, dst) in ((wq, QOF), (wk, KOF)):
                self.load_wchunks(self.sW[:, 0:1024], wap, D, 128,
                                  col0=hp * 128, ncols=128)
                ov = self.sW[:, 1024:2048].rearrange("p (k h f c) -> p k h f c", k=8, h=2, f=2)
                iv = self.sW[:, 0:1024].rearrange("p (k h f c) -> p k h f c", k=8, h=2, f=2)
                S.act(ov[:, :, :, 0, :], iv[:, :, :, 1, :], AF.Copy, scale=-1.0)
                S.copy(ov[:, :, :, 1, :], iv[:, :, :, 0, :])
                for tc in range(NTC):
                    lo = tc * SL
                    for c in range(NDT):
                        S.mm(self.pA[:, lo:lo + SL],
                             self.sW[:, c * 128:(c + 1) * 128],
                             self.xnt(c, lo, lo + SL),
                             start=(c == 0), stop=(c == NDT - 1))
                    for c in range(NDT):
                        S.mm(self.pB[:, 0:SL],
                             self.sW[:, 1024 + c * 128:1024 + (c + 1) * 128],
                             self.xnt(c, lo, lo + SL),
                             start=(c == 0), stop=(c == NDT - 1))
                    S.tt(self.sC[:, 0:SL], self.pA[:, lo:lo + SL],
                         cos[:, lo:lo + SL], ALU.mult)
                    S.tt(self.sC[:, SL:2 * SL], self.pB[:, 0:SL],
                         sin[:, lo:lo + SL], ALU.mult)
                    S.tt(self.sA[:, dst + lo:dst + lo + SL],
                         self.sC[:, 0:SL], self.sC[:, SL:2 * SL], ALU.add)
            self.load_wchunks(self.sW[:, 0:1024], wv, D, 128,
                              col0=hp * 128, ncols=128)
            for tt in range(16):
                t0 = tt * 128
                for c in range(NDT):
                    S.mm(self.pC[:, 0:128], self.xnt(c, t0, t0 + 128),
                         self.sW[:, c * 128:(c + 1) * 128],
                         start=(c == 0), stop=(c == NDT - 1))
                S.copy(self.sA[:, VOF + t0:VOF + t0 + 128], self.pC[:, 0:128])
            if hp == 0:
                self.tap("q0", self.sA[:, QOF:QOF + T])
                self.tap("k0", self.sA[:, KOF:KOF + T])
                self.tap("v0", self.sA[:, VOF:VOF + T])
            for hh in range(2):
                hb = hh * 64
                hglob = 2 * hp + hh
                for qt in range(16):
                    q0 = qt * 128
                    KF = q0 + 128
                    nch = (KF + SL - 1) // SL
                    for cc in range(nch):
                        k0 = cc * SL
                        N = min(SL, KF - k0)
                        pre = (q0 - (k0 + N - 1) <= 512) if l0 else (k0 + N == KF)
                        if pre:
                            if l0:
                                off = 1023 + k0 - q0 + hh * 1536
                            else:
                                off = 511 + k0 - q0
                            S.mm(self.pA[:, k0:k0 + N], self.k_id_bf[:],
                                 self.sB[:, EOFF + off:EOFF + off + N],
                                 start=True, stop=False)
                        S.mm(self.pA[:, k0:k0 + N],
                             self.sA[hb:hb + 64, QOF + q0:QOF + q0 + 128],
                             self.sA[hb:hb + 64, KOF + k0:KOF + k0 + N],
                             start=not pre, stop=True)
                    den = self.sR[:, 56:57]
                    rd = self.sR[:, 57:58]
                    if hp == 0 and hh == 0 and qt == 5 and DEBUG_TAPS:
                        S.copy(self.sC[:, 0:KF], self.pA[:, 0:KF])
                        self.tap("sc5", self.sC[:, 0:KF])
                    S.act(self.sB[:, 0:KF], self.pA[:, 0:KF], AF.Exp,
                          accum_out=den)
                    if hp == 0 and hh == 0 and qt == 5:
                        self.tap("exp5", self.sB[:, 0:KF])
                        self.tap("den5", den)
                    S.recip(rd, den)
                    if l0:
                        c2 = self.sR[:, 58:59]
                        c2d = self.sR[:, 59:60]
                        S.tt(c2, self.sR[:, 24 + hglob:25 + hglob],
                             self.sR[:, 40 + qt:41 + qt], ALU.mult)
                        S.tt(c2d, c2, den, ALU.mult)
                        S.ts(self.sB[:, 0:KF], self.sB[:, 0:KF], c2d, 0.0,
                             ALU.add, ALU.max)
                    S.ts(self.sB[:, 0:KF], self.sB[:, 0:KF], rd, None, ALU.mult)
                    nkc = KF // 128
                    for g in range((nkc + 7) // 8):
                        b0 = g * 8
                        nb = min(8, nkc - b0)
                        for j in range(nb):
                            S.trans(self.pT[:, j * 128:(j + 1) * 128],
                                    self.sB[:, (b0 + j) * 128:(b0 + j + 1) * 128],
                                    self.k_id_bf[:])
                        S.copy(self.sB[:, 2048 + b0 * 128:2048 + (b0 + nb) * 128],
                               self.pT[:, 0:nb * 128])
                    for kc in range(nkc):
                        S.mm(self.pC[hb:hb + 64, 0:128],
                             self.sA[:, VOF + kc * 128 + hb:VOF + kc * 128 + hb + 64],
                             self.sB[:, 2048 + kc * 128:2048 + (kc + 1) * 128],
                             start=(kc == 0), stop=(kc == nkc - 1))
                    S.copy(self.sA[hb:hb + 64, hp * T + q0:hp * T + q0 + 128],
                           self.pC[hb:hb + 64, 0:128])
                    if hp == 0 and hh == 0 and qt == 5:
                        self.tap("eT5", self.sB[:, 2048:2048 + KF])
                        self.tap("av5", self.sA[0:64, q0:q0 + 128])
        for i in range(NDT):
            self.load_wchunks(self.sW[:, 0:1024], wo, D, 128,
                              col0=i * 128, ncols=128)
            for tc in range(NTC):
                lo = tc * SL
                for c in range(NDT):
                    S.mm(self.pA[:, lo:lo + SL], self.sW[:, c * 128:(c + 1) * 128],
                         self.sA[:, c * T + lo:c * T + lo + SL],
                         start=(c == 0), stop=(c == NDT - 1))
                xi = self.xt(i, lo, lo + SL)
                S.tt(xi, xi, self.pA[:, lo:lo + SL], ALU.add)
        self.postln()

    def l0_attn(self):
        self._attention(0)

    def l3_attn(self):
        self._rmsnorm_l3()
        self._attention(3)

    def l0_ffn(self):
        S = self.S
        wr = self.inp("d_L0f_wr_cols", [128, 8])
        b1c = self.inp("d_L0f_b1_cols", [128, 32])
        b2c = self.inp("d_L0f_b2_cols", [128, 8])
        W1 = self.inp("L0_ffn_W1", [D, 4096])
        W2 = self.inp("L0_ffn_W2", [4096, D])
        S.dma(self.sR[:, 96:104], wr[:])
        S.dma(self.sR[:, 184:216], b1c[:])
        S.dma(self.sR[:, 216:224], b2c[:])
        S.act(self.sR[:, 224:232], self.sR[:, 216:224], AF.Copy, scale=2.0)
        for tc in range(NTC):
            lo = tc * SL
            for c in range(NDT):
                S.mm(self.pA[0:1, lo:lo + SL], self.sR[:, 96 + c:97 + c],
                     self.xt(c, lo, lo + SL), start=(c == 0), stop=(c == NDT - 1))
        srow = self.sF[0:1, 0:T]
        S.acopy(srow, self.pA[0:1, 0:T])
        scr_s = self.scratch([T], F32)
        S.dma(scr_s, srow)
        for j in range(16):
            S.trans(self.pB[:, j:j + 1], srow[0:1, j * 128:(j + 1) * 128],
                    self.k_id_f32[0:1, 0:1])
        S.copy(self.sR[:, 104:120], self.pB[:, 0:16])
        sbrd = self.sC[:, 0:T]
        S.dma(sbrd, bass.AP(tensor=scr_s.tensor, offset=0, ap=[[0, 128], [1, T]]))
        self.tap("sbrd", sbrd)
        for j in range(16):
            S.ts(self.sB[:, 0:T], sbrd, self.sR[:, 104 + j:105 + j], 0.0,
                 ALU.is_gt, ALU.add, accum_out=self.sR[:, 120 + j:121 + j])
            if j == 0:
                self.tap("cmp0", self.sB[:, 0:T])
        S.tss(self.sR[:, 136:152], self.sR[:, 120:136], 1024.0, ALU.is_lt)
        S.act(self.sR[:, 152:168], self.sR[:, 104:120], AF.Sigmoid)
        S.tt(self.sR[:, 168:184], self.sR[:, 152:168], self.sR[:, 136:152],
             ALU.mult)
        S.act(self.sR[:, 168:184], self.sR[:, 168:184], AF.Copy, scale=0.5)
        scr_w = self.scratch([T], F32)
        S.trans(self.pB[0:16, 0:128], self.sR[:, 168:184], self.k_id_f32[:])
        S.copy(self.sF[0:16, 0:128], self.pB[0:16, 0:128])
        S.dma(scr_w.rearrange("(j p) -> j p", p=128), self.sF[0:16, 0:128])
        wtb = self.sB[:, 0:T]
        S.dma(wtb, bass.AP(tensor=scr_w.tensor, offset=0, ap=[[0, 128], [1, T]]),
              q="gpsimd")
        self.tap("srow", srow)
        self.tap("scol", self.sR[:, 104:120])
        self.tap("rank", self.sR[:, 120:136])
        self.tap("wt", self.sR[:, 168:184])
        self.tap("wtb", wtb)
        A0, B0 = 0.7978845608028654, 0.0356774081363001

        def h1sl(ft, lo, hi):
            if ft < 24:
                return self.sA[:, ft * 1024 + lo:ft * 1024 + hi]
            return self.sB[:, 2048 + (ft - 24) * 1024 + lo:2048 + (ft - 24) * 1024 + hi]

        for th in range(2):
            t0 = th * 1024
            for ft in range(32):
                self.load_wchunks(self.sW[:, 0:1024], W1, D, 128,
                                  col0=ft * 128, ncols=128)
                for tcc in range(2):
                    lo = t0 + tcc * SL
                    for c in range(NDT):
                        S.mm(self.pC[:, 0:SL], self.sW[:, c * 128:(c + 1) * 128],
                             self.xnt(c, lo, lo + SL),
                             start=(c == 0), stop=(c == NDT - 1))
                    b1 = self.sR[:, 184 + ft:185 + ft]
                    uf = self.sW[:, 1024:1536]
                    s2 = self.sW[:, 1536:2048]
                    vv = self.sW[:, 2048:2560]
                    ww = self.sW[:, 2560:3072]
                    S.ts(uf, self.pC[:, 0:SL], b1, None, ALU.add)
                    S.act(s2, self.pC[:, 0:SL], AF.Square, bias=b1)
                    S.ts(vv, s2, B0, A0, ALU.mult, ALU.add)
                    S.tt(ww, vv, uf, ALU.mult)
                    S.act(vv, ww, AF.Tanh)
                    S.stt(h1sl(ft, tcc * SL, tcc * SL + SL), vv, 1.0, uf,
                          ALU.add, ALU.mult)
            if th == 0:
                self.tap("h1_f0", h1sl(0, 0, 1024))
            for dt in range(NDT):
                for kh in range(2):
                    src = W2.rearrange("(c p) o -> p c o", p=128)
                    S.dma(self.sW[:, 0:2048],
                          src[:, kh * 16:(kh + 1) * 16, dt * 128:(dt + 1) * 128],
                          q="gpsimd")
                    for tcc in range(2):
                        ptile = self.pC[:, 0:SL] if tcc == 0 else self.pB[:, 0:SL]
                        for c in range(16):
                            ft = kh * 16 + c
                            S.mm(ptile, self.sW[:, c * 128:(c + 1) * 128],
                                 h1sl(ft, tcc * SL, tcc * SL + SL),
                                 start=(kh == 0 and c == 0),
                                 stop=(kh == 1 and c == 15))
                for tcc in range(2):
                    lo = t0 + tcc * SL
                    ptile = self.pC[:, 0:SL] if tcc == 0 else self.pB[:, 0:SL]
                    tv = self.sC[:, 0:SL]
                    t2 = self.sC[:, SL:2 * SL]
                    S.tt(tv, ptile, wtb[:, lo:lo + SL], ALU.mult)
                    S.stt(t2, wtb[:, lo:lo + SL], self.sR[:, 224 + dt:225 + dt],
                          tv, ALU.mult, ALU.add)
                    xi = self.xt(dt, lo, lo + SL)
                    S.stt(xi, xi, 2.0, t2, ALU.mult, ALU.add)
        self.postln()

    def l3_ffn(self):
        S = self.S
        rw = self.inp("L3_ffn_router_w", [D, 8])
        rb = self.inp("L3_ffn_router_b", [8])
        W1 = self.inp("L3_ffn_W1", [8, D, 512])
        W2 = self.inp("L3_ffn_W2", [8, 512, D])
        b1c = self.inp("d_L3f_b1_cols", [8, 128, 4])
        b2 = self.inp("L3_ffn_b2", [8, D])
        self.load_wchunks(self.sW[:, 0:64], rw, D, 8, col0=0, ncols=8)
        S.dma(self.sR[0:8, 0:1], rb.rearrange("(e o) -> e o", o=1))
        for tc in range(NTC):
            lo = tc * SL
            for c in range(NDT):
                S.mm(self.pB[0:8, 0:SL], self.sW[:, c * 8:(c + 1) * 8],
                     self.xnt(c, lo, lo + SL), start=(c == 0), stop=(c == NDT - 1))
            S.ts(self.sF[0:8, lo:lo + SL], self.pB[0:8, 0:SL],
                 self.sR[0:8, 0:1], None, ALU.add)
        for j in range(16):
            S.trans(self.pA[:, j * 8:(j + 1) * 8],
                    self.sF[0:8, j * 128:(j + 1) * 128], self.k_id_f32[0:8, 0:8])
        lgT = self.sC[:, 0:128]
        S.copy(lgT, self.pA[:, 0:128])
        v3 = lambda ap: ap.rearrange("p (j e) -> p j e", e=8)
        m1 = self.sC[:, 384:400]
        m2 = self.sC[:, 400:416]
        s12 = self.sC[:, 416:432]
        w1 = self.sC[:, 128:256]
        w2 = self.sC[:, 256:384]
        gm = self.sC[:, 432:560]
        rmt = self.sC[:, 560:688]
        S.red(m1, v3(lgT), ALU.max)
        m1b = m1.unsqueeze(2).broadcast_to((128, 16, 8))
        S.tt(v3(w1), v3(lgT), m1b, ALU.subtract)
        S.act(gm, w1, AF.Exp)
        S.tt(v3(w1), v3(lgT), m1b, ALU.is_equal)
        S.stt(w2, w1, -1.0e30, lgT, ALU.mult, ALU.add)
        S.red(m2, v3(w2), ALU.max)
        m2b = m2.unsqueeze(2).broadcast_to((128, 16, 8))
        S.tt(v3(w2), v3(w2), m2b, ALU.is_equal)
        S.tt(w1, w1, w2, ALU.add)
        S.tt(gm, gm, w1, ALU.mult)
        S.red(s12, v3(gm), ALU.add)
        S.recip(s12, s12)
        s12b = s12.unsqueeze(2).broadcast_to((128, 16, 8))
        S.tt(v3(rmt), v3(gm), s12b, ALU.mult)
        for j in range(16):
            S.trans(self.pA[0:8, j * 128:(j + 1) * 128],
                    rmt[:, j * 8:(j + 1) * 8], self.k_id_f32[:])
        S.copy(self.sB[0:8, 0:T], self.pA[0:8, 0:T])
        scr_rm = self.scratch([8, T], BF16)
        S.dma(scr_rm, self.sB[0:8, 0:T])
        S.dma(self.sB[0:8, T:T + D], b2, q="gpsimd")
        for e in range(8):
            S.dma(self.sR[:, 64 + e * 4:68 + e * 4], b1c[e])
        for th in range(2):
            t0 = th * 1024
            for e in range(8):
                rbc = self.sB[:, 2048 + 8 * 1024:2048 + 9 * 1024]
                S.dma(rbc, bass.AP(tensor=scr_rm.tensor, offset=e * T + t0,
                                   ap=[[0, 128], [1, 1024]]))
                for fq in range(4):
                    src = W1.rearrange("e (c p) f -> e p c f", p=128)
                    S.dma(self.sW[:, 0:1024],
                          src[e][:, :, fq * 128:(fq + 1) * 128], q="gpsimd")
                    for tcc in range(2):
                        lo = t0 + tcc * SL
                        for c in range(NDT):
                            S.mm(self.pC[:, 0:SL], self.sW[:, c * 128:(c + 1) * 128],
                                 self.xnt(c, lo, lo + SL),
                                 start=(c == 0), stop=(c == NDT - 1))
                        tmp = self.sW[:, 1024:1536]
                        S.act(tmp, self.pC[:, 0:SL], AF.Gelu,
                              bias=self.sR[:, 64 + e * 4 + fq:65 + e * 4 + fq])
                        S.tt(self.sA[:, fq * 1024 + tcc * SL:fq * 1024 + tcc * SL + SL],
                             tmp, rbc[:, tcc * SL:tcc * SL + SL], ALU.mult)
                for dtp in range(4):
                    src2 = W2.rearrange("e (c p) o -> e p c o", p=128)
                    S.dma(self.sW[:, 0:1024],
                          src2[e][:, :, dtp * 256:(dtp + 1) * 256], q="gpsimd")
                    for dh in range(2):
                        dt = dtp * 2 + dh
                        for tcc in range(2):
                            lo = t0 + tcc * SL
                            for c in range(4):
                                S.mm(self.pC[:, 0:SL],
                                     self.sW[:, c * 256 + dh * 128:c * 256 + dh * 128 + 128],
                                     self.sA[:, c * 1024 + tcc * SL:c * 1024 + tcc * SL + SL],
                                     start=(c == 0), stop=(c == 3))
                            xi = self.xt(dt, lo, lo + SL)
                            S.tt(xi, xi, self.pC[:, 0:SL], ALU.add)
            for dt in range(NDT):
                for tcc in range(2):
                    lo = t0 + tcc * SL
                    S.mm(self.pC[:, 0:SL],
                         self.sB[0:8, T + dt * 128:T + (dt + 1) * 128],
                         self.sB[0:8, lo:lo + SL], start=True, stop=True)
                    xi = self.xt(dt, lo, lo + SL)
                    S.tt(xi, xi, self.pC[:, 0:SL], ALU.add)
        self.postln()


    def _h1sl(self, ft, lo, hi):
        if ft < 24:
            return self.sA[:, ft * 1024 + lo:ft * 1024 + hi]
        return self.sB[:, (ft - 24) * 1024 + lo:(ft - 24) * 1024 + hi]

    def l1_ffn(self):
        S = self.S
        Wexp = self.inp("L1_ffn_W_exp", [D, 4096])
        bexp = self.inp("d_bexp_cols", [128, 32])
        upb = self.inp("d_upbiasP_cols", [128, 32])
        WconP = self.inp("d_W_conP", [4096, D])
        bcon = self.inp("d_bcon_cols", [128, 8])
        dbias = self.inp("d_dbias_cols", [128, 8])
        S.dma(self.sR[:, 64:96], bexp[:])
        S.dma(self.sR[:, 96:128], upb[:])
        S.dma(self.sR[:, 128:136], bcon[:])
        S.dma(self.sR[:, 136:144], dbias[:])
        S.tt(self.sR[:, 128:136], self.sR[:, 128:136], self.sR[:, 136:144], ALU.add)
        CTOF = {"up1": 8192, "up2": 10240, "dn1": 12288, "dn2": 14336}
        for gname in ("up1", "up2", "dn1", "dn2"):
            fac = self.inp(f"d_bf_{gname}", [128, 12288])
            S.dma(self.sA[:, 0:12288], fac[:], q="gpsimd")
            f = lambda s: self.sA[:, s * 2048:(s + 1) * 2048]
            pong = self.sA[:, 12288:14336]
            ping = self.sW[:, 0:2048]
            S.tt(pong, f(0), f(1), ALU.mult)
            S.tt(ping, pong, f(2), ALU.mult)
            S.tt(pong, ping, f(3), ALU.mult)
            S.tt(ping, pong, f(4), ALU.mult)
            S.tt(self.sB[:, CTOF[gname]:CTOF[gname] + 2048], ping, f(5), ALU.mult)

        def apply_group(gname, t0, gelu=False):
            off = CTOF[gname]
            for m in range(32):
                for tcc in range(2):
                    lo = t0 + tcc * SL
                    hl = self._h1sl(m, (lo - t0), (lo - t0) + SL)
                    for kb in range(2):
                        S.mm(self.pC[kb * 64:kb * 64 + 64, 0:SL],
                             self.sB[kb * 64:kb * 64 + 64, off + m * 64:off + (m + 1) * 64],
                             self._h1sl(m, (lo - t0), (lo - t0) + SL)[kb * 64:kb * 64 + 64, :],
                             start=True, stop=True)
                    if gelu:
                        S.act(hl, self.pC[:, 0:SL], AF.Gelu,
                              bias=self.sR[:, 96 + m:97 + m])
                    else:
                        S.copy(hl, self.pC[:, 0:SL])

        scr_h = self.scratch([4096, 1024], BF16)

        def permute(t0):
            S.dma(scr_h.rearrange("(i p) t -> p i t", p=128)[:, 0:24, :],
                  self.sA[:, 0:24576])
            S.dma(scr_h.rearrange("(i p) t -> p i t", p=128)[:, 24:32, :],
                  self.sB[:, 0:8192])
            for p1 in range(2):
                S.dma(self.sA[p1 * 64:(p1 + 1) * 64, 0:24576]
                      .rearrange("p (i t) -> p i t", t=1024),
                      bass.AP(tensor=scr_h.tensor, offset=p1 * 1024,
                              ap=[[65536, 64], [2048, 24], [1, 1024]]))
                S.dma(self.sB[p1 * 64:(p1 + 1) * 64, 0:8192]
                      .rearrange("p (i t) -> p i t", t=1024),
                      bass.AP(tensor=scr_h.tensor, offset=p1 * 1024 + 48 * 1024,
                              ap=[[65536, 64], [2048, 8], [1, 1024]]))

        for th in range(2):
            t0 = th * 1024
            for ft in range(32):
                self.load_wchunks(self.sW[:, 0:1024], Wexp, D, 128,
                                  col0=ft * 128, ncols=128)
                for tcc in range(2):
                    lo = t0 + tcc * SL
                    for c in range(NDT):
                        S.mm(self.pC[:, 0:SL], self.sW[:, c * 128:(c + 1) * 128],
                             self.xnt(c, lo, lo + SL),
                             start=(c == 0), stop=(c == NDT - 1))
                    S.ts(self._h1sl(ft, tcc * SL, tcc * SL + SL), self.pC[:, 0:SL],
                         self.sR[:, 64 + ft:65 + ft], None, ALU.add)
            apply_group("up1", t0)
            permute(t0)
            apply_group("up2", t0, gelu=True)
            permute(t0)
            apply_group("dn1", t0)
            permute(t0)
            apply_group("dn2", t0)
            for dt in range(NDT):
                for kh in range(2):
                    src = WconP.rearrange("(c p) o -> p c o", p=128)
                    S.dma(self.sW[:, 0:2048],
                          src[:, kh * 16:(kh + 1) * 16, dt * 128:(dt + 1) * 128],
                          q="gpsimd")
                    for tcc in range(2):
                        ptile = self.pC[:, 0:SL] if tcc == 0 else self.pB[:, 0:SL]
                        for c in range(16):
                            S.mm(ptile, self.sW[:, c * 128:(c + 1) * 128],
                                 self._h1sl(kh * 16 + c, tcc * SL, tcc * SL + SL),
                                 start=(kh == 0 and c == 0),
                                 stop=(kh == 1 and c == 15))
                for tcc in range(2):
                    lo = t0 + tcc * SL
                    ptile = self.pC[:, 0:SL] if tcc == 0 else self.pB[:, 0:SL]
                    S.ts(self.sC[:, 0:SL], ptile, self.sR[:, 128 + dt:129 + dt],
                         None, ALU.add)
                    xi = self.xt(dt, lo, lo + SL)
                    S.tt(xi, xi, self.sC[:, 0:SL], ALU.add)
        self.postln()

    def _quant(self, dst, src, width, btmp):
        S = self.S
        nb = width // 16
        mx = self.sR[:, 320:320 + nb]
        sc = self.sR[:, 408:408 + nb]
        B1, B2, B3 = btmp
        v = self.sF[:, 0:width]
        ftmp = self.sF[:, width:2 * width]
        vview = lambda ap: ap.rearrange("p (b g) -> p b g", g=16)
        S.red(mx[:, 0:nb], vview(src), ALU.max, absval=True)
        S.ts(sc[:, 0:nb], mx[:, 0:nb], 1.0 / 6.0, 1e-12, ALU.mult, ALU.max)
        S.recip(sc[:, 0:nb], sc[:, 0:nb])
        S.act(v, src, AF.Abs)
        rcb = sc[:, 0:nb].unsqueeze(2).broadcast_to((128, nb, 16))
        S.tt(vview(v), vview(v), rcb, ALU.mult)
        S.ts(ftmp, v, 2.25, 2.0, ALU.min, ALU.mult)
        S.ts(B2, ftmp, MAGIC, MAGIC, ALU.add, ALU.subtract)
        S.ts(ftmp, v, 2.25, 5.0, ALU.max, ALU.min)
        S.ts(B3, ftmp, MAGIC, MAGIC, ALU.add, ALU.subtract)
        S.ts(B1, B3, 4.0, -2.0, ALU.min, ALU.add)
        S.stt(B3, B2, 0.5, B1, ALU.mult, ALU.add)
        S.ts(B1, v, 5.0, 2.0, ALU.is_gt, ALU.mult)
        S.tt(B2, B3, B1, ALU.add)
        S.act(B1, src, AF.Sign)
        S.tt(B3, B2, B1, ALU.mult)
        S.recip(sc[:, 0:nb], sc[:, 0:nb])
        scb = sc[:, 0:nb].unsqueeze(2).broadcast_to((128, nb, 16))
        S.tt(vview(dst), vview(B3), scb, ALU.mult)

    def l2_ffn(self):
        S = self.S
        wg = self.inp("L2_ffn_w_gate", [2816, D])
        wu = self.inp("L2_ffn_w_up", [2816, D])
        wd = self.inp("L2_ffn_w_down", [D, 2816])
        scr_h = self.scratch([2816, T], BF16)
        scr_hq = self.scratch([T, 2816], BF16)
        scr_wdT = self.scratch([2816, D], BF16)
        btmp1k = (self.sB[:, 1024:2048], self.sB[:, 2048:3072], self.sB[:, 3072:4096])
        for tt in range(16):
            t0 = tt * 128
            for i in range(NDT):
                S.trans(self.pA[:, i * 128:(i + 1) * 128],
                        self.xt(i, t0, t0 + 128), self.k_id_f32[:])
            S.copy(self.sC[:, 0:1024], self.pA[:, 0:1024])
            self._quant(self.sB[:, 0:1024], self.sC[:, 0:1024], 1024, btmp1k)
            for i in range(NDT):
                S.trans(self.pT[:, i * 128:(i + 1) * 128],
                        self.sB[:, i * 128:(i + 1) * 128], self.k_id_bf[:])
            for i in range(NDT):
                S.copy(self.xnt(i, t0, t0 + 128), self.pT[:, i * 128:(i + 1) * 128])
        for rt in range(22):
            r0 = rt * 128
            for (wap, slot) in ((wg, 4096), (wu, 5120)):
                S.dma(self.sC[:, 0:1024], wap[r0:r0 + 128, :])
                self._quant(self.sB[:, 0:1024], self.sC[:, 0:1024], 1024, btmp1k)
                for i in range(NDT):
                    S.trans(self.pT[:, i * 128:(i + 1) * 128],
                            self.sB[:, i * 128:(i + 1) * 128], self.k_id_bf[:])
                S.copy(self.sB[:, slot:slot + 1024], self.pT[:, 0:1024])
            for tc in range(NTC):
                lo = tc * SL
                for c in range(NDT):
                    S.mm(self.pC[:, 0:SL], self.sB[:, 4096 + c * 128:4096 + (c + 1) * 128],
                         self.xnt(c, lo, lo + SL), start=(c == 0), stop=(c == NDT - 1))
                for c in range(NDT):
                    S.mm(self.pB[:, 0:SL], self.sB[:, 5120 + c * 128:5120 + (c + 1) * 128],
                         self.xnt(c, lo, lo + SL), start=(c == 0), stop=(c == NDT - 1))
                S.act(self.sW[:, 0:SL], self.pC[:, 0:SL], AF.Silu)
                S.tt(self.sW[:, SL:2 * SL], self.sW[:, 0:SL], self.pB[:, 0:SL],
                     ALU.mult)
                S.dma(scr_h[r0:r0 + 128, lo:lo + SL], self.sW[:, SL:2 * SL])
        wdv = scr_wdT.rearrange("(kb p) o -> p kb o", p=128)
        btmp14 = (self.sB[:, 1408:2816], self.sB[:, 2816:4224], self.sB[:, 4224:5632])
        for rt in range(NDT):
            for ch in range(2):
                wcols = 1408
                S.dma(self.sC[:, 0:wcols], wd[rt * 128:(rt + 1) * 128,
                                              ch * wcols:(ch + 1) * wcols])
                self._quant(self.sB[:, 0:wcols], self.sC[:, 0:wcols], wcols, btmp14)
                for g in range(2):
                    nblk = 8 if g == 0 else 3
                    for j in range(nblk):
                        blk = g * 8 + j
                        S.trans(self.pT[:, j * 128:(j + 1) * 128],
                                self.sB[:, blk * 128:(blk + 1) * 128], self.k_id_bf[:])
                    S.copy(self.sB[:, 5632 + g * 1024:5632 + g * 1024 + nblk * 128],
                           self.pT[:, 0:nblk * 128])
                S.dma(wdv[:, ch * 11:(ch + 1) * 11, rt * 128:(rt + 1) * 128],
                      self.sB[:, 5632:5632 + 1408])
        for tt in range(16):
            t0 = tt * 128
            S.dmaT(self.sW[:, 0:2816], scr_h[:, t0:t0 + 128])
            self._quant(self.sB[:, 0:1408], self.sW[:, 0:1408], 1408, btmp14)
            self._quant(self.sB[:, 1408:2816] if False else self.sB[:, 5632:7040],
                        self.sW[:, 1408:2816], 1408, btmp14)
            S.dma(scr_hq[t0:t0 + 128, 0:1408], self.sB[:, 0:1408])
            S.dma(scr_hq[t0:t0 + 128, 1408:2816], self.sB[:, 5632:7040])
        wdl = scr_wdT.rearrange("(c p) o -> p c o", p=128)
        for kh in range(2):
            for j in range(11):
                kc = kh * 11 + j
                S.dmaT(self.sA[:, j * T:(j + 1) * T], scr_hq[:, kc * 128:(kc + 1) * 128])
            for dt in range(NDT):
                S.dma(self.sW[:, 0:1408],
                      wdl[:, kh * 11:(kh + 1) * 11, dt * 128:(dt + 1) * 128],
                      q="sync")
                for tc in range(NTC):
                    lo = tc * SL
                    for c in range(11):
                        S.mm(self.pC[:, 0:SL], self.sW[:, c * 128:(c + 1) * 128],
                             self.sA[:, c * T + lo:c * T + lo + SL],
                             start=(c == 0), stop=(c == 10))
                    xi = self.xt(dt, lo, lo + SL)
                    S.tt(xi, xi, self.pC[:, 0:SL], ALU.add)
        self.postln()

    def _ut(self, slab, lo, hi):
        if slab < 12:
            return self.sA[:, slab * T + lo:slab * T + hi]
        return self.sB[:, (slab - 12) * T + lo:(slab - 12) * T + hi]

    def l1_attn(self):
        S = self.S
        Win = self.inp("L1_attn_W_in", [D, 4096])
        Wx = self.inp("L1_attn_W_x", [2048, 96])
        Wdt = self.inp("L1_attn_W_dt", [64, 2048])
        Wout = self.inp("L1_attn_W_out", [2048, D])
        convc = self.inp("d_L1_conv_cols", [128, 64])
        convb = self.inp("d_L1_convb_cols", [128, 16])
        bdt = self.inp("d_L1_bdt_cols", [128, 16])
        Dcol = self.inp("d_L1_D_cols", [128, 16])
        alog = self.inp("d_L1_alog_cols", [128, 256])
        S.dma(self.sR[:, 0:64], convc[:])
        S.dma(self.sR[:, 64:80], convb[:])
        S.dma(self.sR[:, 80:96], bdt[:])
        S.dma(self.sR[:, 96:112], Dcol[:])
        S.dma(self.sR[:, 256:512], alog[:])
        S.act(self.sC[:, 0:256], self.sR[:, 256:512], AF.Exp)
        S.ts(self.sR[:, 256:512], self.sC[:, 0:256], -1.0, None, ALU.mult)
        scr_x = self.scratch([128, NDT * T], F32)
        scr_res = self.scratch([2048, T], BF16)
        scr_B = self.scratch([16, T], BF16)
        scr_C = self.scratch([16, T], BF16)
        scr_y2 = self.scratch([2048, T], BF16)
        S.dma(scr_x, self.x[:])
        dlt_all = self.x[:].bitcast(BF16)
        for ft in range(32):
            self.load_wchunks(self.sB[:, 8192:9216], Win, D, 128,
                              col0=ft * 128, ncols=128)
            for tc in range(NTC):
                lo = tc * SL
                for c in range(NDT):
                    S.mm(self.pA[:, lo:lo + SL],
                         self.sB[:, 8192 + c * 128:8192 + (c + 1) * 128],
                         self.xnt(c, lo, lo + SL),
                         start=(c == 0), stop=(c == NDT - 1))
            if ft < 16:
                S.memset(self.sW[:, 0:3], 0.0)
                S.copy(self.sW[:, 3:2051], self.pA[:, 0:T])
                wc = lambda i: self.sR[:, ft * 4 + i:ft * 4 + i + 1]
                S.ts(self.sC[:, 0:T], self.sW[:, 0:T], wc(0), None, ALU.mult)
                S.stt(self.sF[:, 0:2048], self.sW[:, 1:2049], wc(1),
                      self.sC[:, 0:T], ALU.mult, ALU.add)
                S.stt(self.sC[:, 0:T], self.sW[:, 2:2050], wc(2),
                      self.sF[:, 0:2048], ALU.mult, ALU.add)
                S.stt(self.sF[:, 0:2048], self.sW[:, 3:2051], wc(3),
                      self.sC[:, 0:T], ALU.mult, ALU.add)
                S.act(self._ut(ft, 0, T), self.sF[:, 0:2048], AF.Silu,
                      bias=self.sR[:, 64 + ft:65 + ft])
            else:
                S.copy(self.sB[:, 9216:11264], self.pA[:, 0:T])
                S.dma(scr_res[(ft - 16) * 128:(ft - 15) * 128, :],
                      self.sB[:, 9216:11264])
        self.load_wchunks(self.sB[:, 8192:9728], Wx, 2048, 96, col0=0, ncols=96)
        for tc in range(NTC):
            lo = tc * SL
            for c in range(16):
                S.mm(self.pB[0:96, 0:SL], self.sB[:, 8192 + c * 96:8192 + (c + 1) * 96],
                     self._ut(c, lo, lo + SL), start=(c == 0), stop=(c == 15))
            S.copy(self.sB[0:64, 11264 + lo:11264 + lo + SL], self.pB[0:64, 0:SL])
            S.copy(self.sB[64:96, 13312 + lo:13312 + lo + SL], self.pB[64:96, 0:SL])
        S.dma(scr_B, self.sB[64:80, 13312:15360])
        S.dma(scr_C, self.sB[80:96, 13312:15360])
        S.dma(self.sB[0:64, 13312:15360], Wdt, q="gpsimd")
        for slab in range(16):
            for tc in range(NTC):
                lo = tc * SL
                S.mm(self.pC[:, 0:SL],
                     self.sB[0:64, 13312 + slab * 128:13312 + (slab + 1) * 128],
                     self.sB[0:64, 11264 + lo:11264 + lo + SL],
                     start=True, stop=True)
                bc = self.sR[:, 80 + slab:81 + slab]
                S.act(self.sC[:, 0:SL], self.pC[:, 0:SL], AF.Abs, bias=bc)
                S.act(self.sC[:, SL:2 * SL], self.sC[:, 0:SL], AF.Exp, scale=-1.0)
                S.act(self.sC[:, 0:SL], self.sC[:, SL:2 * SL], AF.Ln, bias=1.0)
                S.ts(self.sF[:, 0:SL], self.pC[:, 0:SL], bc, 0.0, ALU.add, ALU.max)
                S.tt(dlt_all[:, slab * T + lo:slab * T + lo + SL],
                     self.sF[:, 0:SL], self.sC[:, 0:SL], ALU.add)
        sCbf = self.sC[:].bitcast(BF16)
        carry = self.sF[:, 0:256]
        S.memset(carry, 0.0)
        for te in range(4):
            t0 = te * SL
            S.dma(self.xn[:, 0:8192],
                  bass.AP(tensor=scr_B.tensor, offset=t0,
                          ap=[[0, 128], [T, 16], [1, SL]]))
            S.dma(self.xn[:, 8192:16384],
                  bass.AP(tensor=scr_C.tensor, offset=t0,
                          ap=[[0, 128], [T, 16], [1, SL]]))
            for slab in range(16):
                du = self.sW[:, 0:SL]
                S.tt(du, dlt_all[:, slab * T + t0:slab * T + t0 + SL],
                     self._ut(slab, t0, t0 + SL), ALU.mult)
                S.ts(self.pA[:, 0:SL], self._ut(slab, t0, t0 + SL),
                     self.sR[:, 96 + slab:97 + slab], None, ALU.mult)
                for n in range(16):
                    a = sCbf[:, 0:SL]
                    b = sCbf[:, SL:2 * SL]
                    h = sCbf[:, 2 * SL:3 * SL]
                    g = sCbf[:, 3 * SL:4 * SL]
                    S.act(a, dlt_all[:, slab * T + t0:slab * T + t0 + SL],
                          AF.Exp, scale=self.sR[:, 256 + slab * 16 + n:257 + slab * 16 + n])
                    S.tt(b, du, self.xn[:, n * SL:(n + 1) * SL], ALU.mult)
                    S.scan(h, a, b, carry[:, slab * 16 + n:slab * 16 + n + 1],
                           ALU.mult, ALU.add)
                    S.copy(carry[:, slab * 16 + n:slab * 16 + n + 1],
                           h[:, SL - 1:SL])
                    S.tt(g, h, self.xn[:, 8192 + n * SL:8192 + (n + 1) * SL],
                         ALU.mult)
                    S.tt(self.pA[:, 0:SL], self.pA[:, 0:SL], g, ALU.add)
                S.dma(sCbf[:, 0:SL],
                      scr_res[slab * 128:(slab + 1) * 128, t0:t0 + SL])
                S.act(sCbf[:, SL:2 * SL], sCbf[:, 0:SL], AF.Silu)
                S.tt(sCbf[:, 2 * SL:3 * SL], self.pA[:, 0:SL],
                     sCbf[:, SL:2 * SL], ALU.mult)
                S.dma(scr_y2[slab * 128:(slab + 1) * 128, t0:t0 + SL],
                      sCbf[:, 2 * SL:3 * SL])
        S.dma(self.x[:], scr_x)
        for slab in range(16):
            S.dma(self._ut(slab, 0, T), scr_y2[slab * 128:(slab + 1) * 128, :])
        for dt in range(NDT):
            self.load_wchunks(self.sW[:, 0:2048], Wout, 2048, 128,
                              col0=dt * 128, ncols=128)
            for tc in range(NTC):
                lo = tc * SL
                for c in range(16):
                    S.mm(self.pC[:, 0:SL], self.sW[:, c * 128:(c + 1) * 128],
                         self._ut(c, lo, lo + SL), start=(c == 0), stop=(c == 15))
                xi = self.xt(dt, lo, lo + SL)
                S.tt(xi, xi, self.pC[:, 0:SL], ALU.add)
        self.postln()



_SUBLAYER_FN = {0: "l0_attn", 1: "l0_ffn", 2: "l1_attn", 3: "l1_ffn",
                4: "l2_attn", 5: "l2_ffn", 6: "l3_attn", 7: "l3_ffn"}

_cache = {}


def _build_group(g):
    key = tuple(g)
    p = Prog(g)
    p.load_consts()
    p.prelude()
    for s in g:
        getattr(p, _SUBLAYER_FN[s])()
    p.epilogue()
    sem = p.nc.semaphore("gs").__enter__()
    p.S.build(sem)
    return p


def kernel(**inputs):
    consts = make_consts()
    derived = make_derived(inputs)
    pool = {}
    pool.update(consts)
    pool.update({k: np.asarray(v) for k, v in derived.items()})
    for k, v in inputs.items():
        pool[k] = np.asarray(v)
    x = np.asarray(inputs["L0_attn_x"], dtype=np.float32)
    xcur = [np.ascontiguousarray(x[c].T) for c in range(8)]
    for g in FUSE_GROUPS:
        p = _build_group(g)
        base = {name: pool[name] for name in p.used if name != "xin"}
        in_maps = [dict(base, xin=xcur[c]) for c in range(8)]
        res = run_bass_kernel_spmd(p.nc, in_maps, core_ids=list(range(8)))
        xcur = [np.asarray(res.results[c]["xout"]) for c in range(8)]
    return np.stack([xc.T for xc in xcur]).astype(np.float32)


# version 03
# speedup vs baseline: 1.0185x; 1.0185x over previous
import numpy as np
import ml_dtypes

import concourse.bass as bass
import concourse.mybir as mybir
from concourse.bass_utils import run_bass_kernel_spmd

F32 = mybir.dt.float32
BF16 = mybir.dt.bfloat16
AF = mybir.ActivationFunctionType
ALU = mybir.AluOpType
AX = mybir.AxisListType

T = 2048
D = 1024
NDT = D // 128
SL = 512
NTC = T // SL
MAGIC = 8388608.0

FUSE_GROUPS = [[0, 1, 2, 3, 4, 5, 6, 7]]
DEBUG_TAPS = False



class Seq:
    ENGINES = ("sync", "scalar", "vector", "tensor", "gpsimd")

    def __init__(self, nc):
        self.nc = nc
        self.steps = []
        self.n = 0

    def _add(self, eng, inc, fn):
        self.steps.append((eng, self.n, inc, fn))
        self.n += inc

    def dma(self, out, in_, q="sync"):
        self._add(q, 16, lambda E, o=out, i=in_: E.dma_start(out=o, in_=i))

    def dmaT(self, out, in_, q="sync"):
        self._add(q, 16, lambda E, o=out, i=in_: E.dma_start_transpose(o, i))

    def mm(self, out, lhsT, rhs, start=True, stop=True, tile_position=None):
        def f(E, o=out, l=lhsT, r=rhs, st=start, sp=stop, tp=tile_position):
            return E.matmul(o, l, r, start=st, stop=sp, tile_position=tp)
        self._add("tensor", 1, f)

    def trans(self, out, in_, ident):
        self._add("tensor", 1, lambda E, o=out, i=in_, d=ident: E.transpose(o, i, d))

    def act(self, out, in_, func, bias=0.0, scale=1.0, accum_out=None):
        def f(E, o=out, i=in_, fu=func, b=bias, s=scale, a=accum_out):
            return E.activation(o, i, fu, bias=b, scale=s, accum_out=a)
        self._add("scalar", 1, f)

    def tt(self, out, a, b, op, eng="vector"):
        self._add(eng, 1, lambda E, o=out, x=a, y=b, p=op: E.tensor_tensor(o, x, y, p))

    def ts(self, out, in0, s1, s2, op0, op1=ALU.bypass, accum_out=None, eng="vector"):
        def f(E, o=out, i=in0, a=s1, b=s2, p0=op0, p1=op1, ac=accum_out):
            return E.tensor_scalar(o, i, a, b, p0, p1, accum_out=ac)
        self._add(eng, 1, f)

    def tss(self, out, in0, s1, op, eng="vector"):
        self._add(eng, 1, lambda E, o=out, i=in0, a=s1, p=op: E.tensor_single_scalar(o, i, a, p))

    def stt(self, out, in0, scalar, in1, op0, op1, eng="vector"):
        def f(E, o=out, i=in0, s=scalar, j=in1, p0=op0, p1=op1):
            return E.scalar_tensor_tensor(o, i, s, j, p0, p1)
        self._add(eng, 1, f)

    def red(self, out, in_, op, axis=AX.X, absval=None, negate=None, eng="vector"):
        def f(E, o=out, i=in_, p=op, a=axis, ab=absval, ng=negate):
            return E.tensor_reduce(o, i, a, p, apply_absolute_value=ab, negate=ng)
        self._add(eng, 1, f)

    def copy(self, out, in_, eng="vector"):
        self._add(eng, 1, lambda E, o=out, i=in_: E.tensor_copy(o, i))

    def acopy(self, out, in_, bias=0.0, scale=1.0):
        if isinstance(bias, (int, float)):
            self.act(out, in_, AF.Copy, bias=bias, scale=scale)
        else:
            assert scale == 1.0
            self.ts(out, in_, bias, None, ALU.add)

    def recip(self, out, in_):
        self._add("vector", 1, lambda E, o=out, i=in_: E.reciprocal(o, i))

    def scan(self, out, d0, d1, initial, op0, op1, eng="vector"):
        def f(E, o=out, a=d0, b=d1, ini=initial, p0=op0, p1=op1):
            return E.tensor_tensor_scan(o, a, b, ini, p0, p1)
        self._add(eng, 1, f)

    def memset(self, ap, val, eng="vector"):
        self._add(eng, 1, lambda E, a=ap, v=val: E.memset(a, v))

    def affine_sel(self, out, in_, pattern, cmp, fill, base=0, cm=1):
        def f(E, o=out, i=in_, p=pattern, c=cmp, fl=fill, b=base, m=cm):
            return E.affine_select(o, i, p, c, fl, base=b, channel_multiplier=m)
        self._add("gpsimd", 1, f)

    def build(self, sem):
        nc = self.nc
        with nc.Block() as block:
            for engname in self.ENGINES:
                def section(E, engname=engname):
                    for (e, base, inc, fn) in self.steps:
                        if e == engname:
                            E.wait_ge(sem, base)
                            fn(E).then_inc(sem, inc)
                    E.wait_ge(sem, self.n)
                getattr(block, engname)(section)



def _rope_tables(base):
    inv = 1.0 / (base ** (np.arange(0, 32, dtype=np.float64) / 32.0))
    t = np.arange(T, dtype=np.float64)
    ang = t[None, :] * inv[:, None]
    cos = np.concatenate([np.cos(ang)] * 2, 0)
    sin = np.concatenate([np.sin(ang)] * 2, 0)
    s = 8.0 ** -0.5
    cosf = (np.concatenate([cos, cos], 0) * s).astype(ml_dtypes.bfloat16)
    sinf = (np.concatenate([sin, sin], 0) * s).astype(ml_dtypes.bfloat16)
    return cosf, sinf


def _cols(a, n=None):
    a = np.asarray(a, dtype=np.float32).reshape(-1)
    n = n or (a.size // 128)
    return np.ascontiguousarray(a.reshape(n, 128).T)


def make_consts():
    c = {}
    c["c_id_bf"] = np.eye(128, dtype=ml_dtypes.bfloat16)
    c["c_id_f32"] = np.eye(128, dtype=np.float32)
    c["c_cols_bf"] = np.concatenate(
        [np.ones((128, 1), np.float32), np.full((128, 1), 1.0 / 1024.0, np.float32)],
        axis=1).astype(ml_dtypes.bfloat16)
    c["c_ones_row_f32"] = np.ones((1, 128), dtype=np.float32)
    c["c_misc_cols"] = np.ascontiguousarray(np.broadcast_to(
        np.array([1e-5, 1e-6, 1.0, 0.0], np.float32)[None, :], (128, 4)))
    c["c_ones_row_bf"] = np.ones((1, 128), dtype=ml_dtypes.bfloat16)
    c["c_cos_l0"], c["c_sin_l0"] = _rope_tables(10000.0)
    c["c_cos_l3"], c["c_sin_l3"] = _rope_tables(1000000.0)
    p = np.arange(128)[:, None]
    qt = np.arange(16)[None, :]
    c["c_rowrecip"] = (1.0 / (qt * 128 + p + 1)).astype(np.float32)
    m = np.arange(640)[None, :]
    c["c_e3shift_bf"] = np.where(p - m + 511 >= 0, 0.0, -2.0e30).astype(ml_dtypes.bfloat16)
    return c


def make_derived(inp):
    d = {}
    tr = lambda a: np.ascontiguousarray(np.asarray(a, dtype=np.float32).T)
    asf = lambda a: np.ascontiguousarray(np.asarray(a, dtype=np.float32))
    for w in ("Wq", "Wk", "Wv", "Wo"):
        d[f"d_L0_{w}T"] = tr(inp[f"L0_attn_{w}"])
    biases = np.asarray(inp["L0_attn_biases"], dtype=np.float32)
    W = 1536
    pp = np.arange(128)[:, None]
    mm_ = np.arange(W)[None, :]
    v = pp - mm_ + 1023
    es = biases[:, np.clip(v, 0, 512)]
    es = np.where(v[None] < 0, -2.0e30, np.where(v[None] > 512, 0.0, es))
    d["d_eshift_l0"] = np.ascontiguousarray(es.astype(np.float32))
    d["d_tau_bc"] = np.ascontiguousarray(
        np.broadcast_to(np.asarray(inp["L0_attn_tau"], np.float32)[None, :], (128, 16)))
    d["d_L0f_wr_cols"] = _cols(inp["L0_ffn_w_router"])
    d["d_L0f_b1_cols"] = _cols(inp["L0_ffn_b1"])
    d["d_L0f_b2_cols"] = _cols(inp["L0_ffn_b2"])
    d["d_L1_conv_cols"] = np.ascontiguousarray(
        np.asarray(inp["L1_attn_conv_w"], np.float32)[:, 0, :]
        .reshape(16, 128, 4).transpose(1, 0, 2).reshape(128, 64))
    d["d_L1_convb_cols"] = _cols(inp["L1_attn_conv_b"])
    d["d_L1_bdt_cols"] = _cols(inp["L1_attn_b_dt"])
    d["d_L1_D_cols"] = _cols(inp["L1_attn_D"])
    d["d_L1_alog_cols"] = np.ascontiguousarray(
        np.asarray(inp["L1_attn_A_log"], np.float32)
        .reshape(16, 128, 16).transpose(1, 0, 2).reshape(128, 256))
    upw = np.asarray(inp["L1_ffn_up_weights"], dtype=np.float32)
    dnw = np.asarray(inp["L1_ffn_down_weights"], dtype=np.float32)
    d["d_bf_up1"] = _bf_factors_g1(upw)
    d["d_bf_up2"] = _bf_factors_g2(upw)
    d["d_bf_dn1"] = _bf_factors_g1(dnw)
    d["d_bf_dn2"] = _bf_factors_g2(dnw)
    fp = np.arange(4096)
    f_of_fp = (fp % 64) * 64 + (fp // 64)
    d["d_bexp_cols"] = _cols(inp["L1_ffn_b_exp"])
    d["d_upbiasP_cols"] = _cols(np.asarray(inp["L1_ffn_up_bias"], np.float32)[f_of_fp])
    d["d_W_conP"] = np.ascontiguousarray(
        np.asarray(inp["L1_ffn_W_con"], np.float32)[f_of_fp, :])
    d["d_bcon_cols"] = _cols(inp["L1_ffn_b_con"])
    d["d_dbias_cols"] = _cols(inp["L1_ffn_down_bias"])
    Win = np.asarray(inp["L2_attn_W_in"], dtype=np.float32)
    d["d_L2_WinT_half"] = tr(Win[D:2 * D, :])
    d["d_L2_binh_cols"] = _cols(np.asarray(inp["L2_attn_b_in"], np.float32)[D:2 * D])
    d["d_L2_WgateT"] = tr(inp["L2_attn_W_gate"])
    d["d_L2_bg_cols"] = _cols(inp["L2_attn_b_gate"])
    d["d_L2_A"] = asf(inp["L2_attn_A"])
    d["d_L2_AT"] = tr(inp["L2_attn_A"])
    d["d_L2_BmT"] = tr(inp["L2_attn_Bm"])
    d["d_L2_CmT"] = asf(inp["L2_attn_Cm"]).T.copy()
    d["d_L2_WoutT"] = tr(inp["L2_attn_W_out"])
    d["d_L2_bo_cols"] = _cols(inp["L2_attn_b_out"])
    d["d_L3_ns_cols"] = _cols(inp["L3_attn_norm_scale"])
    d["d_L3f_b1_cols"] = np.ascontiguousarray(
        np.asarray(inp["L3_ffn_b1"], np.float32).reshape(8, 4, 128).transpose(0, 2, 1))
    return d


def _dev_layout(out):
    dev = np.empty((128, 6 * 32 * 64), dtype=np.float32)
    for s in range(6):
        for k in range(64):
            kb, kh = k % 2, k // 2
            dev[kb * 64:(kb + 1) * 64, s * 2048 + kh * 64:s * 2048 + (kh + 1) * 64] = out[s, k]
    return np.ascontiguousarray(dev)


def _bf_factors_g1(wts):
    out = np.empty((6, 64, 64, 64), dtype=np.float32)
    i = np.arange(64)[None, :]
    j = np.arange(64)[:, None]
    for s in range(6):
        sig = 1 << s
        w = np.asarray(wts[s]).reshape(2048 // sig, sig, 2, 2)
        for k in range(64):
            nb = k * (1 << (5 - s)) + (j >> (s + 1))
            low = np.broadcast_to(i & (sig - 1), (64, 64))
            r = np.broadcast_to((i >> s) & 1, (64, 64))
            c = np.broadcast_to((j >> s) & 1, (64, 64))
            nbb = np.broadcast_to(nb, (64, 64))
            out[s, k] = w[nbb, low, r, c]
    return _dev_layout(out)


def _bf_factors_g2(wts):
    out = np.empty((6, 64, 64, 64), dtype=np.float32)
    i = np.arange(64)[None, :]
    j = np.arange(64)[:, None]
    for sp in range(6):
        s = sp + 6
        sig = 1 << s
        w = np.asarray(wts[s]).reshape(4096 // (2 * sig), sig, 2, 2)
        for kp in range(64):
            nb = np.broadcast_to(j >> (sp + 1), (64, 64))
            low = np.broadcast_to(kp + 64 * (i & ((1 << sp) - 1)), (64, 64))
            r = np.broadcast_to((i >> sp) & 1, (64, 64))
            c = np.broadcast_to((j >> sp) & 1, (64, 64))
            out[sp, kp] = w[nb, low, r, c]
    return _dev_layout(out)



class Prog:
    def __init__(self, sublayers):
        self.nc = bass.Bass("TRN2", target_bir_lowering=False,
                            dynamic_dma_scratch_size=16384)
        self.S = Seq(self.nc)
        self.sub = sublayers
        self.dram = {}
        self.used = []
        self._scr = 0
        self._alloc()

    def inp(self, name, shape, dtype=F32):
        if name not in self.dram:
            self.dram[name] = self.nc.dram_tensor(name, list(shape), dtype,
                                                  kind="ExternalInput").ap()
            self.used.append(name)
        return self.dram[name]

    def tap(self, name, ap):
        if not DEBUG_TAPS:
            return
        sh = list(ap.shape)
        out = self.nc.dram_tensor("tap_" + name, sh, ap.dtype, kind="ExternalOutput").ap()
        self.S.dma(out, ap)

    def scratch(self, shape, dtype):
        self._scr += 1
        return self.nc.dram_tensor(f"scr{self._scr}", list(shape), dtype,
                                   kind="Internal").ap()

    def _alloc(self):
        nc = self.nc
        a = lambda nm, sh, dt: nc.alloc_sbuf_tensor(nm, sh, dt).ap()
        self.x = a("x", [128, NDT * T], F32)
        self.xn = a("xn", [128, NDT * T], BF16)
        self.sA = a("sA", [128, 24576], BF16)
        self.sB = a("sB", [128, 16384], BF16)
        self.sF = a("sF", [128, 3072], F32)
        self.sW = a("sW", [128, 3072], BF16)
        self.sC = a("sC", [128, 2048], F32)
        self.sR = a("sR", [128, 512], F32)
        self.k_id_bf = a("k_id_bf", [128, 128], BF16)
        self.k_id_f32 = a("k_id_f32", [128, 128], F32)
        self.k_cols = a("k_cols", [128, 2], BF16)
        self.k_row_f32 = a("k_row_f32", [1, 128], F32)
        self.k_misc = a("k_misc", [128, 4], F32)
        self.k_row_bf = a("k_row_bf", [1, 128], BF16)
        self.pA = nc.alloc_psum_tensor("pA", [128, 2048], F32).ap()
        self.pB = nc.alloc_psum_tensor("pB", [128, 1024], F32).ap()
        self.pC = nc.alloc_psum_tensor("pC", [128, 512], F32).ap()
        self.pT = nc.alloc_psum_tensor("pT", [128, 1024], BF16).ap()

    def load_consts(self):
        S = self.S
        S.dma(self.k_id_bf[:], self.inp("c_id_bf", [128, 128], BF16)[:])
        S.dma(self.k_id_f32[:], self.inp("c_id_f32", [128, 128], F32)[:])
        S.dma(self.k_cols[:], self.inp("c_cols_bf", [128, 2], BF16)[:])
        S.dma(self.k_row_f32[:], self.inp("c_ones_row_f32", [1, 128], F32)[:])
        S.dma(self.k_misc[:], self.inp("c_misc_cols", [128, 4], F32)[:])
        S.dma(self.k_row_bf[:], self.inp("c_ones_row_bf", [1, 128], BF16)[:])

    def xt(self, i, lo=0, hi=T):
        return self.x[:, i * T + lo:i * T + hi]

    def xnt(self, i, lo=0, hi=T):
        return self.xn[:, i * T + lo:i * T + hi]

    def load_wchunks(self, dst, dram2d, rows, cols, col0=0, ncols=None):
        w = ncols if ncols is not None else cols
        nch = rows // 128
        src = dram2d.rearrange("(c p) o -> p c o", p=128)[:, :, col0:col0 + w]
        self.S.dma(dst[:, 0:nch * w], src, q="gpsimd")

    def prelude(self, skip_xn=False):
        S = self.S
        xin = self.inp("xin", [D, T], F32)
        S.dma(self.x[:], xin.rearrange("(i p) t -> p i t", p=128))
        if not skip_xn:
            for i in range(NDT):
                S.copy(self.xnt(i), self.xt(i))

    def epilogue(self):
        S = self.S
        xout = self.nc.dram_tensor("xout", [D, T], F32, kind="ExternalOutput").ap()
        S.dma(xout.rearrange("(i p) t -> p i t", p=128), self.x[:])

    def bcast_row(self, psum_dst, row_ap):
        self.S.mm(psum_dst, self.k_row_f32[:], row_ap, start=True, stop=True)

    def postln(self, eps=1e-5):
        S = self.S
        inv = self.k_cols[:, 1:2]
        for i in range(NDT):
            S.copy(self.sB[:, i * T:(i + 1) * T], self.xt(i))
        for tc in range(NTC):
            lo = tc * SL
            for i in range(NDT):
                S.mm(self.pA[0:1, lo:lo + SL], inv,
                     self.sB[:, i * T + lo:i * T + lo + SL],
                     start=(i == 0), stop=(i == NDT - 1))
        mean_row = self.sF[0:1, 0:T]
        S.acopy(mean_row, self.pA[0:1, 0:T])
        for i in range(NDT):
            S.act(self.sB[:, i * T:(i + 1) * T], self.xt(i), AF.Square)
        for tc in range(NTC):
            lo = tc * SL
            for i in range(NDT):
                S.mm(self.pA[0:1, lo:lo + SL], inv,
                     self.sB[:, i * T + lo:i * T + lo + SL],
                     start=(i == 0), stop=(i == NDT - 1))
        for tc in range(NTC):
            lo = tc * SL
            t1 = self.sF[0:1, T + 0:T + SL]
            t2 = self.sF[0:1, T + SL:T + 2 * SL]
            mch = self.sF[0:1, lo:lo + SL]
            S.act(t1, mch, AF.Square)
            S.tt(t1, self.pA[0:1, lo:lo + SL], t1, ALU.subtract)
            S.act(t2, t1, AF.Ln, bias=self.k_misc[0:1, 0:1])
            S.act(t1, t2, AF.Exp, scale=-0.5)
            S.tt(t2, mch, t1, ALU.mult)
            self.bcast_row(self.pB[:, 0:SL], t1)
            self.bcast_row(self.pB[:, SL:2 * SL], t2)
            for i in range(NDT):
                xi = self.xt(i, lo, lo + SL)
                S.tt(xi, xi, self.pB[:, 0:SL], ALU.mult)
                S.tt(xi, xi, self.pB[:, SL:2 * SL], ALU.subtract)
                S.copy(self.xnt(i, lo, lo + SL), xi)

    def l2_attn(self):
        S = self.S
        winT = self.inp("d_L2_WinT_half", [D, D])
        wgT = self.inp("d_L2_WgateT", [D, D])
        binh = self.inp("d_L2_binh_cols", [128, 8])
        bg = self.inp("d_L2_bg_cols", [128, 8])
        S.dma(self.sR[:, 0:8], binh[:])
        S.dma(self.sR[:, 8:16], bg[:])
        for i in range(NDT):
            self.load_wchunks(self.sW[:, 0:1024], winT, D, 128, col0=i * 128, ncols=128)
            self.load_wchunks(self.sW[:, 1024:2048], wgT, D, 128, col0=i * 128, ncols=128)
            for tc in range(NTC):
                lo = tc * SL
                for c in range(NDT):
                    S.mm(self.pA[:, lo:lo + SL], self.sW[:, c * 128:(c + 1) * 128],
                         self.xnt(c, lo, lo + SL), start=(c == 0), stop=(c == NDT - 1))
                for c in range(NDT):
                    S.mm(self.pB[:, 0:SL], self.sW[:, 1024 + c * 128:1024 + (c + 1) * 128],
                         self.xnt(c, lo, lo + SL), start=(c == 0), stop=(c == NDT - 1))
                S.acopy(self.sB[:, 0:SL], self.pA[:, lo:lo + SL], bias=self.sR[:, i:i + 1])
                S.act(self.sB[:, SL:2 * SL], self.pB[:, 0:SL], AF.Sigmoid,
                      bias=self.sR[:, 8 + i:9 + i])
                S.tt(self.sA[:, i * T + lo:i * T + lo + SL],
                     self.sB[:, 0:SL], self.sB[:, SL:2 * SL], ALU.mult)
        self.tap("u", self.sA[:, 0:NDT * T])
        self.tap("x_a", self.x[:])
        bmT = self.inp("d_L2_BmT", [D, 16])
        self.load_wchunks(self.sW[:, 0:128], bmT, D, 16, col0=0, ncols=16)
        for tc in range(NTC):
            lo = tc * SL
            for c in range(NDT):
                S.mm(self.pA[0:16, lo:lo + SL], self.sW[:, c * 16:(c + 1) * 16],
                     self.sA[:, c * T + lo:c * T + lo + SL],
                     start=(c == 0), stop=(c == NDT - 1))
        H = self.sB[0:16, 0:T]
        S.copy(H, self.pA[0:16, 0:T])
        self.tap("uB", H)
        self.tap("x_b", self.x[:])
        S.dma(self.sW[0:16, 2048:2064], self.inp("d_L2_A", [16, 16])[:], q="gpsimd")
        S.dma(self.sW[0:16, 2064:2080], self.inp("d_L2_AT", [16, 16])[:], q="gpsimd")
        Msb = self.sW[0:16, 2048:2064]
        Tsb = self.sW[0:16, 2064:2080]
        for k in range(11):
            sh = 1 << k
            for tc in range(NTC):
                lo = tc * SL
                r0 = max(lo, sh)
                if r0 >= (tc + 1) * SL:
                    continue
                n = (tc + 1) * SL - r0
                S.mm(self.pA[0:16, r0:r0 + n], Tsb, self.sB[0:16, r0 - sh:r0 - sh + n],
                     start=True, stop=True)
            S.tt(self.sB[0:16, sh:T], self.sB[0:16, sh:T], self.pA[0:16, sh:T], ALU.add)
            if k < 10:
                S.mm(self.pC[0:16, 0:16], Tsb, Msb, start=True, stop=True)
                S.copy(Msb, self.pC[0:16, 0:16])
                S.trans(self.pT[0:16, 0:16], Msb, self.k_id_bf[0:16, 0:16])
                S.copy(Tsb, self.pT[0:16, 0:16])
        self.tap("H", H)
        self.tap("x_c", self.x[:])
        cmT = self.inp("d_L2_CmT", [16, D])
        S.dma(self.sW[0:16, 0:1024], cmT[:], q="gpsimd")
        for i in range(NDT):
            for tc in range(NTC):
                lo = tc * SL
                S.mm(self.pA[:, lo:lo + SL], self.sW[0:16, i * 128:(i + 1) * 128],
                     self.sB[0:16, lo:lo + SL], start=True, stop=True)
            S.copy(self.sA[:, i * T:(i + 1) * T], self.pA[:, 0:T])
        self.tap("y", self.sA[:, 0:NDT * T])
        self.tap("xpre", self.x[:])
        woT = self.inp("d_L2_WoutT", [D, D])
        bo = self.inp("d_L2_bo_cols", [128, 8])
        S.dma(self.sR[:, 16:24], bo[:])
        for i in range(NDT):
            self.load_wchunks(self.sW[:, 0:1024], woT, D, 128, col0=i * 128, ncols=128)
            for tc in range(NTC):
                lo = tc * SL
                for c in range(NDT):
                    S.mm(self.pA[:, lo:lo + SL], self.sW[:, c * 128:(c + 1) * 128],
                         self.sA[:, c * T + lo:c * T + lo + SL],
                         start=(c == 0), stop=(c == NDT - 1))
                xi = self.xt(i, lo, lo + SL)
                S.ts(self.sC[:, 0:SL], self.pA[:, lo:lo + SL],
                     self.sR[:, 16 + i:17 + i], None, ALU.add)
                S.tt(xi, xi, self.sC[:, 0:SL], ALU.add)
        self.tap("xres", self.x[:])
        self.postln()
        self.tap("xfin", self.x[:])


    def _rmsnorm_l3(self):
        S = self.S
        ns = self.inp("d_L3_ns_cols", [128, 8])
        S.dma(self.sR[:, 48:56], ns[:])
        inv = self.k_cols[:, 1:2]
        for i in range(NDT):
            S.act(self.sB[:, i * T:(i + 1) * T], self.xt(i), AF.Square)
        for tc in range(NTC):
            lo = tc * SL
            for i in range(NDT):
                S.mm(self.pA[0:1, lo:lo + SL], inv,
                     self.sB[:, i * T + lo:i * T + lo + SL],
                     start=(i == 0), stop=(i == NDT - 1))
        for tc in range(NTC):
            lo = tc * SL
            t1 = self.sF[0:1, 0:SL]
            S.act(t1, self.pA[0:1, lo:lo + SL], AF.Ln, bias=self.k_misc[0:1, 1:2])
            S.act(t1, t1, AF.Exp, scale=-0.5)
            self.bcast_row(self.pB[:, 0:SL], t1)
            for i in range(NDT):
                S.tt(self.sC[:, 0:SL], self.xt(i, lo, lo + SL),
                     self.pB[:, 0:SL], ALU.mult)
                S.ts(self.xnt(i, lo, lo + SL), self.sC[:, 0:SL],
                     self.sR[:, 48 + i:49 + i], None, ALU.mult)

    def _attention(self, layer):
        S = self.S
        l0 = (layer == 0)
        cosd = self.inp("c_cos_l0" if l0 else "c_cos_l3", [128, T], BF16)
        sind = self.inp("c_sin_l0" if l0 else "c_sin_l3", [128, T], BF16)
        S.dma(self.sB[:, 4096:6144], cosd[:])
        S.dma(self.sB[:, 6144:8192], sind[:])
        cos = self.sB[:, 4096:6144]
        sin = self.sB[:, 6144:8192]
        EOFF = 8192
        if l0:
            esh = self.inp("d_eshift_l0", [16, 128, 1536])
            tau = self.inp("d_tau_bc", [128, 16])
            rr = self.inp("c_rowrecip", [128, 16])
            S.dma(self.sR[:, 24:40], tau[:])
            S.dma(self.sR[:, 40:56], rr[:])
        else:
            e3 = self.inp("c_e3shift_bf", [128, 640], BF16)
            S.dma(self.sB[:, EOFF:EOFF + 640], e3[:])
        if l0:
            wq = self.inp("d_L0_WqT", [D, D])
            wk = self.inp("d_L0_WkT", [D, D])
            wv = self.inp("d_L0_WvT", [D, D])
            wo = self.inp("d_L0_WoT", [D, D])
        else:
            wq = self.inp("L3_attn_Wq", [D, D])
            wk = self.inp("L3_attn_Wk", [D, D])
            wv = self.inp("L3_attn_Wv", [D, D])
            wo = self.inp("L3_attn_Wo", [D, D])
        QOF, KOF, VOF = 16384, 18432, 20480
        for hp in range(8):
            if l0:
                for hh in range(2):
                    S.dma(self.sB[:, EOFF + hh * 1536:EOFF + (hh + 1) * 1536],
                          esh[2 * hp + hh], q="gpsimd")
            for (wap, dst) in ((wq, QOF), (wk, KOF)):
                self.load_wchunks(self.sW[:, 0:1024], wap, D, 128,
                                  col0=hp * 128, ncols=128)
                ov = self.sW[:, 1024:2048].rearrange("p (k h f c) -> p k h f c", k=8, h=2, f=2)
                iv = self.sW[:, 0:1024].rearrange("p (k h f c) -> p k h f c", k=8, h=2, f=2)
                S.act(ov[:, :, :, 0, :], iv[:, :, :, 1, :], AF.Copy, scale=-1.0)
                S.copy(ov[:, :, :, 1, :], iv[:, :, :, 0, :])
                for tc in range(NTC):
                    lo = tc * SL
                    for c in range(NDT):
                        S.mm(self.pA[:, lo:lo + SL],
                             self.sW[:, c * 128:(c + 1) * 128],
                             self.xnt(c, lo, lo + SL),
                             start=(c == 0), stop=(c == NDT - 1))
                    for c in range(NDT):
                        S.mm(self.pB[:, 0:SL],
                             self.sW[:, 1024 + c * 128:1024 + (c + 1) * 128],
                             self.xnt(c, lo, lo + SL),
                             start=(c == 0), stop=(c == NDT - 1))
                    S.tt(self.sC[:, 0:SL], self.pA[:, lo:lo + SL],
                         cos[:, lo:lo + SL], ALU.mult)
                    S.tt(self.sC[:, SL:2 * SL], self.pB[:, 0:SL],
                         sin[:, lo:lo + SL], ALU.mult)
                    S.tt(self.sA[:, dst + lo:dst + lo + SL],
                         self.sC[:, 0:SL], self.sC[:, SL:2 * SL], ALU.add)
            self.load_wchunks(self.sW[:, 0:1024], wv, D, 128,
                              col0=hp * 128, ncols=128)
            for tt in range(16):
                t0 = tt * 128
                for c in range(NDT):
                    S.mm(self.pC[:, 0:128], self.xnt(c, t0, t0 + 128),
                         self.sW[:, c * 128:(c + 1) * 128],
                         start=(c == 0), stop=(c == NDT - 1))
                S.copy(self.sA[:, VOF + t0:VOF + t0 + 128], self.pC[:, 0:128])
            if hp == 0:
                self.tap("q0", self.sA[:, QOF:QOF + T])
                self.tap("k0", self.sA[:, KOF:KOF + T])
                self.tap("v0", self.sA[:, VOF:VOF + T])
            for hh in range(2):
                hb = hh * 64
                hglob = 2 * hp + hh
                for qt in range(16):
                    q0 = qt * 128
                    KF = q0 + 128
                    nch = (KF + SL - 1) // SL
                    for cc in range(nch):
                        k0 = cc * SL
                        N = min(SL, KF - k0)
                        pre = (q0 - (k0 + N - 1) <= 512) if l0 else (k0 + N == KF)
                        if pre:
                            if l0:
                                off = 1023 + k0 - q0 + hh * 1536
                            else:
                                off = 511 + k0 - q0
                            S.mm(self.pA[:, k0:k0 + N], self.k_id_bf[:],
                                 self.sB[:, EOFF + off:EOFF + off + N],
                                 start=True, stop=False)
                        S.mm(self.pA[:, k0:k0 + N],
                             self.sA[hb:hb + 64, QOF + q0:QOF + q0 + 128],
                             self.sA[hb:hb + 64, KOF + k0:KOF + k0 + N],
                             start=not pre, stop=True)
                    den = self.sR[:, 56:57]
                    rd = self.sR[:, 57:58]
                    if hp == 0 and hh == 0 and qt == 5 and DEBUG_TAPS:
                        S.copy(self.sC[:, 0:KF], self.pA[:, 0:KF])
                        self.tap("sc5", self.sC[:, 0:KF])
                    S.act(self.sB[:, 0:KF], self.pA[:, 0:KF], AF.Exp,
                          accum_out=den)
                    if hp == 0 and hh == 0 and qt == 5:
                        self.tap("exp5", self.sB[:, 0:KF])
                        self.tap("den5", den)
                    S.recip(rd, den)
                    if l0:
                        c2 = self.sR[:, 58:59]
                        c2d = self.sR[:, 59:60]
                        S.tt(c2, self.sR[:, 24 + hglob:25 + hglob],
                             self.sR[:, 40 + qt:41 + qt], ALU.mult)
                        S.tt(c2d, c2, den, ALU.mult)
                        S.ts(self.sB[:, 0:KF], self.sB[:, 0:KF], c2d, 0.0,
                             ALU.add, ALU.max)
                    S.ts(self.sB[:, 0:KF], self.sB[:, 0:KF], rd, None, ALU.mult)
                    nkc = KF // 128
                    for g in range((nkc + 7) // 8):
                        b0 = g * 8
                        nb = min(8, nkc - b0)
                        for j in range(nb):
                            S.trans(self.pT[:, j * 128:(j + 1) * 128],
                                    self.sB[:, (b0 + j) * 128:(b0 + j + 1) * 128],
                                    self.k_id_bf[:])
                        S.copy(self.sB[:, 2048 + b0 * 128:2048 + (b0 + nb) * 128],
                               self.pT[:, 0:nb * 128])
                    for kc in range(nkc):
                        S.mm(self.pC[hb:hb + 64, 0:128],
                             self.sA[:, VOF + kc * 128 + hb:VOF + kc * 128 + hb + 64],
                             self.sB[:, 2048 + kc * 128:2048 + (kc + 1) * 128],
                             start=(kc == 0), stop=(kc == nkc - 1))
                    S.copy(self.sA[hb:hb + 64, hp * T + q0:hp * T + q0 + 128],
                           self.pC[hb:hb + 64, 0:128])
                    if hp == 0 and hh == 0 and qt == 5:
                        self.tap("eT5", self.sB[:, 2048:2048 + KF])
                        self.tap("av5", self.sA[0:64, q0:q0 + 128])
        for i in range(NDT):
            self.load_wchunks(self.sW[:, 0:1024], wo, D, 128,
                              col0=i * 128, ncols=128)
            for tc in range(NTC):
                lo = tc * SL
                for c in range(NDT):
                    S.mm(self.pA[:, lo:lo + SL], self.sW[:, c * 128:(c + 1) * 128],
                         self.sA[:, c * T + lo:c * T + lo + SL],
                         start=(c == 0), stop=(c == NDT - 1))
                xi = self.xt(i, lo, lo + SL)
                S.tt(xi, xi, self.pA[:, lo:lo + SL], ALU.add)
        self.postln()

    def l0_attn(self):
        self._attention(0)

    def l3_attn(self):
        self._rmsnorm_l3()
        self._attention(3)

    def l0_ffn(self):
        S = self.S
        wr = self.inp("d_L0f_wr_cols", [128, 8])
        b1c = self.inp("d_L0f_b1_cols", [128, 32])
        b2c = self.inp("d_L0f_b2_cols", [128, 8])
        W1 = self.inp("L0_ffn_W1", [D, 4096])
        W2 = self.inp("L0_ffn_W2", [4096, D])
        S.dma(self.sR[:, 96:104], wr[:])
        S.dma(self.sR[:, 184:216], b1c[:])
        S.dma(self.sR[:, 216:224], b2c[:])
        S.act(self.sR[:, 224:232], self.sR[:, 216:224], AF.Copy, scale=2.0)
        for tc in range(NTC):
            lo = tc * SL
            for c in range(NDT):
                S.mm(self.pA[0:1, lo:lo + SL], self.sR[:, 96 + c:97 + c],
                     self.xt(c, lo, lo + SL), start=(c == 0), stop=(c == NDT - 1))
        srow = self.sF[0:1, 0:T]
        S.acopy(srow, self.pA[0:1, 0:T])
        scr_s = self.scratch([T], F32)
        S.dma(scr_s, srow)
        for j in range(16):
            S.trans(self.pB[:, j:j + 1], srow[0:1, j * 128:(j + 1) * 128],
                    self.k_id_f32[0:1, 0:1])
        S.copy(self.sR[:, 104:120], self.pB[:, 0:16])
        sbrd = self.sC[:, 0:T]
        S.dma(sbrd, bass.AP(tensor=scr_s.tensor, offset=0, ap=[[0, 128], [1, T]]))
        self.tap("sbrd", sbrd)
        for j in range(16):
            S.ts(self.sB[:, 0:T], sbrd, self.sR[:, 104 + j:105 + j], 0.0,
                 ALU.is_gt, ALU.add, accum_out=self.sR[:, 120 + j:121 + j])
            if j == 0:
                self.tap("cmp0", self.sB[:, 0:T])
        S.tss(self.sR[:, 136:152], self.sR[:, 120:136], 1024.0, ALU.is_lt)
        S.act(self.sR[:, 152:168], self.sR[:, 104:120], AF.Sigmoid)
        S.tt(self.sR[:, 168:184], self.sR[:, 152:168], self.sR[:, 136:152],
             ALU.mult)
        S.act(self.sR[:, 168:184], self.sR[:, 168:184], AF.Copy, scale=0.5)
        scr_w = self.scratch([T], F32)
        S.trans(self.pB[0:16, 0:128], self.sR[:, 168:184], self.k_id_f32[:])
        S.copy(self.sF[0:16, 0:128], self.pB[0:16, 0:128])
        S.dma(scr_w.rearrange("(j p) -> j p", p=128), self.sF[0:16, 0:128])
        wtb = self.sB[:, 0:T]
        S.dma(wtb, bass.AP(tensor=scr_w.tensor, offset=0, ap=[[0, 128], [1, T]]),
              q="gpsimd")
        self.tap("srow", srow)
        self.tap("scol", self.sR[:, 104:120])
        self.tap("rank", self.sR[:, 120:136])
        self.tap("wt", self.sR[:, 168:184])
        self.tap("wtb", wtb)
        A0, B0 = 0.7978845608028654, 0.0356774081363001

        def h1sl(ft, lo, hi):
            if ft < 24:
                return self.sA[:, ft * 1024 + lo:ft * 1024 + hi]
            return self.sB[:, 2048 + (ft - 24) * 1024 + lo:2048 + (ft - 24) * 1024 + hi]

        for th in range(2):
            t0 = th * 1024
            for ft in range(32):
                self.load_wchunks(self.sW[:, 0:1024], W1, D, 128,
                                  col0=ft * 128, ncols=128)
                for tcc in range(2):
                    lo = t0 + tcc * SL
                    for c in range(NDT):
                        S.mm(self.pC[:, 0:SL], self.sW[:, c * 128:(c + 1) * 128],
                             self.xnt(c, lo, lo + SL),
                             start=(c == 0), stop=(c == NDT - 1))
                    b1 = self.sR[:, 184 + ft:185 + ft]
                    uf = self.sW[:, 1024:1536]
                    s2 = self.sW[:, 1536:2048]
                    vv = self.sW[:, 2048:2560]
                    ww = self.sW[:, 2560:3072]
                    S.ts(uf, self.pC[:, 0:SL], b1, None, ALU.add)
                    S.act(s2, self.pC[:, 0:SL], AF.Square, bias=b1)
                    S.ts(vv, s2, B0, A0, ALU.mult, ALU.add)
                    S.tt(ww, vv, uf, ALU.mult)
                    S.act(vv, ww, AF.Tanh)
                    S.stt(h1sl(ft, tcc * SL, tcc * SL + SL), vv, 1.0, uf,
                          ALU.add, ALU.mult)
            if th == 0:
                self.tap("h1_f0", h1sl(0, 0, 1024))
            for dt in range(NDT):
                for kh in range(2):
                    src = W2.rearrange("(c p) o -> p c o", p=128)
                    S.dma(self.sW[:, 0:2048],
                          src[:, kh * 16:(kh + 1) * 16, dt * 128:(dt + 1) * 128],
                          q="gpsimd")
                    for tcc in range(2):
                        ptile = self.pC[:, 0:SL] if tcc == 0 else self.pB[:, 0:SL]
                        for c in range(16):
                            ft = kh * 16 + c
                            S.mm(ptile, self.sW[:, c * 128:(c + 1) * 128],
                                 h1sl(ft, tcc * SL, tcc * SL + SL),
                                 start=(kh == 0 and c == 0),
                                 stop=(kh == 1 and c == 15))
                for tcc in range(2):
                    lo = t0 + tcc * SL
                    ptile = self.pC[:, 0:SL] if tcc == 0 else self.pB[:, 0:SL]
                    tv = self.sC[:, 0:SL]
                    t2 = self.sC[:, SL:2 * SL]
                    S.tt(tv, ptile, wtb[:, lo:lo + SL], ALU.mult)
                    S.stt(t2, wtb[:, lo:lo + SL], self.sR[:, 224 + dt:225 + dt],
                          tv, ALU.mult, ALU.add)
                    xi = self.xt(dt, lo, lo + SL)
                    S.stt(xi, xi, 2.0, t2, ALU.mult, ALU.add)
        self.postln()

    def l3_ffn(self):
        S = self.S
        rw = self.inp("L3_ffn_router_w", [D, 8])
        rb = self.inp("L3_ffn_router_b", [8])
        W1 = self.inp("L3_ffn_W1", [8, D, 512])
        W2 = self.inp("L3_ffn_W2", [8, 512, D])
        b1c = self.inp("d_L3f_b1_cols", [8, 128, 4])
        b2 = self.inp("L3_ffn_b2", [8, D])
        self.load_wchunks(self.sW[:, 0:64], rw, D, 8, col0=0, ncols=8)
        S.dma(self.sR[0:8, 0:1], rb.rearrange("(e o) -> e o", o=1))
        for tc in range(NTC):
            lo = tc * SL
            for c in range(NDT):
                S.mm(self.pB[0:8, 0:SL], self.sW[:, c * 8:(c + 1) * 8],
                     self.xnt(c, lo, lo + SL), start=(c == 0), stop=(c == NDT - 1))
            S.ts(self.sF[0:8, lo:lo + SL], self.pB[0:8, 0:SL],
                 self.sR[0:8, 0:1], None, ALU.add)
        for j in range(16):
            S.trans(self.pA[:, j * 8:(j + 1) * 8],
                    self.sF[0:8, j * 128:(j + 1) * 128], self.k_id_f32[0:8, 0:8])
        lgT = self.sC[:, 0:128]
        S.copy(lgT, self.pA[:, 0:128])
        v3 = lambda ap: ap.rearrange("p (j e) -> p j e", e=8)
        m1 = self.sC[:, 384:400]
        m2 = self.sC[:, 400:416]
        s12 = self.sC[:, 416:432]
        w1 = self.sC[:, 128:256]
        w2 = self.sC[:, 256:384]
        gm = self.sC[:, 432:560]
        rmt = self.sC[:, 560:688]
        S.red(m1, v3(lgT), ALU.max)
        m1b = m1.unsqueeze(2).broadcast_to((128, 16, 8))
        S.tt(v3(w1), v3(lgT), m1b, ALU.subtract)
        S.act(gm, w1, AF.Exp)
        S.tt(v3(w1), v3(lgT), m1b, ALU.is_equal)
        S.stt(w2, w1, -1.0e30, lgT, ALU.mult, ALU.add)
        S.red(m2, v3(w2), ALU.max)
        m2b = m2.unsqueeze(2).broadcast_to((128, 16, 8))
        S.tt(v3(w2), v3(w2), m2b, ALU.is_equal)
        S.tt(w1, w1, w2, ALU.add)
        S.tt(gm, gm, w1, ALU.mult)
        S.red(s12, v3(gm), ALU.add)
        S.recip(s12, s12)
        s12b = s12.unsqueeze(2).broadcast_to((128, 16, 8))
        S.tt(v3(rmt), v3(gm), s12b, ALU.mult)
        for j in range(16):
            S.trans(self.pA[0:8, j * 128:(j + 1) * 128],
                    rmt[:, j * 8:(j + 1) * 8], self.k_id_f32[:])
        S.copy(self.sB[0:8, 0:T], self.pA[0:8, 0:T])
        scr_rm = self.scratch([8, T], BF16)
        S.dma(scr_rm, self.sB[0:8, 0:T])
        S.dma(self.sB[0:8, T:T + D], b2, q="gpsimd")
        for e in range(8):
            S.dma(self.sR[:, 64 + e * 4:68 + e * 4], b1c[e])
        for th in range(2):
            t0 = th * 1024
            for e in range(8):
                rbc = self.sB[:, 2048 + 8 * 1024:2048 + 9 * 1024]
                S.dma(rbc, bass.AP(tensor=scr_rm.tensor, offset=e * T + t0,
                                   ap=[[0, 128], [1, 1024]]))
                for fq in range(4):
                    src = W1.rearrange("e (c p) f -> e p c f", p=128)
                    S.dma(self.sW[:, 0:1024],
                          src[e][:, :, fq * 128:(fq + 1) * 128], q="gpsimd")
                    for tcc in range(2):
                        lo = t0 + tcc * SL
                        for c in range(NDT):
                            S.mm(self.pC[:, 0:SL], self.sW[:, c * 128:(c + 1) * 128],
                                 self.xnt(c, lo, lo + SL),
                                 start=(c == 0), stop=(c == NDT - 1))
                        tmp = self.sW[:, 1024:1536]
                        S.act(tmp, self.pC[:, 0:SL], AF.Gelu,
                              bias=self.sR[:, 64 + e * 4 + fq:65 + e * 4 + fq])
                        S.tt(self.sA[:, fq * 1024 + tcc * SL:fq * 1024 + tcc * SL + SL],
                             tmp, rbc[:, tcc * SL:tcc * SL + SL], ALU.mult)
                for dtp in range(4):
                    src2 = W2.rearrange("e (c p) o -> e p c o", p=128)
                    S.dma(self.sW[:, 0:1024],
                          src2[e][:, :, dtp * 256:(dtp + 1) * 256], q="gpsimd")
                    for dh in range(2):
                        dt = dtp * 2 + dh
                        for tcc in range(2):
                            lo = t0 + tcc * SL
                            for c in range(4):
                                S.mm(self.pC[:, 0:SL],
                                     self.sW[:, c * 256 + dh * 128:c * 256 + dh * 128 + 128],
                                     self.sA[:, c * 1024 + tcc * SL:c * 1024 + tcc * SL + SL],
                                     start=(c == 0), stop=(c == 3))
                            xi = self.xt(dt, lo, lo + SL)
                            S.tt(xi, xi, self.pC[:, 0:SL], ALU.add)
            for dt in range(NDT):
                for tcc in range(2):
                    lo = t0 + tcc * SL
                    S.mm(self.pC[:, 0:SL],
                         self.sB[0:8, T + dt * 128:T + (dt + 1) * 128],
                         self.sB[0:8, lo:lo + SL], start=True, stop=True)
                    xi = self.xt(dt, lo, lo + SL)
                    S.tt(xi, xi, self.pC[:, 0:SL], ALU.add)
        self.postln()


    def _h1sl(self, ft, lo, hi):
        if ft < 24:
            return self.sA[:, ft * 1024 + lo:ft * 1024 + hi]
        return self.sB[:, (ft - 24) * 1024 + lo:(ft - 24) * 1024 + hi]

    def l1_ffn(self):
        S = self.S
        Wexp = self.inp("L1_ffn_W_exp", [D, 4096])
        bexp = self.inp("d_bexp_cols", [128, 32])
        upb = self.inp("d_upbiasP_cols", [128, 32])
        WconP = self.inp("d_W_conP", [4096, D])
        bcon = self.inp("d_bcon_cols", [128, 8])
        dbias = self.inp("d_dbias_cols", [128, 8])
        S.dma(self.sR[:, 64:96], bexp[:])
        S.dma(self.sR[:, 96:128], upb[:])
        S.dma(self.sR[:, 128:136], bcon[:])
        S.dma(self.sR[:, 136:144], dbias[:])
        S.tt(self.sR[:, 128:136], self.sR[:, 128:136], self.sR[:, 136:144], ALU.add)
        CTOF = {"up1": 8192, "up2": 10240, "dn1": 12288, "dn2": 14336}
        for gname in ("up1", "up2", "dn1", "dn2"):
            fac = self.inp(f"d_bf_{gname}", [128, 12288])
            S.dma(self.sA[:, 0:12288], fac[:], q="gpsimd")
            f = lambda s: self.sA[:, s * 2048:(s + 1) * 2048]
            pong = self.sA[:, 12288:14336]
            ping = self.sW[:, 0:2048]
            S.tt(pong, f(0), f(1), ALU.mult)
            S.tt(ping, pong, f(2), ALU.mult)
            S.tt(pong, ping, f(3), ALU.mult)
            S.tt(ping, pong, f(4), ALU.mult)
            S.tt(self.sB[:, CTOF[gname]:CTOF[gname] + 2048], ping, f(5), ALU.mult)

        def apply_group(gname, t0, gelu=False):
            off = CTOF[gname]
            for m in range(32):
                for tcc in range(2):
                    lo = t0 + tcc * SL
                    hl = self._h1sl(m, (lo - t0), (lo - t0) + SL)
                    for kb in range(2):
                        S.mm(self.pC[kb * 64:kb * 64 + 64, 0:SL],
                             self.sB[kb * 64:kb * 64 + 64, off + m * 64:off + (m + 1) * 64],
                             self._h1sl(m, (lo - t0), (lo - t0) + SL)[kb * 64:kb * 64 + 64, :],
                             start=True, stop=True)
                    if gelu:
                        S.act(hl, self.pC[:, 0:SL], AF.Gelu,
                              bias=self.sR[:, 96 + m:97 + m])
                    else:
                        S.copy(hl, self.pC[:, 0:SL])

        scr_h = self.scratch([4096, 1024], BF16)

        def permute(t0):
            S.dma(scr_h.rearrange("(i p) t -> p i t", p=128)[:, 0:24, :],
                  self.sA[:, 0:24576])
            S.dma(scr_h.rearrange("(i p) t -> p i t", p=128)[:, 24:32, :],
                  self.sB[:, 0:8192])
            for p1 in range(2):
                S.dma(self.sA[p1 * 64:(p1 + 1) * 64, 0:24576]
                      .rearrange("p (i t) -> p i t", t=1024),
                      bass.AP(tensor=scr_h.tensor, offset=p1 * 1024,
                              ap=[[65536, 64], [2048, 24], [1, 1024]]))
                S.dma(self.sB[p1 * 64:(p1 + 1) * 64, 0:8192]
                      .rearrange("p (i t) -> p i t", t=1024),
                      bass.AP(tensor=scr_h.tensor, offset=p1 * 1024 + 48 * 1024,
                              ap=[[65536, 64], [2048, 8], [1, 1024]]))

        for th in range(2):
            t0 = th * 1024
            for ft in range(32):
                self.load_wchunks(self.sW[:, 0:1024], Wexp, D, 128,
                                  col0=ft * 128, ncols=128)
                for tcc in range(2):
                    lo = t0 + tcc * SL
                    for c in range(NDT):
                        S.mm(self.pC[:, 0:SL], self.sW[:, c * 128:(c + 1) * 128],
                             self.xnt(c, lo, lo + SL),
                             start=(c == 0), stop=(c == NDT - 1))
                    S.ts(self._h1sl(ft, tcc * SL, tcc * SL + SL), self.pC[:, 0:SL],
                         self.sR[:, 64 + ft:65 + ft], None, ALU.add)
            apply_group("up1", t0)
            permute(t0)
            apply_group("up2", t0, gelu=True)
            permute(t0)
            apply_group("dn1", t0)
            permute(t0)
            apply_group("dn2", t0)
            for dt in range(NDT):
                for kh in range(2):
                    src = WconP.rearrange("(c p) o -> p c o", p=128)
                    S.dma(self.sW[:, 0:2048],
                          src[:, kh * 16:(kh + 1) * 16, dt * 128:(dt + 1) * 128],
                          q="gpsimd")
                    for tcc in range(2):
                        ptile = self.pC[:, 0:SL] if tcc == 0 else self.pB[:, 0:SL]
                        for c in range(16):
                            S.mm(ptile, self.sW[:, c * 128:(c + 1) * 128],
                                 self._h1sl(kh * 16 + c, tcc * SL, tcc * SL + SL),
                                 start=(kh == 0 and c == 0),
                                 stop=(kh == 1 and c == 15))
                for tcc in range(2):
                    lo = t0 + tcc * SL
                    ptile = self.pC[:, 0:SL] if tcc == 0 else self.pB[:, 0:SL]
                    S.ts(self.sC[:, 0:SL], ptile, self.sR[:, 128 + dt:129 + dt],
                         None, ALU.add)
                    xi = self.xt(dt, lo, lo + SL)
                    S.tt(xi, xi, self.sC[:, 0:SL], ALU.add)
        self.postln()

    def _quant(self, dst, src, width, btmp):
        S = self.S
        nb = width // 16
        mx = self.sR[:, 320:320 + nb]
        sc = self.sR[:, 408:408 + nb]
        B1, B2, B3 = btmp
        v = self.sF[:, 0:width]
        ftmp = self.sF[:, width:2 * width]
        vview = lambda ap: ap.rearrange("p (b g) -> p b g", g=16)
        S.red(mx[:, 0:nb], vview(src), ALU.max, absval=True)
        S.ts(sc[:, 0:nb], mx[:, 0:nb], 1.0 / 6.0, 1e-12, ALU.mult, ALU.max)
        S.recip(sc[:, 0:nb], sc[:, 0:nb])
        S.act(v, src, AF.Abs)
        rcb = sc[:, 0:nb].unsqueeze(2).broadcast_to((128, nb, 16))
        S.tt(vview(v), vview(v), rcb, ALU.mult)
        S.ts(ftmp, v, 2.25, 2.0, ALU.min, ALU.mult)
        S.ts(B2, ftmp, MAGIC, MAGIC, ALU.add, ALU.subtract)
        S.ts(ftmp, v, 2.25, 5.0, ALU.max, ALU.min)
        S.ts(B3, ftmp, MAGIC, MAGIC, ALU.add, ALU.subtract)
        S.ts(B1, B3, 4.0, -2.0, ALU.min, ALU.add)
        S.stt(B3, B2, 0.5, B1, ALU.mult, ALU.add)
        S.ts(B1, v, 5.0, 2.0, ALU.is_gt, ALU.mult)
        S.tt(B2, B3, B1, ALU.add)
        S.act(B1, src, AF.Sign)
        S.tt(B3, B2, B1, ALU.mult)
        S.recip(sc[:, 0:nb], sc[:, 0:nb])
        scb = sc[:, 0:nb].unsqueeze(2).broadcast_to((128, nb, 16))
        S.tt(vview(dst), vview(B3), scb, ALU.mult)

    def l2_ffn(self):
        S = self.S
        wg = self.inp("L2_ffn_w_gate", [2816, D])
        wu = self.inp("L2_ffn_w_up", [2816, D])
        wd = self.inp("L2_ffn_w_down", [D, 2816])
        scr_h = self.scratch([2816, T], BF16)
        scr_hq = self.scratch([T, 2816], BF16)
        scr_wdT = self.scratch([2816, D], BF16)
        btmp1k = (self.sB[:, 1024:2048], self.sB[:, 2048:3072], self.sB[:, 3072:4096])
        for tt in range(16):
            t0 = tt * 128
            for i in range(NDT):
                S.trans(self.pA[:, i * 128:(i + 1) * 128],
                        self.xt(i, t0, t0 + 128), self.k_id_f32[:])
            S.copy(self.sC[:, 0:1024], self.pA[:, 0:1024])
            self._quant(self.sB[:, 0:1024], self.sC[:, 0:1024], 1024, btmp1k)
            for i in range(NDT):
                S.trans(self.pT[:, i * 128:(i + 1) * 128],
                        self.sB[:, i * 128:(i + 1) * 128], self.k_id_bf[:])
            for i in range(NDT):
                S.copy(self.xnt(i, t0, t0 + 128), self.pT[:, i * 128:(i + 1) * 128])
        for rt in range(22):
            r0 = rt * 128
            for (wap, slot) in ((wg, 4096), (wu, 5120)):
                S.dma(self.sC[:, 0:1024], wap[r0:r0 + 128, :])
                self._quant(self.sB[:, 0:1024], self.sC[:, 0:1024], 1024, btmp1k)
                for i in range(NDT):
                    S.trans(self.pT[:, i * 128:(i + 1) * 128],
                            self.sB[:, i * 128:(i + 1) * 128], self.k_id_bf[:])
                S.copy(self.sB[:, slot:slot + 1024], self.pT[:, 0:1024])
            for tc in range(NTC):
                lo = tc * SL
                for c in range(NDT):
                    S.mm(self.pC[:, 0:SL], self.sB[:, 4096 + c * 128:4096 + (c + 1) * 128],
                         self.xnt(c, lo, lo + SL), start=(c == 0), stop=(c == NDT - 1))
                for c in range(NDT):
                    S.mm(self.pB[:, 0:SL], self.sB[:, 5120 + c * 128:5120 + (c + 1) * 128],
                         self.xnt(c, lo, lo + SL), start=(c == 0), stop=(c == NDT - 1))
                S.act(self.sW[:, 0:SL], self.pC[:, 0:SL], AF.Silu)
                S.tt(self.sW[:, SL:2 * SL], self.sW[:, 0:SL], self.pB[:, 0:SL],
                     ALU.mult)
                S.dma(scr_h[r0:r0 + 128, lo:lo + SL], self.sW[:, SL:2 * SL])
        wdv = scr_wdT.rearrange("(kb p) o -> p kb o", p=128)
        btmp14 = (self.sB[:, 1408:2816], self.sB[:, 2816:4224], self.sB[:, 4224:5632])
        for rt in range(NDT):
            for ch in range(2):
                wcols = 1408
                S.dma(self.sC[:, 0:wcols], wd[rt * 128:(rt + 1) * 128,
                                              ch * wcols:(ch + 1) * wcols])
                self._quant(self.sB[:, 0:wcols], self.sC[:, 0:wcols], wcols, btmp14)
                for g in range(2):
                    nblk = 8 if g == 0 else 3
                    for j in range(nblk):
                        blk = g * 8 + j
                        S.trans(self.pT[:, j * 128:(j + 1) * 128],
                                self.sB[:, blk * 128:(blk + 1) * 128], self.k_id_bf[:])
                    S.copy(self.sB[:, 5632 + g * 1024:5632 + g * 1024 + nblk * 128],
                           self.pT[:, 0:nblk * 128])
                S.dma(wdv[:, ch * 11:(ch + 1) * 11, rt * 128:(rt + 1) * 128],
                      self.sB[:, 5632:5632 + 1408])
        for tt in range(16):
            t0 = tt * 128
            S.dmaT(self.sW[:, 0:2816], scr_h[:, t0:t0 + 128])
            self._quant(self.sB[:, 0:1408], self.sW[:, 0:1408], 1408, btmp14)
            self._quant(self.sB[:, 1408:2816] if False else self.sB[:, 5632:7040],
                        self.sW[:, 1408:2816], 1408, btmp14)
            S.dma(scr_hq[t0:t0 + 128, 0:1408], self.sB[:, 0:1408])
            S.dma(scr_hq[t0:t0 + 128, 1408:2816], self.sB[:, 5632:7040])
        wdl = scr_wdT.rearrange("(c p) o -> p c o", p=128)
        for kh in range(2):
            for j in range(11):
                kc = kh * 11 + j
                S.dmaT(self.sA[:, j * T:(j + 1) * T], scr_hq[:, kc * 128:(kc + 1) * 128])
            for dt in range(NDT):
                S.dma(self.sW[:, 0:1408],
                      wdl[:, kh * 11:(kh + 1) * 11, dt * 128:(dt + 1) * 128],
                      q="sync")
                for tc in range(NTC):
                    lo = tc * SL
                    for c in range(11):
                        S.mm(self.pC[:, 0:SL], self.sW[:, c * 128:(c + 1) * 128],
                             self.sA[:, c * T + lo:c * T + lo + SL],
                             start=(c == 0), stop=(c == 10))
                    xi = self.xt(dt, lo, lo + SL)
                    S.tt(xi, xi, self.pC[:, 0:SL], ALU.add)
        self.postln()

    def _ut(self, slab, lo, hi):
        if slab < 12:
            return self.sA[:, slab * T + lo:slab * T + hi]
        return self.sB[:, (slab - 12) * T + lo:(slab - 12) * T + hi]

    def l1_attn(self):
        S = self.S
        Win = self.inp("L1_attn_W_in", [D, 4096])
        Wx = self.inp("L1_attn_W_x", [2048, 96])
        Wdt = self.inp("L1_attn_W_dt", [64, 2048])
        Wout = self.inp("L1_attn_W_out", [2048, D])
        convc = self.inp("d_L1_conv_cols", [128, 64])
        convb = self.inp("d_L1_convb_cols", [128, 16])
        bdt = self.inp("d_L1_bdt_cols", [128, 16])
        Dcol = self.inp("d_L1_D_cols", [128, 16])
        alog = self.inp("d_L1_alog_cols", [128, 256])
        S.dma(self.sR[:, 0:64], convc[:])
        S.dma(self.sR[:, 64:80], convb[:])
        S.dma(self.sR[:, 80:96], bdt[:])
        S.dma(self.sR[:, 96:112], Dcol[:])
        S.dma(self.sR[:, 256:512], alog[:])
        S.act(self.sC[:, 0:256], self.sR[:, 256:512], AF.Exp)
        S.ts(self.sR[:, 256:512], self.sC[:, 0:256], -1.0, None, ALU.mult)
        scr_x = self.scratch([128, NDT * T], F32)
        scr_res = self.scratch([2048, T], BF16)
        scr_B = self.scratch([16, T], BF16)
        scr_C = self.scratch([16, T], BF16)
        scr_y2 = self.scratch([2048, T], BF16)
        S.dma(scr_x, self.x[:])
        dlt_all = self.x[:].bitcast(BF16)
        for ft in range(32):
            self.load_wchunks(self.sB[:, 8192:9216], Win, D, 128,
                              col0=ft * 128, ncols=128)
            for tc in range(NTC):
                lo = tc * SL
                for c in range(NDT):
                    S.mm(self.pA[:, lo:lo + SL],
                         self.sB[:, 8192 + c * 128:8192 + (c + 1) * 128],
                         self.xnt(c, lo, lo + SL),
                         start=(c == 0), stop=(c == NDT - 1))
            if ft < 16:
                S.memset(self.sW[:, 0:3], 0.0)
                S.copy(self.sW[:, 3:2051], self.pA[:, 0:T])
                wc = lambda i: self.sR[:, ft * 4 + i:ft * 4 + i + 1]
                S.ts(self.sC[:, 0:T], self.sW[:, 0:T], wc(0), None, ALU.mult)
                S.stt(self.sF[:, 0:2048], self.sW[:, 1:2049], wc(1),
                      self.sC[:, 0:T], ALU.mult, ALU.add)
                S.stt(self.sC[:, 0:T], self.sW[:, 2:2050], wc(2),
                      self.sF[:, 0:2048], ALU.mult, ALU.add)
                S.stt(self.sF[:, 0:2048], self.sW[:, 3:2051], wc(3),
                      self.sC[:, 0:T], ALU.mult, ALU.add)
                S.act(self._ut(ft, 0, T), self.sF[:, 0:2048], AF.Silu,
                      bias=self.sR[:, 64 + ft:65 + ft])
            else:
                S.copy(self.sB[:, 9216:11264], self.pA[:, 0:T])
                S.dma(scr_res[(ft - 16) * 128:(ft - 15) * 128, :],
                      self.sB[:, 9216:11264])
        self.load_wchunks(self.sB[:, 8192:9728], Wx, 2048, 96, col0=0, ncols=96)
        for tc in range(NTC):
            lo = tc * SL
            for c in range(16):
                S.mm(self.pB[0:96, 0:SL], self.sB[:, 8192 + c * 96:8192 + (c + 1) * 96],
                     self._ut(c, lo, lo + SL), start=(c == 0), stop=(c == 15))
            S.copy(self.sB[0:64, 11264 + lo:11264 + lo + SL], self.pB[0:64, 0:SL])
            S.copy(self.sB[64:96, 13312 + lo:13312 + lo + SL], self.pB[64:96, 0:SL])
        S.dma(scr_B, self.sB[64:80, 13312:15360])
        S.dma(scr_C, self.sB[80:96, 13312:15360])
        S.dma(self.sB[0:64, 13312:15360], Wdt, q="gpsimd")
        for slab in range(16):
            for tc in range(NTC):
                lo = tc * SL
                S.mm(self.pC[:, 0:SL],
                     self.sB[0:64, 13312 + slab * 128:13312 + (slab + 1) * 128],
                     self.sB[0:64, 11264 + lo:11264 + lo + SL],
                     start=True, stop=True)
                bc = self.sR[:, 80 + slab:81 + slab]
                S.act(self.sC[:, 0:SL], self.pC[:, 0:SL], AF.Abs, bias=bc)
                S.act(self.sC[:, SL:2 * SL], self.sC[:, 0:SL], AF.Exp, scale=-1.0)
                S.act(self.sC[:, 0:SL], self.sC[:, SL:2 * SL], AF.Ln, bias=1.0)
                S.ts(self.sF[:, 0:SL], self.pC[:, 0:SL], bc, 0.0, ALU.add, ALU.max)
                S.tt(dlt_all[:, slab * T + lo:slab * T + lo + SL],
                     self.sF[:, 0:SL], self.sC[:, 0:SL], ALU.add)
        sCbf = self.sC[:].bitcast(BF16)
        carry = self.sF[:, 0:256]
        S.memset(carry, 0.0)
        for te in range(4):
            t0 = te * SL
            S.dma(self.xn[:, 0:8192],
                  bass.AP(tensor=scr_B.tensor, offset=t0,
                          ap=[[0, 128], [T, 16], [1, SL]]))
            S.dma(self.xn[:, 8192:16384],
                  bass.AP(tensor=scr_C.tensor, offset=t0,
                          ap=[[0, 128], [T, 16], [1, SL]]))
            for slab in range(16):
                du = self.sW[:, 0:SL]
                S.tt(du, dlt_all[:, slab * T + t0:slab * T + t0 + SL],
                     self._ut(slab, t0, t0 + SL), ALU.mult)
                S.ts(self.pA[:, 0:SL], self._ut(slab, t0, t0 + SL),
                     self.sR[:, 96 + slab:97 + slab], None, ALU.mult)
                for n in range(16):
                    a = sCbf[:, 0:SL]
                    b = sCbf[:, SL:2 * SL]
                    h = sCbf[:, 2 * SL:3 * SL]
                    g = sCbf[:, 3 * SL:4 * SL]
                    S.act(a, dlt_all[:, slab * T + t0:slab * T + t0 + SL],
                          AF.Exp, scale=self.sR[:, 256 + slab * 16 + n:257 + slab * 16 + n])
                    S.tt(b, du, self.xn[:, n * SL:(n + 1) * SL], ALU.mult)
                    S.scan(h, a, b, carry[:, slab * 16 + n:slab * 16 + n + 1],
                           ALU.mult, ALU.add)
                    S.copy(carry[:, slab * 16 + n:slab * 16 + n + 1],
                           h[:, SL - 1:SL])
                    S.tt(g, h, self.xn[:, 8192 + n * SL:8192 + (n + 1) * SL],
                         ALU.mult)
                    S.tt(self.pA[:, 0:SL], self.pA[:, 0:SL], g, ALU.add)
                S.dma(sCbf[:, 0:SL],
                      scr_res[slab * 128:(slab + 1) * 128, t0:t0 + SL])
                S.act(sCbf[:, SL:2 * SL], sCbf[:, 0:SL], AF.Silu)
                S.tt(sCbf[:, 2 * SL:3 * SL], self.pA[:, 0:SL],
                     sCbf[:, SL:2 * SL], ALU.mult)
                S.dma(scr_y2[slab * 128:(slab + 1) * 128, t0:t0 + SL],
                      sCbf[:, 2 * SL:3 * SL])
        S.dma(self.x[:], scr_x)
        for slab in range(16):
            S.dma(self._ut(slab, 0, T), scr_y2[slab * 128:(slab + 1) * 128, :])
        for dt in range(NDT):
            self.load_wchunks(self.sW[:, 0:2048], Wout, 2048, 128,
                              col0=dt * 128, ncols=128)
            for tc in range(NTC):
                lo = tc * SL
                for c in range(16):
                    S.mm(self.pC[:, 0:SL], self.sW[:, c * 128:(c + 1) * 128],
                         self._ut(c, lo, lo + SL), start=(c == 0), stop=(c == 15))
                xi = self.xt(dt, lo, lo + SL)
                S.tt(xi, xi, self.pC[:, 0:SL], ALU.add)
        self.postln()



_SUBLAYER_FN = {0: "l0_attn", 1: "l0_ffn", 2: "l1_attn", 3: "l1_ffn",
                4: "l2_attn", 5: "l2_ffn", 6: "l3_attn", 7: "l3_ffn"}

_cache = {}


def _build_group(g):
    key = tuple(g)
    p = Prog(g)
    p.load_consts()
    p.prelude()
    for s in g:
        getattr(p, _SUBLAYER_FN[s])()
    p.epilogue()
    sem = p.nc.semaphore("gs").__enter__()
    p.S.build(sem)
    return p


def kernel(**inputs):
    consts = make_consts()
    derived = make_derived(inputs)
    pool = {}
    pool.update(consts)
    pool.update({k: np.asarray(v) for k, v in derived.items()})
    for k, v in inputs.items():
        pool[k] = np.asarray(v)
    x = np.asarray(inputs["L0_attn_x"], dtype=np.float32)
    xcur = [np.ascontiguousarray(x[c].T) for c in range(8)]
    for g in FUSE_GROUPS:
        p = _build_group(g)
        base = {name: pool[name] for name in p.used if name != "xin"}
        in_maps = [dict(base, xin=xcur[c]) for c in range(8)]
        res = run_bass_kernel_spmd(p.nc, in_maps, core_ids=list(range(8)))
        xcur = [np.asarray(res.results[c]["xout"]) for c in range(8)]
    return np.stack([xc.T for xc in xcur]).astype(np.float32)


# version 04
# speedup vs baseline: 1.2370x; 1.2144x over previous
import numpy as np
import ml_dtypes

import concourse.bass as bass
import concourse.mybir as mybir
from concourse.bass_utils import run_bass_kernel_spmd

F32 = mybir.dt.float32
BF16 = mybir.dt.bfloat16
AF = mybir.ActivationFunctionType
ALU = mybir.AluOpType
AX = mybir.AxisListType

T = 2048
D = 1024
NDT = D // 128
SL = 512
NTC = T // SL
MAGIC = 8388608.0

FUSE_GROUPS = [[0, 1, 2, 3, 4, 5, 6, 7]]
DEBUG_TAPS = False



def _isz(dt):
    s = str(dt)
    return 4 if "32" in s else (2 if "16" in s else 1)


_PGRAN = 32
_CGRAN = 1024


def _foot(ap):
    if not isinstance(ap, bass.AP):
        return None
    isz = _isz(ap.dtype)
    dims = ap.ap
    off = ap.offset
    sp = str(ap.space)
    if sp == "MemorySpace.DRAM":
        span = sum((n - 1) * abs(s) for s, n in dims)
        return (ap.tensor.name, 0, 1, off * isz, (off + span + 1) * isz, False)
    pitch = dims[0][0]
    npart = dims[0][1]
    if pitch > 0:
        p0 = off // pitch
        c0 = off - p0 * pitch
    else:
        p0, c0 = 0, off
    cspan = sum((n - 1) * abs(s) for s, n in dims[1:]) if len(dims) > 1 else 0
    if sp == "MemorySpace.PSUM":
        b0 = (c0 * isz) // 2048 * 2048
        b1 = -(-((c0 + cspan + 1) * isz) // 2048) * 2048
        return (ap.tensor.name, 0, 128, b0, b1, True)
    return (ap.tensor.name, p0, p0 + npart, c0 * isz, (c0 + cspan + 1) * isz, False)


class Seq:
    NLANES = 8

    @staticmethod
    def host_of(eng):
        if eng.startswith("qsync"):
            return "sync"
        if eng.startswith("qpool") or eng == "pool":
            return "gpsimd"
        return eng

    def __init__(self, nc):
        self.nc = nc
        self.steps = []
        self.n = 0
        self._lane_ctr = {"qsync": 0, "qpool": 0}

    def _add(self, eng, inc, fn, rd=(), wr=()):
        rd = [f for f in (_foot(a) for a in rd if a is not None) if f is not None]
        wr = [f for f in (_foot(a) for a in wr if a is not None) if f is not None]
        wr += [f for f in rd if f[5]]
        rd = [f for f in rd if not f[5]]
        self.steps.append((eng, inc, fn, rd, wr))
        self.n += inc

    def _lane(self, q):
        q = {"sync": "qsync", "gpsimd": "qpool"}[q]
        i = self._lane_ctr[q]
        self._lane_ctr[q] = (i + 1) % self.NLANES
        return f"{q}{i}"

    def dma(self, out, in_, q="sync"):
        self._add(self._lane(q), 16, lambda E, o=out, i=in_: E.dma_start(out=o, in_=i),
                  rd=[in_], wr=[out])

    def dmaT(self, out, in_, q="sync"):
        self._add(self._lane(q), 16, lambda E, o=out, i=in_: E.dma_start_transpose(o, i),
                  rd=[in_], wr=[out])

    def mm(self, out, lhsT, rhs, start=True, stop=True, tile_position=None):
        def f(E, o=out, l=lhsT, r=rhs, st=start, sp=stop, tp=tile_position):
            return E.matmul(o, l, r, start=st, stop=sp, tile_position=tp)
        self._add("tensor", 1, f, rd=[lhsT, rhs] + ([] if start else [out]), wr=[out])

    def trans(self, out, in_, ident):
        self._add("tensor", 1, lambda E, o=out, i=in_, d=ident: E.transpose(o, i, d),
                  rd=[in_, ident], wr=[out])

    def act(self, out, in_, func, bias=0.0, scale=1.0, accum_out=None):
        def f(E, o=out, i=in_, fu=func, b=bias, s=scale, a=accum_out):
            return E.activation(o, i, fu, bias=b, scale=s, accum_out=a)
        self._add("scalar", 1, f,
                  rd=[in_] + [b for b in (bias, scale) if isinstance(b, bass.AP)],
                  wr=[out] + ([accum_out] if accum_out is not None else []))

    def tt(self, out, a, b, op, eng="vector"):
        eng = "pool" if eng == "gpsimd" else eng
        self._add(eng, 1, lambda E, o=out, x=a, y=b, p=op: E.tensor_tensor(o, x, y, p),
                  rd=[a, b], wr=[out])

    def ts(self, out, in0, s1, s2, op0, op1=ALU.bypass, accum_out=None, eng="vector"):
        def f(E, o=out, i=in0, a=s1, b=s2, p0=op0, p1=op1, ac=accum_out):
            return E.tensor_scalar(o, i, a, b, p0, p1, accum_out=ac)
        eng = "pool" if eng == "gpsimd" else eng
        self._add(eng, 1, f,
                  rd=[in0] + [s for s in (s1, s2) if isinstance(s, bass.AP)],
                  wr=[out] + ([accum_out] if accum_out is not None else []))

    def tss(self, out, in0, s1, op, eng="vector"):
        eng = "pool" if eng == "gpsimd" else eng
        self._add(eng, 1, lambda E, o=out, i=in0, a=s1, p=op: E.tensor_single_scalar(o, i, a, p),
                  rd=[in0] + ([s1] if isinstance(s1, bass.AP) else []), wr=[out])

    def stt(self, out, in0, scalar, in1, op0, op1, eng="vector"):
        def f(E, o=out, i=in0, s=scalar, j=in1, p0=op0, p1=op1):
            return E.scalar_tensor_tensor(o, i, s, j, p0, p1)
        eng = "pool" if eng == "gpsimd" else eng
        self._add(eng, 1, f,
                  rd=[in0, in1] + ([scalar] if isinstance(scalar, bass.AP) else []),
                  wr=[out])

    def red(self, out, in_, op, axis=AX.X, absval=None, negate=None, eng="vector"):
        def f(E, o=out, i=in_, p=op, a=axis, ab=absval, ng=negate):
            return E.tensor_reduce(o, i, a, p, apply_absolute_value=ab, negate=ng)
        eng = "pool" if eng == "gpsimd" else eng
        self._add(eng, 1, f, rd=[in_], wr=[out])

    def copy(self, out, in_, eng="vector"):
        eng = "pool" if eng == "gpsimd" else eng
        self._add(eng, 1, lambda E, o=out, i=in_: E.tensor_copy(o, i),
                  rd=[in_], wr=[out])

    def acopy(self, out, in_, bias=0.0, scale=1.0):
        if isinstance(bias, (int, float)):
            self.act(out, in_, AF.Copy, bias=bias, scale=scale)
        else:
            assert scale == 1.0
            self.ts(out, in_, bias, None, ALU.add)

    def recip(self, out, in_):
        self._add("vector", 1, lambda E, o=out, i=in_: E.reciprocal(o, i),
                  rd=[in_], wr=[out])

    def scan(self, out, d0, d1, initial, op0, op1, eng="vector"):
        def f(E, o=out, a=d0, b=d1, ini=initial, p0=op0, p1=op1):
            return E.tensor_tensor_scan(o, a, b, ini, p0, p1)
        eng = "pool" if eng == "gpsimd" else eng
        self._add(eng, 1, f,
                  rd=[d0, d1] + ([initial] if isinstance(initial, bass.AP) else []),
                  wr=[out])

    def memset(self, ap, val, eng="vector"):
        eng = "pool" if eng == "gpsimd" else eng
        self._add(eng, 1, lambda E, a=ap, v=val: E.memset(a, v), rd=[], wr=[ap])

    def affine_sel(self, out, in_, pattern, cmp, fill, base=0, cm=1):
        def f(E, o=out, i=in_, p=pattern, c=cmp, fl=fill, b=base, m=cm):
            return E.affine_select(o, i, p, c, fl, base=b, channel_multiplier=m)
        self._add("pool", 1, f, rd=[in_], wr=[out])

    def _schedule(self):
        nsteps = len(self.steps)
        self.vengs = sorted({s[0] for s in self.steps})
        cum = {e: 0 for e in self.vengs}
        cumafter = [0] * nsteps
        veng_of = [s[0] for s in self.steps]
        tbl = {}
        deps_of = []
        last_of_veng = {}
        for i, (eng, inc, fn, rd, wr) in enumerate(self.steps):
            cum[eng] += inc
            cumafter[i] = cum[eng]
            deps = set()
            if eng.startswith("q"):
                if eng in last_of_veng:
                    deps.add(last_of_veng[eng])
            last_of_veng[eng] = i

            def buckets(fp):
                tk, p0, p1, b0, b1 = fp[:5]
                for pb in range(p0 // _PGRAN, (p1 + _PGRAN - 1) // _PGRAN):
                    for cb in range(b0 // _CGRAN, (b1 + _CGRAN - 1) // _CGRAN):
                        yield (tk, pb, cb)

            for fp in rd:
                for bk in buckets(fp):
                    e = tbl.get(bk)
                    if e is None:
                        tbl[bk] = e = {"w": None, "rs": []}
                    if e["w"] is not None:
                        deps.add(e["w"])
                    e["rs"].append(i)
            for fp in wr:
                for bk in buckets(fp):
                    e = tbl.get(bk)
                    if e is None:
                        tbl[bk] = e = {"w": None, "rs": []}
                    if e["w"] is not None:
                        deps.add(e["w"])
                    deps.update(e["rs"])
                    e["w"] = i
                    e["rs"] = []
            deps.discard(i)
            deps_of.append(deps)
        known_after = [None] * nsteps
        hosts = {self.host_of(e) for e in self.vengs}
        last_known = {h: {} for h in hosts}
        waits_of = []
        for i, (eng, inc, fn, rd, wr) in enumerate(self.steps):
            host = self.host_of(eng)
            kn = dict(last_known[host])
            waits = {}
            for p in sorted(deps_of[i]):
                pe = veng_of[p]
                v = cumafter[p]
                if kn.get(pe, 0) >= v:
                    continue
                if waits.get(pe, 0) >= v:
                    continue
                waits[pe] = v
            for pe, v in waits.items():
                if kn.get(pe, 0) < v:
                    kn[pe] = v
            for p in sorted(deps_of[i]):
                pe = veng_of[p]
                if kn.get(pe, 0) >= cumafter[p] and known_after[p]:
                    for e2, v2 in known_after[p].items():
                        if kn.get(e2, 0) < v2:
                            kn[e2] = v2
            waits_of.append(waits)
            known_after[i] = kn
            last_known[host] = dict(kn)
        return waits_of

    @staticmethod
    def _ov(a, b):
        return (a[0] == b[0] and a[1] < b[2] and b[1] < a[2]
                and a[3] < b[4] and b[3] < a[4])

    def build(self, sem_unused=None):
        nc = self.nc
        waits_of = self._schedule()
        sems = {e: nc.semaphore("sem_" + e).__enter__() for e in self.vengs}
        finals = {e: 0 for e in self.vengs}
        for (eng, inc, fn, rd, wr) in self.steps:
            finals[eng] += inc
        per_host = {h: [] for h in ("sync", "scalar", "vector", "tensor", "gpsimd")}
        for i, (eng, inc, fn, rd, wr) in enumerate(self.steps):
            per_host[self.host_of(eng)].append((i, eng, inc, fn, waits_of[i]))
        with nc.Block() as block:
            for host in ("sync", "scalar", "vector", "tensor", "gpsimd"):
                def section(E, host=host):
                    for (i, eng, inc, fn, waits) in per_host[host]:
                        for pe, v in sorted(waits.items()):
                            E.wait_ge(sems[pe], v)
                        fn(E).then_inc(sems[eng], inc)
                    for e2, v in finals.items():
                        if v > 0:
                            E.wait_ge(sems[e2], v)
                getattr(block, host)(section)



def _rope_tables(base):
    inv = 1.0 / (base ** (np.arange(0, 32, dtype=np.float64) / 32.0))
    t = np.arange(T, dtype=np.float64)
    ang = t[None, :] * inv[:, None]
    cos = np.concatenate([np.cos(ang)] * 2, 0)
    sin = np.concatenate([np.sin(ang)] * 2, 0)
    s = 8.0 ** -0.5
    cosf = (np.concatenate([cos, cos], 0) * s).astype(ml_dtypes.bfloat16)
    sinf = (np.concatenate([sin, sin], 0) * s).astype(ml_dtypes.bfloat16)
    return cosf, sinf


def _cols(a, n=None):
    a = np.asarray(a, dtype=np.float32).reshape(-1)
    n = n or (a.size // 128)
    return np.ascontiguousarray(a.reshape(n, 128).T)


def make_consts():
    c = {}
    c["c_id_bf"] = np.eye(128, dtype=ml_dtypes.bfloat16)
    c["c_id_f32"] = np.eye(128, dtype=np.float32)
    c["c_cols_bf"] = np.concatenate(
        [np.ones((128, 1), np.float32), np.full((128, 1), 1.0 / 1024.0, np.float32)],
        axis=1).astype(ml_dtypes.bfloat16)
    c["c_ones_row_f32"] = np.ones((1, 128), dtype=np.float32)
    c["c_misc_cols"] = np.ascontiguousarray(np.broadcast_to(
        np.array([1e-5, 1e-6, 1.0, 0.0], np.float32)[None, :], (128, 4)))
    c["c_ones_row_bf"] = np.ones((1, 128), dtype=ml_dtypes.bfloat16)
    c["c_cos_l0"], c["c_sin_l0"] = _rope_tables(10000.0)
    c["c_cos_l3"], c["c_sin_l3"] = _rope_tables(1000000.0)
    p = np.arange(128)[:, None]
    qt = np.arange(16)[None, :]
    c["c_rowrecip"] = (1.0 / (qt * 128 + p + 1)).astype(np.float32)
    m = np.arange(640)[None, :]
    c["c_e3shift_bf"] = np.where(p - m + 511 >= 0, 0.0, -2.0e30).astype(ml_dtypes.bfloat16)
    return c


def make_derived(inp):
    d = {}
    tr = lambda a: np.ascontiguousarray(np.asarray(a, dtype=np.float32).T)
    asf = lambda a: np.ascontiguousarray(np.asarray(a, dtype=np.float32))
    for w in ("Wq", "Wk", "Wv", "Wo"):
        d[f"d_L0_{w}T"] = tr(inp[f"L0_attn_{w}"])
    biases = np.asarray(inp["L0_attn_biases"], dtype=np.float32)
    W = 1536
    pp = np.arange(128)[:, None]
    mm_ = np.arange(W)[None, :]
    v = pp - mm_ + 1023
    es = biases[:, np.clip(v, 0, 512)]
    es = np.where(v[None] < 0, -2.0e30, np.where(v[None] > 512, 0.0, es))
    d["d_eshift_l0"] = np.ascontiguousarray(es.astype(np.float32))
    d["d_tau_bc"] = np.ascontiguousarray(
        np.broadcast_to(np.asarray(inp["L0_attn_tau"], np.float32)[None, :], (128, 16)))
    d["d_L0f_wr_cols"] = _cols(inp["L0_ffn_w_router"])
    d["d_L0f_b1_cols"] = _cols(inp["L0_ffn_b1"])
    d["d_L0f_b2_cols"] = _cols(inp["L0_ffn_b2"])
    d["d_L1_conv_cols"] = np.ascontiguousarray(
        np.asarray(inp["L1_attn_conv_w"], np.float32)[:, 0, :]
        .reshape(16, 128, 4).transpose(1, 0, 2).reshape(128, 64))
    d["d_L1_convb_cols"] = _cols(inp["L1_attn_conv_b"])
    d["d_L1_bdt_cols"] = _cols(inp["L1_attn_b_dt"])
    d["d_L1_D_cols"] = _cols(inp["L1_attn_D"])
    d["d_L1_alog_cols"] = np.ascontiguousarray(
        np.asarray(inp["L1_attn_A_log"], np.float32)
        .reshape(16, 128, 16).transpose(1, 0, 2).reshape(128, 256))
    upw = np.asarray(inp["L1_ffn_up_weights"], dtype=np.float32)
    dnw = np.asarray(inp["L1_ffn_down_weights"], dtype=np.float32)
    d["d_bf_up1"] = _bf_factors_g1(upw)
    d["d_bf_up2"] = _bf_factors_g2(upw)
    d["d_bf_dn1"] = _bf_factors_g1(dnw)
    d["d_bf_dn2"] = _bf_factors_g2(dnw)
    fp = np.arange(4096)
    f_of_fp = (fp % 64) * 64 + (fp // 64)
    d["d_bexp_cols"] = _cols(inp["L1_ffn_b_exp"])
    d["d_upbiasP_cols"] = _cols(np.asarray(inp["L1_ffn_up_bias"], np.float32)[f_of_fp])
    d["d_W_conP"] = np.ascontiguousarray(
        np.asarray(inp["L1_ffn_W_con"], np.float32)[f_of_fp, :])
    d["d_bcon_cols"] = _cols(inp["L1_ffn_b_con"])
    d["d_dbias_cols"] = _cols(inp["L1_ffn_down_bias"])
    Win = np.asarray(inp["L2_attn_W_in"], dtype=np.float32)
    d["d_L2_WinT_half"] = tr(Win[D:2 * D, :])
    d["d_L2_binh_cols"] = _cols(np.asarray(inp["L2_attn_b_in"], np.float32)[D:2 * D])
    d["d_L2_WgateT"] = tr(inp["L2_attn_W_gate"])
    d["d_L2_bg_cols"] = _cols(inp["L2_attn_b_gate"])
    d["d_L2_A"] = asf(inp["L2_attn_A"])
    d["d_L2_AT"] = tr(inp["L2_attn_A"])
    d["d_L2_BmT"] = tr(inp["L2_attn_Bm"])
    d["d_L2_CmT"] = asf(inp["L2_attn_Cm"]).T.copy()
    d["d_L2_WoutT"] = tr(inp["L2_attn_W_out"])
    d["d_L2_bo_cols"] = _cols(inp["L2_attn_b_out"])
    d["d_L3_ns_cols"] = _cols(inp["L3_attn_norm_scale"])
    d["d_L3f_b1_cols"] = np.ascontiguousarray(
        np.asarray(inp["L3_ffn_b1"], np.float32).reshape(8, 4, 128).transpose(0, 2, 1))
    return d


def _dev_layout(out):
    dev = np.empty((128, 6 * 32 * 64), dtype=np.float32)
    for s in range(6):
        for k in range(64):
            kb, kh = k % 2, k // 2
            dev[kb * 64:(kb + 1) * 64, s * 2048 + kh * 64:s * 2048 + (kh + 1) * 64] = out[s, k]
    return np.ascontiguousarray(dev)


def _bf_factors_g1(wts):
    out = np.empty((6, 64, 64, 64), dtype=np.float32)
    i = np.arange(64)[None, :]
    j = np.arange(64)[:, None]
    for s in range(6):
        sig = 1 << s
        w = np.asarray(wts[s]).reshape(2048 // sig, sig, 2, 2)
        for k in range(64):
            nb = k * (1 << (5 - s)) + (j >> (s + 1))
            low = np.broadcast_to(i & (sig - 1), (64, 64))
            r = np.broadcast_to((i >> s) & 1, (64, 64))
            c = np.broadcast_to((j >> s) & 1, (64, 64))
            nbb = np.broadcast_to(nb, (64, 64))
            out[s, k] = w[nbb, low, r, c]
    return _dev_layout(out)


def _bf_factors_g2(wts):
    out = np.empty((6, 64, 64, 64), dtype=np.float32)
    i = np.arange(64)[None, :]
    j = np.arange(64)[:, None]
    for sp in range(6):
        s = sp + 6
        sig = 1 << s
        w = np.asarray(wts[s]).reshape(4096 // (2 * sig), sig, 2, 2)
        for kp in range(64):
            nb = np.broadcast_to(j >> (sp + 1), (64, 64))
            low = np.broadcast_to(kp + 64 * (i & ((1 << sp) - 1)), (64, 64))
            r = np.broadcast_to((i >> sp) & 1, (64, 64))
            c = np.broadcast_to((j >> sp) & 1, (64, 64))
            out[sp, kp] = w[nb, low, r, c]
    return _dev_layout(out)



class Prog:
    def __init__(self, sublayers):
        self.nc = bass.Bass("TRN2", target_bir_lowering=False,
                            dynamic_dma_scratch_size=16384)
        self.S = Seq(self.nc)
        self.sub = sublayers
        self.dram = {}
        self.used = []
        self._scr = 0
        self._alloc()

    def inp(self, name, shape, dtype=F32):
        if name not in self.dram:
            self.dram[name] = self.nc.dram_tensor(name, list(shape), dtype,
                                                  kind="ExternalInput").ap()
            self.used.append(name)
        return self.dram[name]

    def tap(self, name, ap):
        if not DEBUG_TAPS:
            return
        sh = list(ap.shape)
        out = self.nc.dram_tensor("tap_" + name, sh, ap.dtype, kind="ExternalOutput").ap()
        self.S.dma(out, ap)

    def scratch(self, shape, dtype):
        self._scr += 1
        return self.nc.dram_tensor(f"scr{self._scr}", list(shape), dtype,
                                   kind="Internal").ap()

    def _alloc(self):
        nc = self.nc
        a = lambda nm, sh, dt: nc.alloc_sbuf_tensor(nm, sh, dt).ap()
        self.x = a("x", [128, NDT * T], F32)
        self.xn = a("xn", [128, NDT * T], BF16)
        self.sA = a("sA", [128, 24576], BF16)
        self.sB = a("sB", [128, 16384], BF16)
        self.sF = a("sF", [128, 3072], F32)
        self.sW = a("sW", [128, 3072], BF16)
        self.sC = a("sC", [128, 2048], F32)
        self.sR = a("sR", [128, 512], F32)
        self.k_id_bf = a("k_id_bf", [128, 128], BF16)
        self.k_id_f32 = a("k_id_f32", [128, 128], F32)
        self.k_cols = a("k_cols", [128, 2], BF16)
        self.k_row_f32 = a("k_row_f32", [1, 128], F32)
        self.k_misc = a("k_misc", [128, 4], F32)
        self.k_row_bf = a("k_row_bf", [1, 128], BF16)
        self.pA = nc.alloc_psum_tensor("pA", [128, 2048], F32).ap()
        self.pB = nc.alloc_psum_tensor("pB", [128, 1024], F32).ap()
        self.pC = nc.alloc_psum_tensor("pC", [128, 512], F32).ap()
        self.pT = nc.alloc_psum_tensor("pT", [128, 1024], BF16).ap()

    def load_consts(self):
        S = self.S
        S.dma(self.k_id_bf[:], self.inp("c_id_bf", [128, 128], BF16)[:])
        S.dma(self.k_id_f32[:], self.inp("c_id_f32", [128, 128], F32)[:])
        S.dma(self.k_cols[:], self.inp("c_cols_bf", [128, 2], BF16)[:])
        S.dma(self.k_row_f32[:], self.inp("c_ones_row_f32", [1, 128], F32)[:])
        S.dma(self.k_misc[:], self.inp("c_misc_cols", [128, 4], F32)[:])
        S.dma(self.k_row_bf[:], self.inp("c_ones_row_bf", [1, 128], BF16)[:])

    def xt(self, i, lo=0, hi=T):
        return self.x[:, i * T + lo:i * T + hi]

    def xnt(self, i, lo=0, hi=T):
        return self.xn[:, i * T + lo:i * T + hi]

    def load_wchunks(self, dst, dram2d, rows, cols, col0=0, ncols=None):
        w = ncols if ncols is not None else cols
        nch = rows // 128
        src = dram2d.rearrange("(c p) o -> p c o", p=128)[:, :, col0:col0 + w]
        self.S.dma(dst[:, 0:nch * w], src, q="gpsimd")

    def prelude(self, skip_xn=False):
        S = self.S
        xin = self.inp("xin", [D, T], F32)
        S.dma(self.x[:], xin.rearrange("(i p) t -> p i t", p=128))
        if not skip_xn:
            for i in range(NDT):
                S.copy(self.xnt(i), self.xt(i))

    def epilogue(self):
        S = self.S
        xout = self.nc.dram_tensor("xout", [D, T], F32, kind="ExternalOutput").ap()
        S.dma(xout.rearrange("(i p) t -> p i t", p=128), self.x[:])

    def bcast_row(self, psum_dst, row_ap):
        self.S.mm(psum_dst, self.k_row_f32[:], row_ap, start=True, stop=True)

    def postln(self, eps=1e-5):
        S = self.S
        inv = self.k_cols[:, 1:2]
        for i in range(NDT):
            S.copy(self.sB[:, i * T:(i + 1) * T], self.xt(i))
        for tc in range(NTC):
            lo = tc * SL
            for i in range(NDT):
                S.mm(self.pA[0:1, lo:lo + SL], inv,
                     self.sB[:, i * T + lo:i * T + lo + SL],
                     start=(i == 0), stop=(i == NDT - 1))
        mean_row = self.sF[0:1, 0:T]
        S.acopy(mean_row, self.pA[0:1, 0:T])
        for i in range(NDT):
            S.act(self.sB[:, i * T:(i + 1) * T], self.xt(i), AF.Square)
        for tc in range(NTC):
            lo = tc * SL
            for i in range(NDT):
                S.mm(self.pA[0:1, lo:lo + SL], inv,
                     self.sB[:, i * T + lo:i * T + lo + SL],
                     start=(i == 0), stop=(i == NDT - 1))
        for tc in range(NTC):
            lo = tc * SL
            t1 = self.sF[0:1, T + 0:T + SL]
            t2 = self.sF[0:1, T + SL:T + 2 * SL]
            mch = self.sF[0:1, lo:lo + SL]
            S.act(t1, mch, AF.Square)
            S.tt(t1, self.pA[0:1, lo:lo + SL], t1, ALU.subtract)
            S.act(t2, t1, AF.Ln, bias=self.k_misc[0:1, 0:1])
            S.act(t1, t2, AF.Exp, scale=-0.5)
            S.tt(t2, mch, t1, ALU.mult)
            self.bcast_row(self.pB[:, 0:SL], t1)
            self.bcast_row(self.pB[:, SL:2 * SL], t2)
            for i in range(NDT):
                xi = self.xt(i, lo, lo + SL)
                S.tt(xi, xi, self.pB[:, 0:SL], ALU.mult)
                S.tt(xi, xi, self.pB[:, SL:2 * SL], ALU.subtract)
                S.copy(self.xnt(i, lo, lo + SL), xi)

    def l2_attn(self):
        S = self.S
        winT = self.inp("d_L2_WinT_half", [D, D])
        wgT = self.inp("d_L2_WgateT", [D, D])
        binh = self.inp("d_L2_binh_cols", [128, 8])
        bg = self.inp("d_L2_bg_cols", [128, 8])
        S.dma(self.sR[:, 0:8], binh[:])
        S.dma(self.sR[:, 8:16], bg[:])
        for i in range(NDT):
            self.load_wchunks(self.sW[:, 0:1024], winT, D, 128, col0=i * 128, ncols=128)
            self.load_wchunks(self.sW[:, 1024:2048], wgT, D, 128, col0=i * 128, ncols=128)
            for tc in range(NTC):
                lo = tc * SL
                for c in range(NDT):
                    S.mm(self.pA[:, lo:lo + SL], self.sW[:, c * 128:(c + 1) * 128],
                         self.xnt(c, lo, lo + SL), start=(c == 0), stop=(c == NDT - 1))
                for c in range(NDT):
                    S.mm(self.pB[:, 0:SL], self.sW[:, 1024 + c * 128:1024 + (c + 1) * 128],
                         self.xnt(c, lo, lo + SL), start=(c == 0), stop=(c == NDT - 1))
                S.acopy(self.sB[:, 0:SL], self.pA[:, lo:lo + SL], bias=self.sR[:, i:i + 1])
                S.act(self.sB[:, SL:2 * SL], self.pB[:, 0:SL], AF.Sigmoid,
                      bias=self.sR[:, 8 + i:9 + i])
                S.tt(self.sA[:, i * T + lo:i * T + lo + SL],
                     self.sB[:, 0:SL], self.sB[:, SL:2 * SL], ALU.mult)
        self.tap("u", self.sA[:, 0:NDT * T])
        self.tap("x_a", self.x[:])
        bmT = self.inp("d_L2_BmT", [D, 16])
        self.load_wchunks(self.sW[:, 0:128], bmT, D, 16, col0=0, ncols=16)
        for tc in range(NTC):
            lo = tc * SL
            for c in range(NDT):
                S.mm(self.pA[0:16, lo:lo + SL], self.sW[:, c * 16:(c + 1) * 16],
                     self.sA[:, c * T + lo:c * T + lo + SL],
                     start=(c == 0), stop=(c == NDT - 1))
        H = self.sB[0:16, 0:T]
        S.copy(H, self.pA[0:16, 0:T])
        self.tap("uB", H)
        self.tap("x_b", self.x[:])
        S.dma(self.sW[0:16, 2048:2064], self.inp("d_L2_A", [16, 16])[:], q="gpsimd")
        S.dma(self.sW[0:16, 2064:2080], self.inp("d_L2_AT", [16, 16])[:], q="gpsimd")
        Msb = self.sW[0:16, 2048:2064]
        Tsb = self.sW[0:16, 2064:2080]
        for k in range(11):
            sh = 1 << k
            for tc in range(NTC):
                lo = tc * SL
                r0 = max(lo, sh)
                if r0 >= (tc + 1) * SL:
                    continue
                n = (tc + 1) * SL - r0
                S.mm(self.pA[0:16, r0:r0 + n], Tsb, self.sB[0:16, r0 - sh:r0 - sh + n],
                     start=True, stop=True)
            S.tt(self.sB[0:16, sh:T], self.sB[0:16, sh:T], self.pA[0:16, sh:T], ALU.add)
            if k < 10:
                S.mm(self.pC[0:16, 0:16], Tsb, Msb, start=True, stop=True)
                S.copy(Msb, self.pC[0:16, 0:16])
                S.trans(self.pT[0:16, 0:16], Msb, self.k_id_bf[0:16, 0:16])
                S.copy(Tsb, self.pT[0:16, 0:16])
        self.tap("H", H)
        self.tap("x_c", self.x[:])
        cmT = self.inp("d_L2_CmT", [16, D])
        S.dma(self.sW[0:16, 0:1024], cmT[:], q="gpsimd")
        for i in range(NDT):
            for tc in range(NTC):
                lo = tc * SL
                S.mm(self.pA[:, lo:lo + SL], self.sW[0:16, i * 128:(i + 1) * 128],
                     self.sB[0:16, lo:lo + SL], start=True, stop=True)
            S.copy(self.sA[:, i * T:(i + 1) * T], self.pA[:, 0:T])
        self.tap("y", self.sA[:, 0:NDT * T])
        self.tap("xpre", self.x[:])
        woT = self.inp("d_L2_WoutT", [D, D])
        bo = self.inp("d_L2_bo_cols", [128, 8])
        S.dma(self.sR[:, 16:24], bo[:])
        for i in range(NDT):
            self.load_wchunks(self.sW[:, 0:1024], woT, D, 128, col0=i * 128, ncols=128)
            for tc in range(NTC):
                lo = tc * SL
                for c in range(NDT):
                    S.mm(self.pA[:, lo:lo + SL], self.sW[:, c * 128:(c + 1) * 128],
                         self.sA[:, c * T + lo:c * T + lo + SL],
                         start=(c == 0), stop=(c == NDT - 1))
                xi = self.xt(i, lo, lo + SL)
                S.ts(self.sC[:, 0:SL], self.pA[:, lo:lo + SL],
                     self.sR[:, 16 + i:17 + i], None, ALU.add)
                S.tt(xi, xi, self.sC[:, 0:SL], ALU.add)
        self.tap("xres", self.x[:])
        self.postln()
        self.tap("xfin", self.x[:])


    def _rmsnorm_l3(self):
        S = self.S
        ns = self.inp("d_L3_ns_cols", [128, 8])
        S.dma(self.sR[:, 48:56], ns[:])
        inv = self.k_cols[:, 1:2]
        for i in range(NDT):
            S.act(self.sB[:, i * T:(i + 1) * T], self.xt(i), AF.Square)
        for tc in range(NTC):
            lo = tc * SL
            for i in range(NDT):
                S.mm(self.pA[0:1, lo:lo + SL], inv,
                     self.sB[:, i * T + lo:i * T + lo + SL],
                     start=(i == 0), stop=(i == NDT - 1))
        for tc in range(NTC):
            lo = tc * SL
            t1 = self.sF[0:1, 0:SL]
            S.act(t1, self.pA[0:1, lo:lo + SL], AF.Ln, bias=self.k_misc[0:1, 1:2])
            S.act(t1, t1, AF.Exp, scale=-0.5)
            self.bcast_row(self.pB[:, 0:SL], t1)
            for i in range(NDT):
                S.tt(self.sC[:, 0:SL], self.xt(i, lo, lo + SL),
                     self.pB[:, 0:SL], ALU.mult)
                S.ts(self.xnt(i, lo, lo + SL), self.sC[:, 0:SL],
                     self.sR[:, 48 + i:49 + i], None, ALU.mult)

    def _attention(self, layer):
        S = self.S
        l0 = (layer == 0)
        cosd = self.inp("c_cos_l0" if l0 else "c_cos_l3", [128, T], BF16)
        sind = self.inp("c_sin_l0" if l0 else "c_sin_l3", [128, T], BF16)
        S.dma(self.sB[:, 4096:6144], cosd[:])
        S.dma(self.sB[:, 6144:8192], sind[:])
        cos = self.sB[:, 4096:6144]
        sin = self.sB[:, 6144:8192]
        EOFF = 8192
        if l0:
            esh = self.inp("d_eshift_l0", [16, 128, 1536])
            tau = self.inp("d_tau_bc", [128, 16])
            rr = self.inp("c_rowrecip", [128, 16])
            S.dma(self.sR[:, 24:40], tau[:])
            S.dma(self.sR[:, 40:56], rr[:])
        else:
            e3 = self.inp("c_e3shift_bf", [128, 640], BF16)
            S.dma(self.sB[:, EOFF:EOFF + 640], e3[:])
        if l0:
            wq = self.inp("d_L0_WqT", [D, D])
            wk = self.inp("d_L0_WkT", [D, D])
            wv = self.inp("d_L0_WvT", [D, D])
            wo = self.inp("d_L0_WoT", [D, D])
        else:
            wq = self.inp("L3_attn_Wq", [D, D])
            wk = self.inp("L3_attn_Wk", [D, D])
            wv = self.inp("L3_attn_Wv", [D, D])
            wo = self.inp("L3_attn_Wo", [D, D])
        QOF, KOF, VOF = 16384, 18432, 20480
        for hp in range(8):
            if l0:
                for hh in range(2):
                    S.dma(self.sB[:, EOFF + hh * 1536:EOFF + (hh + 1) * 1536],
                          esh[2 * hp + hh], q="gpsimd")
            for (wap, dst) in ((wq, QOF), (wk, KOF)):
                self.load_wchunks(self.sW[:, 0:1024], wap, D, 128,
                                  col0=hp * 128, ncols=128)
                ov = self.sW[:, 1024:2048].rearrange("p (k h f c) -> p k h f c", k=8, h=2, f=2)
                iv = self.sW[:, 0:1024].rearrange("p (k h f c) -> p k h f c", k=8, h=2, f=2)
                S.act(ov[:, :, :, 0, :], iv[:, :, :, 1, :], AF.Copy, scale=-1.0)
                S.copy(ov[:, :, :, 1, :], iv[:, :, :, 0, :])
                for tc in range(NTC):
                    lo = tc * SL
                    for c in range(NDT):
                        S.mm(self.pA[:, lo:lo + SL],
                             self.sW[:, c * 128:(c + 1) * 128],
                             self.xnt(c, lo, lo + SL),
                             start=(c == 0), stop=(c == NDT - 1))
                    for c in range(NDT):
                        S.mm(self.pB[:, 0:SL],
                             self.sW[:, 1024 + c * 128:1024 + (c + 1) * 128],
                             self.xnt(c, lo, lo + SL),
                             start=(c == 0), stop=(c == NDT - 1))
                    S.tt(self.sC[:, 0:SL], self.pA[:, lo:lo + SL],
                         cos[:, lo:lo + SL], ALU.mult)
                    S.tt(self.sC[:, SL:2 * SL], self.pB[:, 0:SL],
                         sin[:, lo:lo + SL], ALU.mult)
                    S.tt(self.sA[:, dst + lo:dst + lo + SL],
                         self.sC[:, 0:SL], self.sC[:, SL:2 * SL], ALU.add)
            self.load_wchunks(self.sW[:, 0:1024], wv, D, 128,
                              col0=hp * 128, ncols=128)
            for tt in range(16):
                t0 = tt * 128
                for c in range(NDT):
                    S.mm(self.pC[:, 0:128], self.xnt(c, t0, t0 + 128),
                         self.sW[:, c * 128:(c + 1) * 128],
                         start=(c == 0), stop=(c == NDT - 1))
                S.copy(self.sA[:, VOF + t0:VOF + t0 + 128], self.pC[:, 0:128])
            if hp == 0:
                self.tap("q0", self.sA[:, QOF:QOF + T])
                self.tap("k0", self.sA[:, KOF:KOF + T])
                self.tap("v0", self.sA[:, VOF:VOF + T])
            for hh in range(2):
                hb = hh * 64
                hglob = 2 * hp + hh
                for qt in range(16):
                    q0 = qt * 128
                    KF = q0 + 128
                    nch = (KF + SL - 1) // SL
                    for cc in range(nch):
                        k0 = cc * SL
                        N = min(SL, KF - k0)
                        pre = (q0 - (k0 + N - 1) <= 512) if l0 else (k0 + N == KF)
                        if pre:
                            if l0:
                                off = 1023 + k0 - q0 + hh * 1536
                            else:
                                off = 511 + k0 - q0
                            S.mm(self.pA[:, k0:k0 + N], self.k_id_bf[:],
                                 self.sB[:, EOFF + off:EOFF + off + N],
                                 start=True, stop=False)
                        S.mm(self.pA[:, k0:k0 + N],
                             self.sA[hb:hb + 64, QOF + q0:QOF + q0 + 128],
                             self.sA[hb:hb + 64, KOF + k0:KOF + k0 + N],
                             start=not pre, stop=True)
                    den = self.sR[:, 56:57]
                    rd = self.sR[:, 57:58]
                    if hp == 0 and hh == 0 and qt == 5 and DEBUG_TAPS:
                        S.copy(self.sC[:, 0:KF], self.pA[:, 0:KF])
                        self.tap("sc5", self.sC[:, 0:KF])
                    S.act(self.sB[:, 0:KF], self.pA[:, 0:KF], AF.Exp,
                          accum_out=den)
                    if hp == 0 and hh == 0 and qt == 5:
                        self.tap("exp5", self.sB[:, 0:KF])
                        self.tap("den5", den)
                    S.recip(rd, den)
                    if l0:
                        c2 = self.sR[:, 58:59]
                        c2d = self.sR[:, 59:60]
                        S.tt(c2, self.sR[:, 24 + hglob:25 + hglob],
                             self.sR[:, 40 + qt:41 + qt], ALU.mult)
                        S.tt(c2d, c2, den, ALU.mult)
                        S.ts(self.sB[:, 0:KF], self.sB[:, 0:KF], c2d, 0.0,
                             ALU.add, ALU.max)
                    S.ts(self.sB[:, 0:KF], self.sB[:, 0:KF], rd, None, ALU.mult)
                    nkc = KF // 128
                    for g in range((nkc + 7) // 8):
                        b0 = g * 8
                        nb = min(8, nkc - b0)
                        for j in range(nb):
                            S.trans(self.pT[:, j * 128:(j + 1) * 128],
                                    self.sB[:, (b0 + j) * 128:(b0 + j + 1) * 128],
                                    self.k_id_bf[:])
                        S.copy(self.sB[:, 2048 + b0 * 128:2048 + (b0 + nb) * 128],
                               self.pT[:, 0:nb * 128])
                    for kc in range(nkc):
                        S.mm(self.pC[hb:hb + 64, 0:128],
                             self.sA[:, VOF + kc * 128 + hb:VOF + kc * 128 + hb + 64],
                             self.sB[:, 2048 + kc * 128:2048 + (kc + 1) * 128],
                             start=(kc == 0), stop=(kc == nkc - 1))
                    S.copy(self.sA[hb:hb + 64, hp * T + q0:hp * T + q0 + 128],
                           self.pC[hb:hb + 64, 0:128])
                    if hp == 0 and hh == 0 and qt == 5:
                        self.tap("eT5", self.sB[:, 2048:2048 + KF])
                        self.tap("av5", self.sA[0:64, q0:q0 + 128])
        for i in range(NDT):
            self.load_wchunks(self.sW[:, 0:1024], wo, D, 128,
                              col0=i * 128, ncols=128)
            for tc in range(NTC):
                lo = tc * SL
                for c in range(NDT):
                    S.mm(self.pA[:, lo:lo + SL], self.sW[:, c * 128:(c + 1) * 128],
                         self.sA[:, c * T + lo:c * T + lo + SL],
                         start=(c == 0), stop=(c == NDT - 1))
                xi = self.xt(i, lo, lo + SL)
                S.tt(xi, xi, self.pA[:, lo:lo + SL], ALU.add)
        self.postln()

    def l0_attn(self):
        self._attention(0)

    def l3_attn(self):
        self._rmsnorm_l3()
        self._attention(3)

    def l0_ffn(self):
        S = self.S
        wr = self.inp("d_L0f_wr_cols", [128, 8])
        b1c = self.inp("d_L0f_b1_cols", [128, 32])
        b2c = self.inp("d_L0f_b2_cols", [128, 8])
        W1 = self.inp("L0_ffn_W1", [D, 4096])
        W2 = self.inp("L0_ffn_W2", [4096, D])
        S.dma(self.sR[:, 96:104], wr[:])
        S.dma(self.sR[:, 184:216], b1c[:])
        S.dma(self.sR[:, 216:224], b2c[:])
        S.act(self.sR[:, 224:232], self.sR[:, 216:224], AF.Copy, scale=2.0)
        for tc in range(NTC):
            lo = tc * SL
            for c in range(NDT):
                S.mm(self.pA[0:1, lo:lo + SL], self.sR[:, 96 + c:97 + c],
                     self.xt(c, lo, lo + SL), start=(c == 0), stop=(c == NDT - 1))
        srow = self.sF[0:1, 0:T]
        S.acopy(srow, self.pA[0:1, 0:T])
        scr_s = self.scratch([T], F32)
        S.dma(scr_s, srow)
        for j in range(16):
            S.trans(self.pB[:, j:j + 1], srow[0:1, j * 128:(j + 1) * 128],
                    self.k_id_f32[0:1, 0:1])
        S.copy(self.sR[:, 104:120], self.pB[:, 0:16])
        sbrd = self.sC[:, 0:T]
        S.dma(sbrd, bass.AP(tensor=scr_s.tensor, offset=0, ap=[[0, 128], [1, T]]))
        self.tap("sbrd", sbrd)
        for j in range(16):
            S.ts(self.sB[:, 0:T], sbrd, self.sR[:, 104 + j:105 + j], 0.0,
                 ALU.is_gt, ALU.add, accum_out=self.sR[:, 120 + j:121 + j])
            if j == 0:
                self.tap("cmp0", self.sB[:, 0:T])
        S.tss(self.sR[:, 136:152], self.sR[:, 120:136], 1024.0, ALU.is_lt)
        S.act(self.sR[:, 152:168], self.sR[:, 104:120], AF.Sigmoid)
        S.tt(self.sR[:, 168:184], self.sR[:, 152:168], self.sR[:, 136:152],
             ALU.mult)
        S.act(self.sR[:, 168:184], self.sR[:, 168:184], AF.Copy, scale=0.5)
        scr_w = self.scratch([T], F32)
        S.trans(self.pB[0:16, 0:128], self.sR[:, 168:184], self.k_id_f32[:])
        S.copy(self.sF[0:16, 0:128], self.pB[0:16, 0:128])
        S.dma(scr_w.rearrange("(j p) -> j p", p=128), self.sF[0:16, 0:128])
        wtb = self.sB[:, 0:T]
        S.dma(wtb, bass.AP(tensor=scr_w.tensor, offset=0, ap=[[0, 128], [1, T]]),
              q="gpsimd")
        self.tap("srow", srow)
        self.tap("scol", self.sR[:, 104:120])
        self.tap("rank", self.sR[:, 120:136])
        self.tap("wt", self.sR[:, 168:184])
        self.tap("wtb", wtb)
        A0, B0 = 0.7978845608028654, 0.0356774081363001

        def h1sl(ft, lo, hi):
            if ft < 24:
                return self.sA[:, ft * 1024 + lo:ft * 1024 + hi]
            return self.sB[:, 2048 + (ft - 24) * 1024 + lo:2048 + (ft - 24) * 1024 + hi]

        for th in range(2):
            t0 = th * 1024
            for ft in range(32):
                self.load_wchunks(self.sW[:, 0:1024], W1, D, 128,
                                  col0=ft * 128, ncols=128)
                for tcc in range(2):
                    lo = t0 + tcc * SL
                    for c in range(NDT):
                        S.mm(self.pC[:, 0:SL], self.sW[:, c * 128:(c + 1) * 128],
                             self.xnt(c, lo, lo + SL),
                             start=(c == 0), stop=(c == NDT - 1))
                    b1 = self.sR[:, 184 + ft:185 + ft]
                    uf = self.sW[:, 1024:1536]
                    s2 = self.sW[:, 1536:2048]
                    vv = self.sW[:, 2048:2560]
                    ww = self.sW[:, 2560:3072]
                    S.ts(uf, self.pC[:, 0:SL], b1, None, ALU.add)
                    S.act(s2, self.pC[:, 0:SL], AF.Square, bias=b1)
                    S.ts(vv, s2, B0, A0, ALU.mult, ALU.add)
                    S.tt(ww, vv, uf, ALU.mult)
                    S.act(vv, ww, AF.Tanh)
                    S.stt(h1sl(ft, tcc * SL, tcc * SL + SL), vv, 1.0, uf,
                          ALU.add, ALU.mult)
            if th == 0:
                self.tap("h1_f0", h1sl(0, 0, 1024))
            for dt in range(NDT):
                for kh in range(2):
                    src = W2.rearrange("(c p) o -> p c o", p=128)
                    S.dma(self.sW[:, 0:2048],
                          src[:, kh * 16:(kh + 1) * 16, dt * 128:(dt + 1) * 128],
                          q="gpsimd")
                    for tcc in range(2):
                        ptile = self.pC[:, 0:SL] if tcc == 0 else self.pB[:, 0:SL]
                        for c in range(16):
                            ft = kh * 16 + c
                            S.mm(ptile, self.sW[:, c * 128:(c + 1) * 128],
                                 h1sl(ft, tcc * SL, tcc * SL + SL),
                                 start=(kh == 0 and c == 0),
                                 stop=(kh == 1 and c == 15))
                for tcc in range(2):
                    lo = t0 + tcc * SL
                    ptile = self.pC[:, 0:SL] if tcc == 0 else self.pB[:, 0:SL]
                    tv = self.sC[:, 0:SL]
                    t2 = self.sC[:, SL:2 * SL]
                    S.tt(tv, ptile, wtb[:, lo:lo + SL], ALU.mult)
                    S.stt(t2, wtb[:, lo:lo + SL], self.sR[:, 224 + dt:225 + dt],
                          tv, ALU.mult, ALU.add)
                    xi = self.xt(dt, lo, lo + SL)
                    S.stt(xi, xi, 2.0, t2, ALU.mult, ALU.add)
        self.postln()

    def l3_ffn(self):
        S = self.S
        rw = self.inp("L3_ffn_router_w", [D, 8])
        rb = self.inp("L3_ffn_router_b", [8])
        W1 = self.inp("L3_ffn_W1", [8, D, 512])
        W2 = self.inp("L3_ffn_W2", [8, 512, D])
        b1c = self.inp("d_L3f_b1_cols", [8, 128, 4])
        b2 = self.inp("L3_ffn_b2", [8, D])
        self.load_wchunks(self.sW[:, 0:64], rw, D, 8, col0=0, ncols=8)
        S.dma(self.sR[0:8, 0:1], rb.rearrange("(e o) -> e o", o=1))
        for tc in range(NTC):
            lo = tc * SL
            for c in range(NDT):
                S.mm(self.pB[0:8, 0:SL], self.sW[:, c * 8:(c + 1) * 8],
                     self.xnt(c, lo, lo + SL), start=(c == 0), stop=(c == NDT - 1))
            S.ts(self.sF[0:8, lo:lo + SL], self.pB[0:8, 0:SL],
                 self.sR[0:8, 0:1], None, ALU.add)
        for j in range(16):
            S.trans(self.pA[:, j * 8:(j + 1) * 8],
                    self.sF[0:8, j * 128:(j + 1) * 128], self.k_id_f32[0:8, 0:8])
        lgT = self.sC[:, 0:128]
        S.copy(lgT, self.pA[:, 0:128])
        v3 = lambda ap: ap.rearrange("p (j e) -> p j e", e=8)
        m1 = self.sC[:, 384:400]
        m2 = self.sC[:, 400:416]
        s12 = self.sC[:, 416:432]
        w1 = self.sC[:, 128:256]
        w2 = self.sC[:, 256:384]
        gm = self.sC[:, 432:560]
        rmt = self.sC[:, 560:688]
        S.red(m1, v3(lgT), ALU.max)
        m1b = m1.unsqueeze(2).broadcast_to((128, 16, 8))
        S.tt(v3(w1), v3(lgT), m1b, ALU.subtract)
        S.act(gm, w1, AF.Exp)
        S.tt(v3(w1), v3(lgT), m1b, ALU.is_equal)
        S.stt(w2, w1, -1.0e30, lgT, ALU.mult, ALU.add)
        S.red(m2, v3(w2), ALU.max)
        m2b = m2.unsqueeze(2).broadcast_to((128, 16, 8))
        S.tt(v3(w2), v3(w2), m2b, ALU.is_equal)
        S.tt(w1, w1, w2, ALU.add)
        S.tt(gm, gm, w1, ALU.mult)
        S.red(s12, v3(gm), ALU.add)
        S.recip(s12, s12)
        s12b = s12.unsqueeze(2).broadcast_to((128, 16, 8))
        S.tt(v3(rmt), v3(gm), s12b, ALU.mult)
        for j in range(16):
            S.trans(self.pA[0:8, j * 128:(j + 1) * 128],
                    rmt[:, j * 8:(j + 1) * 8], self.k_id_f32[:])
        S.copy(self.sB[0:8, 0:T], self.pA[0:8, 0:T])
        scr_rm = self.scratch([8, T], BF16)
        S.dma(scr_rm, self.sB[0:8, 0:T])
        S.dma(self.sB[0:8, T:T + D], b2, q="gpsimd")
        for e in range(8):
            S.dma(self.sR[:, 64 + e * 4:68 + e * 4], b1c[e])
        for th in range(2):
            t0 = th * 1024
            for e in range(8):
                rbc = self.sB[:, 2048 + 8 * 1024:2048 + 9 * 1024]
                S.dma(rbc, bass.AP(tensor=scr_rm.tensor, offset=e * T + t0,
                                   ap=[[0, 128], [1, 1024]]))
                for fq in range(4):
                    src = W1.rearrange("e (c p) f -> e p c f", p=128)
                    S.dma(self.sW[:, 0:1024],
                          src[e][:, :, fq * 128:(fq + 1) * 128], q="gpsimd")
                    for tcc in range(2):
                        lo = t0 + tcc * SL
                        for c in range(NDT):
                            S.mm(self.pC[:, 0:SL], self.sW[:, c * 128:(c + 1) * 128],
                                 self.xnt(c, lo, lo + SL),
                                 start=(c == 0), stop=(c == NDT - 1))
                        tmp = self.sW[:, 1024:1536]
                        S.act(tmp, self.pC[:, 0:SL], AF.Gelu,
                              bias=self.sR[:, 64 + e * 4 + fq:65 + e * 4 + fq])
                        S.tt(self.sA[:, fq * 1024 + tcc * SL:fq * 1024 + tcc * SL + SL],
                             tmp, rbc[:, tcc * SL:tcc * SL + SL], ALU.mult)
                for dtp in range(4):
                    src2 = W2.rearrange("e (c p) o -> e p c o", p=128)
                    S.dma(self.sW[:, 0:1024],
                          src2[e][:, :, dtp * 256:(dtp + 1) * 256], q="gpsimd")
                    for dh in range(2):
                        dt = dtp * 2 + dh
                        for tcc in range(2):
                            lo = t0 + tcc * SL
                            for c in range(4):
                                S.mm(self.pC[:, 0:SL],
                                     self.sW[:, c * 256 + dh * 128:c * 256 + dh * 128 + 128],
                                     self.sA[:, c * 1024 + tcc * SL:c * 1024 + tcc * SL + SL],
                                     start=(c == 0), stop=(c == 3))
                            xi = self.xt(dt, lo, lo + SL)
                            S.tt(xi, xi, self.pC[:, 0:SL], ALU.add)
            for dt in range(NDT):
                for tcc in range(2):
                    lo = t0 + tcc * SL
                    S.mm(self.pC[:, 0:SL],
                         self.sB[0:8, T + dt * 128:T + (dt + 1) * 128],
                         self.sB[0:8, lo:lo + SL], start=True, stop=True)
                    xi = self.xt(dt, lo, lo + SL)
                    S.tt(xi, xi, self.pC[:, 0:SL], ALU.add)
        self.postln()


    def _h1sl(self, ft, lo, hi):
        if ft < 24:
            return self.sA[:, ft * 1024 + lo:ft * 1024 + hi]
        return self.sB[:, (ft - 24) * 1024 + lo:(ft - 24) * 1024 + hi]

    def l1_ffn(self):
        S = self.S
        Wexp = self.inp("L1_ffn_W_exp", [D, 4096])
        bexp = self.inp("d_bexp_cols", [128, 32])
        upb = self.inp("d_upbiasP_cols", [128, 32])
        WconP = self.inp("d_W_conP", [4096, D])
        bcon = self.inp("d_bcon_cols", [128, 8])
        dbias = self.inp("d_dbias_cols", [128, 8])
        S.dma(self.sR[:, 64:96], bexp[:])
        S.dma(self.sR[:, 96:128], upb[:])
        S.dma(self.sR[:, 128:136], bcon[:])
        S.dma(self.sR[:, 136:144], dbias[:])
        S.tt(self.sR[:, 128:136], self.sR[:, 128:136], self.sR[:, 136:144], ALU.add)
        CTOF = {"up1": 8192, "up2": 10240, "dn1": 12288, "dn2": 14336}
        for gname in ("up1", "up2", "dn1", "dn2"):
            fac = self.inp(f"d_bf_{gname}", [128, 12288])
            S.dma(self.sA[:, 0:12288], fac[:], q="gpsimd")
            f = lambda s: self.sA[:, s * 2048:(s + 1) * 2048]
            pong = self.sA[:, 12288:14336]
            ping = self.sW[:, 0:2048]
            S.tt(pong, f(0), f(1), ALU.mult)
            S.tt(ping, pong, f(2), ALU.mult)
            S.tt(pong, ping, f(3), ALU.mult)
            S.tt(ping, pong, f(4), ALU.mult)
            S.tt(self.sB[:, CTOF[gname]:CTOF[gname] + 2048], ping, f(5), ALU.mult)

        def apply_group(gname, t0, gelu=False):
            off = CTOF[gname]
            for m in range(32):
                for tcc in range(2):
                    lo = t0 + tcc * SL
                    hl = self._h1sl(m, (lo - t0), (lo - t0) + SL)
                    for kb in range(2):
                        S.mm(self.pC[kb * 64:kb * 64 + 64, 0:SL],
                             self.sB[kb * 64:kb * 64 + 64, off + m * 64:off + (m + 1) * 64],
                             self._h1sl(m, (lo - t0), (lo - t0) + SL)[kb * 64:kb * 64 + 64, :],
                             start=True, stop=True)
                    if gelu:
                        S.act(hl, self.pC[:, 0:SL], AF.Gelu,
                              bias=self.sR[:, 96 + m:97 + m])
                    else:
                        S.copy(hl, self.pC[:, 0:SL])

        scr_h = self.scratch([4096, 1024], BF16)

        def permute(t0):
            S.dma(scr_h.rearrange("(i p) t -> p i t", p=128)[:, 0:24, :],
                  self.sA[:, 0:24576])
            S.dma(scr_h.rearrange("(i p) t -> p i t", p=128)[:, 24:32, :],
                  self.sB[:, 0:8192])
            for p1 in range(2):
                S.dma(self.sA[p1 * 64:(p1 + 1) * 64, 0:24576]
                      .rearrange("p (i t) -> p i t", t=1024),
                      bass.AP(tensor=scr_h.tensor, offset=p1 * 1024,
                              ap=[[65536, 64], [2048, 24], [1, 1024]]))
                S.dma(self.sB[p1 * 64:(p1 + 1) * 64, 0:8192]
                      .rearrange("p (i t) -> p i t", t=1024),
                      bass.AP(tensor=scr_h.tensor, offset=p1 * 1024 + 48 * 1024,
                              ap=[[65536, 64], [2048, 8], [1, 1024]]))

        for th in range(2):
            t0 = th * 1024
            for ft in range(32):
                self.load_wchunks(self.sW[:, 0:1024], Wexp, D, 128,
                                  col0=ft * 128, ncols=128)
                for tcc in range(2):
                    lo = t0 + tcc * SL
                    for c in range(NDT):
                        S.mm(self.pC[:, 0:SL], self.sW[:, c * 128:(c + 1) * 128],
                             self.xnt(c, lo, lo + SL),
                             start=(c == 0), stop=(c == NDT - 1))
                    S.ts(self._h1sl(ft, tcc * SL, tcc * SL + SL), self.pC[:, 0:SL],
                         self.sR[:, 64 + ft:65 + ft], None, ALU.add)
            apply_group("up1", t0)
            permute(t0)
            apply_group("up2", t0, gelu=True)
            permute(t0)
            apply_group("dn1", t0)
            permute(t0)
            apply_group("dn2", t0)
            for dt in range(NDT):
                for kh in range(2):
                    src = WconP.rearrange("(c p) o -> p c o", p=128)
                    S.dma(self.sW[:, 0:2048],
                          src[:, kh * 16:(kh + 1) * 16, dt * 128:(dt + 1) * 128],
                          q="gpsimd")
                    for tcc in range(2):
                        ptile = self.pC[:, 0:SL] if tcc == 0 else self.pB[:, 0:SL]
                        for c in range(16):
                            S.mm(ptile, self.sW[:, c * 128:(c + 1) * 128],
                                 self._h1sl(kh * 16 + c, tcc * SL, tcc * SL + SL),
                                 start=(kh == 0 and c == 0),
                                 stop=(kh == 1 and c == 15))
                for tcc in range(2):
                    lo = t0 + tcc * SL
                    ptile = self.pC[:, 0:SL] if tcc == 0 else self.pB[:, 0:SL]
                    S.ts(self.sC[:, 0:SL], ptile, self.sR[:, 128 + dt:129 + dt],
                         None, ALU.add)
                    xi = self.xt(dt, lo, lo + SL)
                    S.tt(xi, xi, self.sC[:, 0:SL], ALU.add)
        self.postln()

    def _quant(self, dst, src, width, btmp):
        S = self.S
        nb = width // 16
        mx = self.sR[:, 320:320 + nb]
        sc = self.sR[:, 408:408 + nb]
        B1, B2, B3 = btmp
        v = self.sF[:, 0:width]
        ftmp = self.sF[:, width:2 * width]
        vview = lambda ap: ap.rearrange("p (b g) -> p b g", g=16)
        S.red(mx[:, 0:nb], vview(src), ALU.max, absval=True)
        S.ts(sc[:, 0:nb], mx[:, 0:nb], 1.0 / 6.0, 1e-12, ALU.mult, ALU.max)
        S.recip(sc[:, 0:nb], sc[:, 0:nb])
        S.act(v, src, AF.Abs)
        rcb = sc[:, 0:nb].unsqueeze(2).broadcast_to((128, nb, 16))
        S.tt(vview(v), vview(v), rcb, ALU.mult)
        S.ts(ftmp, v, 2.25, 2.0, ALU.min, ALU.mult)
        S.ts(B2, ftmp, MAGIC, MAGIC, ALU.add, ALU.subtract)
        S.ts(ftmp, v, 2.25, 5.0, ALU.max, ALU.min)
        S.ts(B3, ftmp, MAGIC, MAGIC, ALU.add, ALU.subtract)
        S.ts(B1, B3, 4.0, -2.0, ALU.min, ALU.add)
        S.stt(B3, B2, 0.5, B1, ALU.mult, ALU.add)
        S.ts(B1, v, 5.0, 2.0, ALU.is_gt, ALU.mult)
        S.tt(B2, B3, B1, ALU.add)
        S.act(B1, src, AF.Sign)
        S.tt(B3, B2, B1, ALU.mult)
        S.recip(sc[:, 0:nb], sc[:, 0:nb])
        scb = sc[:, 0:nb].unsqueeze(2).broadcast_to((128, nb, 16))
        S.tt(vview(dst), vview(B3), scb, ALU.mult)

    def l2_ffn(self):
        S = self.S
        wg = self.inp("L2_ffn_w_gate", [2816, D])
        wu = self.inp("L2_ffn_w_up", [2816, D])
        wd = self.inp("L2_ffn_w_down", [D, 2816])
        scr_h = self.scratch([2816, T], BF16)
        scr_hq = self.scratch([T, 2816], BF16)
        scr_wdT = self.scratch([2816, D], BF16)
        btmp1k = (self.sB[:, 1024:2048], self.sB[:, 2048:3072], self.sB[:, 3072:4096])
        for tt in range(16):
            t0 = tt * 128
            for i in range(NDT):
                S.trans(self.pA[:, i * 128:(i + 1) * 128],
                        self.xt(i, t0, t0 + 128), self.k_id_f32[:])
            S.copy(self.sC[:, 0:1024], self.pA[:, 0:1024])
            self._quant(self.sB[:, 0:1024], self.sC[:, 0:1024], 1024, btmp1k)
            for i in range(NDT):
                S.trans(self.pT[:, i * 128:(i + 1) * 128],
                        self.sB[:, i * 128:(i + 1) * 128], self.k_id_bf[:])
            for i in range(NDT):
                S.copy(self.xnt(i, t0, t0 + 128), self.pT[:, i * 128:(i + 1) * 128])
        for rt in range(22):
            r0 = rt * 128
            for (wap, slot) in ((wg, 4096), (wu, 5120)):
                S.dma(self.sC[:, 0:1024], wap[r0:r0 + 128, :])
                self._quant(self.sB[:, 0:1024], self.sC[:, 0:1024], 1024, btmp1k)
                for i in range(NDT):
                    S.trans(self.pT[:, i * 128:(i + 1) * 128],
                            self.sB[:, i * 128:(i + 1) * 128], self.k_id_bf[:])
                S.copy(self.sB[:, slot:slot + 1024], self.pT[:, 0:1024])
            for tc in range(NTC):
                lo = tc * SL
                for c in range(NDT):
                    S.mm(self.pC[:, 0:SL], self.sB[:, 4096 + c * 128:4096 + (c + 1) * 128],
                         self.xnt(c, lo, lo + SL), start=(c == 0), stop=(c == NDT - 1))
                for c in range(NDT):
                    S.mm(self.pB[:, 0:SL], self.sB[:, 5120 + c * 128:5120 + (c + 1) * 128],
                         self.xnt(c, lo, lo + SL), start=(c == 0), stop=(c == NDT - 1))
                S.act(self.sW[:, 0:SL], self.pC[:, 0:SL], AF.Silu)
                S.tt(self.sW[:, SL:2 * SL], self.sW[:, 0:SL], self.pB[:, 0:SL],
                     ALU.mult)
                S.dma(scr_h[r0:r0 + 128, lo:lo + SL], self.sW[:, SL:2 * SL])
        wdv = scr_wdT.rearrange("(kb p) o -> p kb o", p=128)
        btmp14 = (self.sB[:, 1408:2816], self.sB[:, 2816:4224], self.sB[:, 4224:5632])
        for rt in range(NDT):
            for ch in range(2):
                wcols = 1408
                S.dma(self.sC[:, 0:wcols], wd[rt * 128:(rt + 1) * 128,
                                              ch * wcols:(ch + 1) * wcols])
                self._quant(self.sB[:, 0:wcols], self.sC[:, 0:wcols], wcols, btmp14)
                for g in range(2):
                    nblk = 8 if g == 0 else 3
                    for j in range(nblk):
                        blk = g * 8 + j
                        S.trans(self.pT[:, j * 128:(j + 1) * 128],
                                self.sB[:, blk * 128:(blk + 1) * 128], self.k_id_bf[:])
                    S.copy(self.sB[:, 5632 + g * 1024:5632 + g * 1024 + nblk * 128],
                           self.pT[:, 0:nblk * 128])
                S.dma(wdv[:, ch * 11:(ch + 1) * 11, rt * 128:(rt + 1) * 128],
                      self.sB[:, 5632:5632 + 1408])
        for tt in range(16):
            t0 = tt * 128
            S.dmaT(self.sW[:, 0:2816], scr_h[:, t0:t0 + 128])
            self._quant(self.sB[:, 0:1408], self.sW[:, 0:1408], 1408, btmp14)
            self._quant(self.sB[:, 1408:2816] if False else self.sB[:, 5632:7040],
                        self.sW[:, 1408:2816], 1408, btmp14)
            S.dma(scr_hq[t0:t0 + 128, 0:1408], self.sB[:, 0:1408])
            S.dma(scr_hq[t0:t0 + 128, 1408:2816], self.sB[:, 5632:7040])
        wdl = scr_wdT.rearrange("(c p) o -> p c o", p=128)
        for kh in range(2):
            for j in range(11):
                kc = kh * 11 + j
                S.dmaT(self.sA[:, j * T:(j + 1) * T], scr_hq[:, kc * 128:(kc + 1) * 128])
            for dt in range(NDT):
                S.dma(self.sW[:, 0:1408],
                      wdl[:, kh * 11:(kh + 1) * 11, dt * 128:(dt + 1) * 128],
                      q="sync")
                for tc in range(NTC):
                    lo = tc * SL
                    for c in range(11):
                        S.mm(self.pC[:, 0:SL], self.sW[:, c * 128:(c + 1) * 128],
                             self.sA[:, c * T + lo:c * T + lo + SL],
                             start=(c == 0), stop=(c == 10))
                    xi = self.xt(dt, lo, lo + SL)
                    S.tt(xi, xi, self.pC[:, 0:SL], ALU.add)
        self.postln()

    def _ut(self, slab, lo, hi):
        if slab < 12:
            return self.sA[:, slab * T + lo:slab * T + hi]
        return self.sB[:, (slab - 12) * T + lo:(slab - 12) * T + hi]

    def l1_attn(self, upto=9):
        S = self.S
        Win = self.inp("L1_attn_W_in", [D, 4096])
        Wx = self.inp("L1_attn_W_x", [2048, 96])
        Wdt = self.inp("L1_attn_W_dt", [64, 2048])
        Wout = self.inp("L1_attn_W_out", [2048, D])
        convc = self.inp("d_L1_conv_cols", [128, 64])
        convb = self.inp("d_L1_convb_cols", [128, 16])
        bdt = self.inp("d_L1_bdt_cols", [128, 16])
        Dcol = self.inp("d_L1_D_cols", [128, 16])
        alog = self.inp("d_L1_alog_cols", [128, 256])
        S.dma(self.sR[:, 0:64], convc[:])
        S.dma(self.sR[:, 64:80], convb[:])
        S.dma(self.sR[:, 80:96], bdt[:])
        S.dma(self.sR[:, 96:112], Dcol[:])
        S.dma(self.sR[:, 256:512], alog[:])
        S.act(self.sC[:, 0:256], self.sR[:, 256:512], AF.Exp)
        S.ts(self.sR[:, 256:512], self.sC[:, 0:256], -1.0, None, ALU.mult)
        scr_x = self.scratch([128, NDT * T], F32)
        scr_res = self.scratch([2048, T], BF16)
        scr_B = self.scratch([16, T], BF16)
        scr_C = self.scratch([16, T], BF16)
        scr_y2 = self.scratch([2048, T], BF16)
        S.dma(scr_x, self.x[:])
        dlt_all = self.x[:].bitcast(BF16)
        for ft in range(32):
            self.load_wchunks(self.sB[:, 8192:9216], Win, D, 128,
                              col0=ft * 128, ncols=128)
            for tc in range(NTC):
                lo = tc * SL
                for c in range(NDT):
                    S.mm(self.pA[:, lo:lo + SL],
                         self.sB[:, 8192 + c * 128:8192 + (c + 1) * 128],
                         self.xnt(c, lo, lo + SL),
                         start=(c == 0), stop=(c == NDT - 1))
            if ft < 16:
                S.memset(self.sW[:, 0:3], 0.0)
                S.copy(self.sW[:, 3:2051], self.pA[:, 0:T])
                wc = lambda i: self.sR[:, ft * 4 + i:ft * 4 + i + 1]
                S.ts(self.sC[:, 0:T], self.sW[:, 0:T], wc(0), None, ALU.mult)
                S.stt(self.sF[:, 0:2048], self.sW[:, 1:2049], wc(1),
                      self.sC[:, 0:T], ALU.mult, ALU.add)
                S.stt(self.sC[:, 0:T], self.sW[:, 2:2050], wc(2),
                      self.sF[:, 0:2048], ALU.mult, ALU.add)
                S.stt(self.sF[:, 0:2048], self.sW[:, 3:2051], wc(3),
                      self.sC[:, 0:T], ALU.mult, ALU.add)
                S.act(self._ut(ft, 0, T), self.sF[:, 0:2048], AF.Silu,
                      bias=self.sR[:, 64 + ft:65 + ft])
            else:
                S.copy(self.sB[:, 9216:11264], self.pA[:, 0:T])
                S.dma(scr_res[(ft - 16) * 128:(ft - 15) * 128, :],
                      self.sB[:, 9216:11264])
        if upto < 2:
            return
        self.load_wchunks(self.sB[:, 8192:9728], Wx, 2048, 96, col0=0, ncols=96)
        for tc in range(NTC):
            lo = tc * SL
            for c in range(16):
                S.mm(self.pB[0:96, 0:SL], self.sB[:, 8192 + c * 96:8192 + (c + 1) * 96],
                     self._ut(c, lo, lo + SL), start=(c == 0), stop=(c == 15))
            S.copy(self.sB[0:64, 11264 + lo:11264 + lo + SL], self.pB[0:64, 0:SL])
            S.copy(self.sB[64:96, 13312 + lo:13312 + lo + SL], self.pB[64:96, 0:SL])
        S.dma(scr_B, self.sB[64:80, 13312:15360])
        S.dma(scr_C, self.sB[80:96, 13312:15360])
        if upto < 3:
            return
        S.dma(self.sB[0:64, 13312:15360], Wdt, q="gpsimd")
        for slab in range(16):
            for tc in range(NTC):
                lo = tc * SL
                S.mm(self.pC[:, 0:SL],
                     self.sB[0:64, 13312 + slab * 128:13312 + (slab + 1) * 128],
                     self.sB[0:64, 11264 + lo:11264 + lo + SL],
                     start=True, stop=True)
                bc = self.sR[:, 80 + slab:81 + slab]
                if upto < 3.2:
                    continue
                S.act(self.sC[:, 0:SL], self.pC[:, 0:SL], AF.Abs, bias=bc)
                S.act(self.sC[:, SL:2 * SL], self.sC[:, 0:SL], AF.Exp, scale=-1.0)
                S.act(self.sC[:, 0:SL], self.sC[:, SL:2 * SL], AF.Ln, bias=1.0)
                if upto < 3.4:
                    continue
                S.ts(self.sF[:, 0:SL], self.pC[:, 0:SL], bc, 0.0, ALU.add, ALU.max)
                if upto < 3.6:
                    continue
                S.tt(dlt_all[:, slab * T + lo:slab * T + lo + SL],
                     self.sF[:, 0:SL], self.sC[:, 0:SL], ALU.add)
        if upto < 5:
            return
        sCbf = self.sC[:].bitcast(BF16)
        carry = self.sF[:, 0:256]
        S.memset(carry, 0.0)
        for te in range(4):
            t0 = te * SL
            S.dma(self.xn[:, 0:8192],
                  bass.AP(tensor=scr_B.tensor, offset=t0,
                          ap=[[0, 128], [T, 16], [1, SL]]))
            S.dma(self.xn[:, 8192:16384],
                  bass.AP(tensor=scr_C.tensor, offset=t0,
                          ap=[[0, 128], [T, 16], [1, SL]]))
            for slab in range(16):
                du = self.sW[:, 0:SL]
                S.tt(du, dlt_all[:, slab * T + t0:slab * T + t0 + SL],
                     self._ut(slab, t0, t0 + SL), ALU.mult)
                S.ts(self.pA[:, 0:SL], self._ut(slab, t0, t0 + SL),
                     self.sR[:, 96 + slab:97 + slab], None, ALU.mult)
                for n in range(16):
                    a = sCbf[:, 0:SL]
                    b = sCbf[:, SL:2 * SL]
                    h = sCbf[:, 2 * SL:3 * SL]
                    g = sCbf[:, 3 * SL:4 * SL]
                    S.act(a, dlt_all[:, slab * T + t0:slab * T + t0 + SL],
                          AF.Exp, scale=self.sR[:, 256 + slab * 16 + n:257 + slab * 16 + n])
                    S.tt(b, du, self.xn[:, n * SL:(n + 1) * SL], ALU.mult)
                    S.scan(h, a, b, carry[:, slab * 16 + n:slab * 16 + n + 1],
                           ALU.mult, ALU.add)
                    S.copy(carry[:, slab * 16 + n:slab * 16 + n + 1],
                           h[:, SL - 1:SL])
                    S.tt(g, h, self.xn[:, 8192 + n * SL:8192 + (n + 1) * SL],
                         ALU.mult)
                    S.tt(self.pA[:, 0:SL], self.pA[:, 0:SL], g, ALU.add)
                S.dma(sCbf[:, 0:SL],
                      scr_res[slab * 128:(slab + 1) * 128, t0:t0 + SL])
                S.act(sCbf[:, SL:2 * SL], sCbf[:, 0:SL], AF.Silu)
                S.tt(sCbf[:, 2 * SL:3 * SL], self.pA[:, 0:SL],
                     sCbf[:, SL:2 * SL], ALU.mult)
                S.dma(scr_y2[slab * 128:(slab + 1) * 128, t0:t0 + SL],
                      sCbf[:, 2 * SL:3 * SL])
        if upto < 6:
            return
        S.dma(self.x[:], scr_x)
        for slab in range(16):
            S.dma(self._ut(slab, 0, T), scr_y2[slab * 128:(slab + 1) * 128, :])
        for dt in range(NDT):
            self.load_wchunks(self.sW[:, 0:2048], Wout, 2048, 128,
                              col0=dt * 128, ncols=128)
            for tc in range(NTC):
                lo = tc * SL
                for c in range(16):
                    S.mm(self.pC[:, 0:SL], self.sW[:, c * 128:(c + 1) * 128],
                         self._ut(c, lo, lo + SL), start=(c == 0), stop=(c == 15))
                xi = self.xt(dt, lo, lo + SL)
                S.tt(xi, xi, self.pC[:, 0:SL], ALU.add)
        self.postln()



_SUBLAYER_FN = {0: "l0_attn", 1: "l0_ffn", 2: "l1_attn", 3: "l1_ffn",
                4: "l2_attn", 5: "l2_ffn", 6: "l3_attn", 7: "l3_ffn"}

_cache = {}


def _build_group(g):
    key = tuple(g)
    p = Prog(g)
    p.load_consts()
    p.prelude()
    for s in g:
        getattr(p, _SUBLAYER_FN[s])()
    p.epilogue()
    sem = p.nc.semaphore("gs").__enter__()
    p.S.build(sem)
    return p


def kernel(**inputs):
    consts = make_consts()
    derived = make_derived(inputs)
    pool = {}
    pool.update(consts)
    pool.update({k: np.asarray(v) for k, v in derived.items()})
    for k, v in inputs.items():
        pool[k] = np.asarray(v)
    x = np.asarray(inputs["L0_attn_x"], dtype=np.float32)
    xcur = [np.ascontiguousarray(x[c].T) for c in range(8)]
    for g in FUSE_GROUPS:
        p = _build_group(g)
        base = {name: pool[name] for name in p.used if name != "xin"}
        in_maps = [dict(base, xin=xcur[c]) for c in range(8)]
        res = run_bass_kernel_spmd(p.nc, in_maps, core_ids=list(range(8)))
        xcur = [np.asarray(res.results[c]["xout"]) for c in range(8)]
    return np.stack([xc.T for xc in xcur]).astype(np.float32)


# version 05 (final)
# speedup vs baseline: 1.2389x; 1.0016x over previous
import numpy as np
import ml_dtypes

import concourse.bass as bass
import concourse.mybir as mybir
from concourse.bass_utils import run_bass_kernel_spmd

F32 = mybir.dt.float32
BF16 = mybir.dt.bfloat16
AF = mybir.ActivationFunctionType
ALU = mybir.AluOpType
AX = mybir.AxisListType

T = 2048
D = 1024
NDT = D // 128
SL = 512
NTC = T // SL
MAGIC = 8388608.0

FUSE_GROUPS = [[0, 1, 2, 3, 4, 5, 6, 7]]
DEBUG_TAPS = False



def _isz(dt):
    s = str(dt)
    return 4 if "32" in s else (2 if "16" in s else 1)


_PGRAN = 32
_CGRAN = 256


def _foot(ap):
    if not isinstance(ap, bass.AP):
        return None
    isz = _isz(ap.dtype)
    dims = ap.ap
    off = ap.offset
    sp = str(ap.space)
    if sp == "MemorySpace.DRAM":
        span = sum((n - 1) * abs(s) for s, n in dims)
        return (ap.tensor.name, 0, 1, off * isz, (off + span + 1) * isz, False)
    pitch = dims[0][0]
    npart = dims[0][1]
    if pitch > 0:
        p0 = off // pitch
        c0 = off - p0 * pitch
    else:
        p0, c0 = 0, off
    cspan = sum((n - 1) * abs(s) for s, n in dims[1:]) if len(dims) > 1 else 0
    if sp == "MemorySpace.PSUM":
        b0 = (c0 * isz) // 2048 * 2048
        b1 = -(-((c0 + cspan + 1) * isz) // 2048) * 2048
        return (ap.tensor.name, 0, 128, b0, b1, True)
    return (ap.tensor.name, p0, p0 + npart, c0 * isz, (c0 + cspan + 1) * isz, False)


class Seq:
    NLANES = 8

    @staticmethod
    def host_of(eng):
        if eng.startswith("qsync"):
            return "sync"
        if eng.startswith("qpool") or eng == "pool":
            return "gpsimd"
        return eng

    def __init__(self, nc):
        self.nc = nc
        self.steps = []
        self.n = 0
        self._lane_ctr = {"qsync": 0, "qpool": 0}

    def _add(self, eng, inc, fn, rd=(), wr=()):
        rd = [f for f in (_foot(a) for a in rd if a is not None) if f is not None]
        wr = [f for f in (_foot(a) for a in wr if a is not None) if f is not None]
        wr += [f for f in rd if f[5]]
        rd = [f for f in rd if not f[5]]
        self.steps.append((eng, inc, fn, rd, wr))
        self.n += inc

    def _lane(self, q):
        q = {"sync": "qsync", "gpsimd": "qpool"}[q]
        i = self._lane_ctr[q]
        self._lane_ctr[q] = (i + 1) % self.NLANES
        return f"{q}{i}"

    def dma(self, out, in_, q="sync"):
        self._add(self._lane(q), 16, lambda E, o=out, i=in_: E.dma_start(out=o, in_=i),
                  rd=[in_], wr=[out])

    def dmaT(self, out, in_, q="sync"):
        self._add(self._lane(q), 16, lambda E, o=out, i=in_: E.dma_start_transpose(o, i),
                  rd=[in_], wr=[out])

    def mm(self, out, lhsT, rhs, start=True, stop=True, tile_position=None):
        def f(E, o=out, l=lhsT, r=rhs, st=start, sp=stop, tp=tile_position):
            return E.matmul(o, l, r, start=st, stop=sp, tile_position=tp)
        self._add("tensor", 1, f, rd=[lhsT, rhs] + ([] if start else [out]), wr=[out])

    def trans(self, out, in_, ident):
        self._add("tensor", 1, lambda E, o=out, i=in_, d=ident: E.transpose(o, i, d),
                  rd=[in_, ident], wr=[out])

    def act(self, out, in_, func, bias=0.0, scale=1.0, accum_out=None):
        def f(E, o=out, i=in_, fu=func, b=bias, s=scale, a=accum_out):
            return E.activation(o, i, fu, bias=b, scale=s, accum_out=a)
        self._add("scalar", 1, f,
                  rd=[in_] + [b for b in (bias, scale) if isinstance(b, bass.AP)],
                  wr=[out] + ([accum_out] if accum_out is not None else []))

    def tt(self, out, a, b, op, eng="vector"):
        eng = "pool" if eng == "gpsimd" else eng
        self._add(eng, 1, lambda E, o=out, x=a, y=b, p=op: E.tensor_tensor(o, x, y, p),
                  rd=[a, b], wr=[out])

    def ts(self, out, in0, s1, s2, op0, op1=ALU.bypass, accum_out=None, eng="vector"):
        def f(E, o=out, i=in0, a=s1, b=s2, p0=op0, p1=op1, ac=accum_out):
            return E.tensor_scalar(o, i, a, b, p0, p1, accum_out=ac)
        eng = "pool" if eng == "gpsimd" else eng
        self._add(eng, 1, f,
                  rd=[in0] + [s for s in (s1, s2) if isinstance(s, bass.AP)],
                  wr=[out] + ([accum_out] if accum_out is not None else []))

    def tss(self, out, in0, s1, op, eng="vector"):
        eng = "pool" if eng == "gpsimd" else eng
        self._add(eng, 1, lambda E, o=out, i=in0, a=s1, p=op: E.tensor_single_scalar(o, i, a, p),
                  rd=[in0] + ([s1] if isinstance(s1, bass.AP) else []), wr=[out])

    def stt(self, out, in0, scalar, in1, op0, op1, eng="vector"):
        def f(E, o=out, i=in0, s=scalar, j=in1, p0=op0, p1=op1):
            return E.scalar_tensor_tensor(o, i, s, j, p0, p1)
        eng = "pool" if eng == "gpsimd" else eng
        self._add(eng, 1, f,
                  rd=[in0, in1] + ([scalar] if isinstance(scalar, bass.AP) else []),
                  wr=[out])

    def red(self, out, in_, op, axis=AX.X, absval=None, negate=None, eng="vector"):
        def f(E, o=out, i=in_, p=op, a=axis, ab=absval, ng=negate):
            return E.tensor_reduce(o, i, a, p, apply_absolute_value=ab, negate=ng)
        eng = "pool" if eng == "gpsimd" else eng
        self._add(eng, 1, f, rd=[in_], wr=[out])

    def copy(self, out, in_, eng="vector"):
        eng = "pool" if eng == "gpsimd" else eng
        self._add(eng, 1, lambda E, o=out, i=in_: E.tensor_copy(o, i),
                  rd=[in_], wr=[out])

    def acopy(self, out, in_, bias=0.0, scale=1.0):
        if isinstance(bias, (int, float)):
            self.act(out, in_, AF.Copy, bias=bias, scale=scale)
        else:
            assert scale == 1.0
            self.ts(out, in_, bias, None, ALU.add)

    def recip(self, out, in_):
        self._add("vector", 1, lambda E, o=out, i=in_: E.reciprocal(o, i),
                  rd=[in_], wr=[out])

    def scan(self, out, d0, d1, initial, op0, op1, eng="vector"):
        def f(E, o=out, a=d0, b=d1, ini=initial, p0=op0, p1=op1):
            return E.tensor_tensor_scan(o, a, b, ini, p0, p1)
        eng = "pool" if eng == "gpsimd" else eng
        self._add(eng, 1, f,
                  rd=[d0, d1] + ([initial] if isinstance(initial, bass.AP) else []),
                  wr=[out])

    def memset(self, ap, val, eng="vector"):
        eng = "pool" if eng == "gpsimd" else eng
        self._add(eng, 1, lambda E, a=ap, v=val: E.memset(a, v), rd=[], wr=[ap])

    def affine_sel(self, out, in_, pattern, cmp, fill, base=0, cm=1):
        def f(E, o=out, i=in_, p=pattern, c=cmp, fl=fill, b=base, m=cm):
            return E.affine_select(o, i, p, c, fl, base=b, channel_multiplier=m)
        self._add("pool", 1, f, rd=[in_], wr=[out])

    def _schedule(self):
        nsteps = len(self.steps)
        self.vengs = sorted({s[0] for s in self.steps})
        cum = {e: 0 for e in self.vengs}
        cumafter = [0] * nsteps
        veng_of = [s[0] for s in self.steps]
        tbl = {}
        deps_of = []
        last_of_veng = {}
        for i, (eng, inc, fn, rd, wr) in enumerate(self.steps):
            cum[eng] += inc
            cumafter[i] = cum[eng]
            deps = set()
            if eng.startswith("q"):
                if eng in last_of_veng:
                    deps.add(last_of_veng[eng])
            last_of_veng[eng] = i

            def buckets(fp):
                tk, p0, p1, b0, b1 = fp[:5]
                for pb in range(p0 // _PGRAN, (p1 + _PGRAN - 1) // _PGRAN):
                    for cb in range(b0 // _CGRAN, (b1 + _CGRAN - 1) // _CGRAN):
                        yield (tk, pb, cb)

            for fp in rd:
                for bk in buckets(fp):
                    e = tbl.get(bk)
                    if e is None:
                        tbl[bk] = e = {"w": None, "rs": []}
                    if e["w"] is not None:
                        deps.add(e["w"])
                    e["rs"].append(i)
            for fp in wr:
                for bk in buckets(fp):
                    e = tbl.get(bk)
                    if e is None:
                        tbl[bk] = e = {"w": None, "rs": []}
                    if e["w"] is not None:
                        deps.add(e["w"])
                    deps.update(e["rs"])
                    e["w"] = i
                    e["rs"] = []
            deps.discard(i)
            deps_of.append(deps)
        known_after = [None] * nsteps
        hosts = {self.host_of(e) for e in self.vengs}
        last_known = {h: {} for h in hosts}
        waits_of = []
        for i, (eng, inc, fn, rd, wr) in enumerate(self.steps):
            host = self.host_of(eng)
            kn = dict(last_known[host])
            waits = {}
            for p in sorted(deps_of[i]):
                pe = veng_of[p]
                v = cumafter[p]
                if kn.get(pe, 0) >= v:
                    continue
                if waits.get(pe, 0) >= v:
                    continue
                waits[pe] = v
            for pe, v in waits.items():
                if kn.get(pe, 0) < v:
                    kn[pe] = v
            for p in sorted(deps_of[i]):
                pe = veng_of[p]
                if kn.get(pe, 0) >= cumafter[p] and known_after[p]:
                    for e2, v2 in known_after[p].items():
                        if kn.get(e2, 0) < v2:
                            kn[e2] = v2
            waits_of.append(waits)
            known_after[i] = kn
            last_known[host] = dict(kn)
        return waits_of

    @staticmethod
    def _ov(a, b):
        return (a[0] == b[0] and a[1] < b[2] and b[1] < a[2]
                and a[3] < b[4] and b[3] < a[4])

    def build(self, sem_unused=None):
        nc = self.nc
        waits_of = self._schedule()
        sems = {e: nc.semaphore("sem_" + e).__enter__() for e in self.vengs}
        finals = {e: 0 for e in self.vengs}
        for (eng, inc, fn, rd, wr) in self.steps:
            finals[eng] += inc
        per_host = {h: [] for h in ("sync", "scalar", "vector", "tensor", "gpsimd")}
        for i, (eng, inc, fn, rd, wr) in enumerate(self.steps):
            per_host[self.host_of(eng)].append((i, eng, inc, fn, waits_of[i]))
        with nc.Block() as block:
            for host in ("sync", "scalar", "vector", "tensor", "gpsimd"):
                def section(E, host=host):
                    for (i, eng, inc, fn, waits) in per_host[host]:
                        for pe, v in sorted(waits.items()):
                            E.wait_ge(sems[pe], v)
                        fn(E).then_inc(sems[eng], inc)
                    for e2, v in finals.items():
                        if v > 0:
                            E.wait_ge(sems[e2], v)
                getattr(block, host)(section)



def _rope_tables(base):
    inv = 1.0 / (base ** (np.arange(0, 32, dtype=np.float64) / 32.0))
    t = np.arange(T, dtype=np.float64)
    ang = t[None, :] * inv[:, None]
    cos = np.concatenate([np.cos(ang)] * 2, 0)
    sin = np.concatenate([np.sin(ang)] * 2, 0)
    s = 8.0 ** -0.5
    cosf = (np.concatenate([cos, cos], 0) * s).astype(ml_dtypes.bfloat16)
    sinf = (np.concatenate([sin, sin], 0) * s).astype(ml_dtypes.bfloat16)
    return cosf, sinf


def _cols(a, n=None):
    a = np.asarray(a, dtype=np.float32).reshape(-1)
    n = n or (a.size // 128)
    return np.ascontiguousarray(a.reshape(n, 128).T)


def make_consts():
    c = {}
    c["c_id_bf"] = np.eye(128, dtype=ml_dtypes.bfloat16)
    c["c_id_f32"] = np.eye(128, dtype=np.float32)
    c["c_cols_bf"] = np.concatenate(
        [np.ones((128, 1), np.float32), np.full((128, 1), 1.0 / 1024.0, np.float32)],
        axis=1).astype(ml_dtypes.bfloat16)
    c["c_ones_row_f32"] = np.ones((1, 128), dtype=np.float32)
    c["c_misc_cols"] = np.ascontiguousarray(np.broadcast_to(
        np.array([1e-5, 1e-6, 1.0, 0.0], np.float32)[None, :], (128, 4)))
    c["c_ones_row_bf"] = np.ones((1, 128), dtype=ml_dtypes.bfloat16)
    c["c_cos_l0"], c["c_sin_l0"] = _rope_tables(10000.0)
    c["c_cos_l3"], c["c_sin_l3"] = _rope_tables(1000000.0)
    p = np.arange(128)[:, None]
    qt = np.arange(16)[None, :]
    c["c_rowrecip"] = (1.0 / (qt * 128 + p + 1)).astype(np.float32)
    m = np.arange(640)[None, :]
    c["c_e3shift_bf"] = np.where(p - m + 511 >= 0, 0.0, -2.0e30).astype(ml_dtypes.bfloat16)
    return c


def make_derived(inp):
    d = {}
    tr = lambda a: np.ascontiguousarray(np.asarray(a, dtype=np.float32).T)
    asf = lambda a: np.ascontiguousarray(np.asarray(a, dtype=np.float32))
    for w in ("Wq", "Wk", "Wv", "Wo"):
        d[f"d_L0_{w}T"] = tr(inp[f"L0_attn_{w}"])
    biases = np.asarray(inp["L0_attn_biases"], dtype=np.float32)
    W = 1536
    pp = np.arange(128)[:, None]
    mm_ = np.arange(W)[None, :]
    v = pp - mm_ + 1023
    es = biases[:, np.clip(v, 0, 512)]
    es = np.where(v[None] < 0, -2.0e30, np.where(v[None] > 512, 0.0, es))
    d["d_eshift_l0"] = np.ascontiguousarray(es.astype(np.float32))
    d["d_tau_bc"] = np.ascontiguousarray(
        np.broadcast_to(np.asarray(inp["L0_attn_tau"], np.float32)[None, :], (128, 16)))
    d["d_L0f_wr_cols"] = _cols(inp["L0_ffn_w_router"])
    d["d_L0f_b1_cols"] = _cols(inp["L0_ffn_b1"])
    d["d_L0f_b2_cols"] = _cols(inp["L0_ffn_b2"])
    d["d_L1_conv_cols"] = np.ascontiguousarray(
        np.asarray(inp["L1_attn_conv_w"], np.float32)[:, 0, :]
        .reshape(16, 128, 4).transpose(1, 0, 2).reshape(128, 64))
    d["d_L1_convb_cols"] = _cols(inp["L1_attn_conv_b"])
    d["d_L1_bdt_cols"] = _cols(inp["L1_attn_b_dt"])
    d["d_L1_D_cols"] = _cols(inp["L1_attn_D"])
    d["d_L1_alog_cols"] = np.ascontiguousarray(
        np.asarray(inp["L1_attn_A_log"], np.float32)
        .reshape(16, 128, 16).transpose(1, 0, 2).reshape(128, 256))
    upw = np.asarray(inp["L1_ffn_up_weights"], dtype=np.float32)
    dnw = np.asarray(inp["L1_ffn_down_weights"], dtype=np.float32)
    d["d_bf_up1"] = _bf_factors_g1(upw)
    d["d_bf_up2"] = _bf_factors_g2(upw)
    d["d_bf_dn1"] = _bf_factors_g1(dnw)
    d["d_bf_dn2"] = _bf_factors_g2(dnw)
    fp = np.arange(4096)
    f_of_fp = (fp % 64) * 64 + (fp // 64)
    d["d_bexp_cols"] = _cols(inp["L1_ffn_b_exp"])
    d["d_upbiasP_cols"] = _cols(np.asarray(inp["L1_ffn_up_bias"], np.float32)[f_of_fp])
    d["d_W_conP"] = np.ascontiguousarray(
        np.asarray(inp["L1_ffn_W_con"], np.float32)[f_of_fp, :])
    d["d_bcon_cols"] = _cols(inp["L1_ffn_b_con"])
    d["d_dbias_cols"] = _cols(inp["L1_ffn_down_bias"])
    Win = np.asarray(inp["L2_attn_W_in"], dtype=np.float32)
    d["d_L2_WinT_half"] = tr(Win[D:2 * D, :])
    d["d_L2_binh_cols"] = _cols(np.asarray(inp["L2_attn_b_in"], np.float32)[D:2 * D])
    d["d_L2_WgateT"] = tr(inp["L2_attn_W_gate"])
    d["d_L2_bg_cols"] = _cols(inp["L2_attn_b_gate"])
    d["d_L2_A"] = asf(inp["L2_attn_A"])
    d["d_L2_AT"] = tr(inp["L2_attn_A"])
    d["d_L2_BmT"] = tr(inp["L2_attn_Bm"])
    d["d_L2_CmT"] = asf(inp["L2_attn_Cm"]).T.copy()
    d["d_L2_WoutT"] = tr(inp["L2_attn_W_out"])
    d["d_L2_bo_cols"] = _cols(inp["L2_attn_b_out"])
    d["d_L3_ns_cols"] = _cols(inp["L3_attn_norm_scale"])
    d["d_L3f_b1_cols"] = np.ascontiguousarray(
        np.asarray(inp["L3_ffn_b1"], np.float32).reshape(8, 4, 128).transpose(0, 2, 1))
    return d


def _dev_layout(out):
    dev = np.empty((128, 6 * 32 * 64), dtype=np.float32)
    for s in range(6):
        for k in range(64):
            kb, kh = k % 2, k // 2
            dev[kb * 64:(kb + 1) * 64, s * 2048 + kh * 64:s * 2048 + (kh + 1) * 64] = out[s, k]
    return np.ascontiguousarray(dev)


def _bf_factors_g1(wts):
    out = np.empty((6, 64, 64, 64), dtype=np.float32)
    i = np.arange(64)[None, :]
    j = np.arange(64)[:, None]
    for s in range(6):
        sig = 1 << s
        w = np.asarray(wts[s]).reshape(2048 // sig, sig, 2, 2)
        for k in range(64):
            nb = k * (1 << (5 - s)) + (j >> (s + 1))
            low = np.broadcast_to(i & (sig - 1), (64, 64))
            r = np.broadcast_to((i >> s) & 1, (64, 64))
            c = np.broadcast_to((j >> s) & 1, (64, 64))
            nbb = np.broadcast_to(nb, (64, 64))
            out[s, k] = w[nbb, low, r, c]
    return _dev_layout(out)


def _bf_factors_g2(wts):
    out = np.empty((6, 64, 64, 64), dtype=np.float32)
    i = np.arange(64)[None, :]
    j = np.arange(64)[:, None]
    for sp in range(6):
        s = sp + 6
        sig = 1 << s
        w = np.asarray(wts[s]).reshape(4096 // (2 * sig), sig, 2, 2)
        for kp in range(64):
            nb = np.broadcast_to(j >> (sp + 1), (64, 64))
            low = np.broadcast_to(kp + 64 * (i & ((1 << sp) - 1)), (64, 64))
            r = np.broadcast_to((i >> sp) & 1, (64, 64))
            c = np.broadcast_to((j >> sp) & 1, (64, 64))
            out[sp, kp] = w[nb, low, r, c]
    return _dev_layout(out)



class Prog:
    def __init__(self, sublayers):
        self.nc = bass.Bass("TRN2", target_bir_lowering=False,
                            dynamic_dma_scratch_size=16384)
        self.S = Seq(self.nc)
        self.sub = sublayers
        self.dram = {}
        self.used = []
        self._scr = 0
        self._alloc()

    def inp(self, name, shape, dtype=F32):
        if name not in self.dram:
            self.dram[name] = self.nc.dram_tensor(name, list(shape), dtype,
                                                  kind="ExternalInput").ap()
            self.used.append(name)
        return self.dram[name]

    def tap(self, name, ap):
        if not DEBUG_TAPS:
            return
        sh = list(ap.shape)
        out = self.nc.dram_tensor("tap_" + name, sh, ap.dtype, kind="ExternalOutput").ap()
        self.S.dma(out, ap)

    def scratch(self, shape, dtype):
        self._scr += 1
        return self.nc.dram_tensor(f"scr{self._scr}", list(shape), dtype,
                                   kind="Internal").ap()

    def _alloc(self):
        nc = self.nc
        a = lambda nm, sh, dt: nc.alloc_sbuf_tensor(nm, sh, dt).ap()
        self.x = a("x", [128, NDT * T], F32)
        self.xn = a("xn", [128, NDT * T], BF16)
        self.sA = a("sA", [128, 24576], BF16)
        self.sB = a("sB", [128, 16384], BF16)
        self.sF = a("sF", [128, 3072], F32)
        self.sW = a("sW", [128, 3072], BF16)
        self.sC = a("sC", [128, 2048], F32)
        self.sR = a("sR", [128, 512], F32)
        self.k_id_bf = a("k_id_bf", [128, 128], BF16)
        self.k_id_f32 = a("k_id_f32", [128, 128], F32)
        self.k_cols = a("k_cols", [128, 2], BF16)
        self.k_row_f32 = a("k_row_f32", [1, 128], F32)
        self.k_misc = a("k_misc", [128, 4], F32)
        self.k_row_bf = a("k_row_bf", [1, 128], BF16)
        self.pA = nc.alloc_psum_tensor("pA", [128, 2048], F32).ap()
        self.pB = nc.alloc_psum_tensor("pB", [128, 1024], F32).ap()
        self.pC = nc.alloc_psum_tensor("pC", [128, 512], F32).ap()
        self.pT = nc.alloc_psum_tensor("pT", [128, 1024], BF16).ap()

    def load_consts(self):
        S = self.S
        S.dma(self.k_id_bf[:], self.inp("c_id_bf", [128, 128], BF16)[:])
        S.dma(self.k_id_f32[:], self.inp("c_id_f32", [128, 128], F32)[:])
        S.dma(self.k_cols[:], self.inp("c_cols_bf", [128, 2], BF16)[:])
        S.dma(self.k_row_f32[:], self.inp("c_ones_row_f32", [1, 128], F32)[:])
        S.dma(self.k_misc[:], self.inp("c_misc_cols", [128, 4], F32)[:])
        S.dma(self.k_row_bf[:], self.inp("c_ones_row_bf", [1, 128], BF16)[:])

    def xt(self, i, lo=0, hi=T):
        return self.x[:, i * T + lo:i * T + hi]

    def xnt(self, i, lo=0, hi=T):
        return self.xn[:, i * T + lo:i * T + hi]

    def load_wchunks(self, dst, dram2d, rows, cols, col0=0, ncols=None):
        w = ncols if ncols is not None else cols
        nch = rows // 128
        src = dram2d.rearrange("(c p) o -> p c o", p=128)[:, :, col0:col0 + w]
        self.S.dma(dst[:, 0:nch * w], src, q="gpsimd")

    def prelude(self, skip_xn=False):
        S = self.S
        xin = self.inp("xin", [D, T], F32)
        S.dma(self.x[:], xin.rearrange("(i p) t -> p i t", p=128))
        if not skip_xn:
            for i in range(NDT):
                S.copy(self.xnt(i), self.xt(i))

    def epilogue(self):
        S = self.S
        xout = self.nc.dram_tensor("xout", [D, T], F32, kind="ExternalOutput").ap()
        S.dma(xout.rearrange("(i p) t -> p i t", p=128), self.x[:])

    def bcast_row(self, psum_dst, row_ap):
        self.S.mm(psum_dst, self.k_row_f32[:], row_ap, start=True, stop=True)

    def postln(self, eps=1e-5):
        S = self.S
        inv = self.k_cols[:, 1:2]
        for i in range(NDT):
            S.copy(self.sB[:, i * T:(i + 1) * T], self.xt(i))
        for tc in range(NTC):
            lo = tc * SL
            for i in range(NDT):
                S.mm(self.pA[0:1, lo:lo + SL], inv,
                     self.sB[:, i * T + lo:i * T + lo + SL],
                     start=(i == 0), stop=(i == NDT - 1))
        mean_row = self.sF[0:1, 0:T]
        S.acopy(mean_row, self.pA[0:1, 0:T])
        for i in range(NDT):
            S.act(self.sB[:, i * T:(i + 1) * T], self.xt(i), AF.Square)
        for tc in range(NTC):
            lo = tc * SL
            for i in range(NDT):
                S.mm(self.pA[0:1, lo:lo + SL], inv,
                     self.sB[:, i * T + lo:i * T + lo + SL],
                     start=(i == 0), stop=(i == NDT - 1))
        for tc in range(NTC):
            lo = tc * SL
            t1 = self.sF[0:1, T + 0:T + SL]
            t2 = self.sF[0:1, T + SL:T + 2 * SL]
            mch = self.sF[0:1, lo:lo + SL]
            S.act(t1, mch, AF.Square)
            S.tt(t1, self.pA[0:1, lo:lo + SL], t1, ALU.subtract)
            S.act(t2, t1, AF.Ln, bias=self.k_misc[0:1, 0:1])
            S.act(t1, t2, AF.Exp, scale=-0.5)
            S.tt(t2, mch, t1, ALU.mult)
            self.bcast_row(self.pB[:, 0:SL], t1)
            self.bcast_row(self.pB[:, SL:2 * SL], t2)
            for i in range(NDT):
                xi = self.xt(i, lo, lo + SL)
                S.tt(xi, xi, self.pB[:, 0:SL], ALU.mult)
                S.tt(xi, xi, self.pB[:, SL:2 * SL], ALU.subtract)
                S.copy(self.xnt(i, lo, lo + SL), xi)

    def l2_attn(self):
        S = self.S
        winT = self.inp("d_L2_WinT_half", [D, D])
        wgT = self.inp("d_L2_WgateT", [D, D])
        binh = self.inp("d_L2_binh_cols", [128, 8])
        bg = self.inp("d_L2_bg_cols", [128, 8])
        S.dma(self.sR[:, 0:8], binh[:])
        S.dma(self.sR[:, 8:16], bg[:])
        for i in range(NDT):
            self.load_wchunks(self.sW[:, 0:1024], winT, D, 128, col0=i * 128, ncols=128)
            self.load_wchunks(self.sW[:, 1024:2048], wgT, D, 128, col0=i * 128, ncols=128)
            for tc in range(NTC):
                lo = tc * SL
                for c in range(NDT):
                    S.mm(self.pA[:, lo:lo + SL], self.sW[:, c * 128:(c + 1) * 128],
                         self.xnt(c, lo, lo + SL), start=(c == 0), stop=(c == NDT - 1))
                for c in range(NDT):
                    S.mm(self.pB[:, 0:SL], self.sW[:, 1024 + c * 128:1024 + (c + 1) * 128],
                         self.xnt(c, lo, lo + SL), start=(c == 0), stop=(c == NDT - 1))
                S.acopy(self.sB[:, 0:SL], self.pA[:, lo:lo + SL], bias=self.sR[:, i:i + 1])
                S.act(self.sB[:, SL:2 * SL], self.pB[:, 0:SL], AF.Sigmoid,
                      bias=self.sR[:, 8 + i:9 + i])
                S.tt(self.sA[:, i * T + lo:i * T + lo + SL],
                     self.sB[:, 0:SL], self.sB[:, SL:2 * SL], ALU.mult)
        self.tap("u", self.sA[:, 0:NDT * T])
        self.tap("x_a", self.x[:])
        bmT = self.inp("d_L2_BmT", [D, 16])
        self.load_wchunks(self.sW[:, 0:128], bmT, D, 16, col0=0, ncols=16)
        for tc in range(NTC):
            lo = tc * SL
            for c in range(NDT):
                S.mm(self.pA[0:16, lo:lo + SL], self.sW[:, c * 16:(c + 1) * 16],
                     self.sA[:, c * T + lo:c * T + lo + SL],
                     start=(c == 0), stop=(c == NDT - 1))
        H = self.sB[0:16, 0:T]
        S.copy(H, self.pA[0:16, 0:T])
        self.tap("uB", H)
        self.tap("x_b", self.x[:])
        S.dma(self.sW[0:16, 2048:2064], self.inp("d_L2_A", [16, 16])[:], q="gpsimd")
        S.dma(self.sW[0:16, 2064:2080], self.inp("d_L2_AT", [16, 16])[:], q="gpsimd")
        Msb = self.sW[0:16, 2048:2064]
        Tsb = self.sW[0:16, 2064:2080]
        for k in range(11):
            sh = 1 << k
            for tc in range(NTC):
                lo = tc * SL
                r0 = max(lo, sh)
                if r0 >= (tc + 1) * SL:
                    continue
                n = (tc + 1) * SL - r0
                S.mm(self.pA[0:16, r0:r0 + n], Tsb, self.sB[0:16, r0 - sh:r0 - sh + n],
                     start=True, stop=True)
            S.tt(self.sB[0:16, sh:T], self.sB[0:16, sh:T], self.pA[0:16, sh:T], ALU.add)
            if k < 10:
                S.mm(self.pC[0:16, 0:16], Tsb, Msb, start=True, stop=True)
                S.copy(Msb, self.pC[0:16, 0:16])
                S.trans(self.pT[0:16, 0:16], Msb, self.k_id_bf[0:16, 0:16])
                S.copy(Tsb, self.pT[0:16, 0:16])
        self.tap("H", H)
        self.tap("x_c", self.x[:])
        cmT = self.inp("d_L2_CmT", [16, D])
        S.dma(self.sW[0:16, 0:1024], cmT[:], q="gpsimd")
        for i in range(NDT):
            for tc in range(NTC):
                lo = tc * SL
                S.mm(self.pA[:, lo:lo + SL], self.sW[0:16, i * 128:(i + 1) * 128],
                     self.sB[0:16, lo:lo + SL], start=True, stop=True)
            S.copy(self.sA[:, i * T:(i + 1) * T], self.pA[:, 0:T])
        self.tap("y", self.sA[:, 0:NDT * T])
        self.tap("xpre", self.x[:])
        woT = self.inp("d_L2_WoutT", [D, D])
        bo = self.inp("d_L2_bo_cols", [128, 8])
        S.dma(self.sR[:, 16:24], bo[:])
        for i in range(NDT):
            self.load_wchunks(self.sW[:, 0:1024], woT, D, 128, col0=i * 128, ncols=128)
            for tc in range(NTC):
                lo = tc * SL
                for c in range(NDT):
                    S.mm(self.pA[:, lo:lo + SL], self.sW[:, c * 128:(c + 1) * 128],
                         self.sA[:, c * T + lo:c * T + lo + SL],
                         start=(c == 0), stop=(c == NDT - 1))
                xi = self.xt(i, lo, lo + SL)
                S.ts(self.sC[:, 0:SL], self.pA[:, lo:lo + SL],
                     self.sR[:, 16 + i:17 + i], None, ALU.add)
                S.tt(xi, xi, self.sC[:, 0:SL], ALU.add)
        self.tap("xres", self.x[:])
        self.postln()
        self.tap("xfin", self.x[:])


    def _rmsnorm_l3(self):
        S = self.S
        ns = self.inp("d_L3_ns_cols", [128, 8])
        S.dma(self.sR[:, 48:56], ns[:])
        inv = self.k_cols[:, 1:2]
        for i in range(NDT):
            S.act(self.sB[:, i * T:(i + 1) * T], self.xt(i), AF.Square)
        for tc in range(NTC):
            lo = tc * SL
            for i in range(NDT):
                S.mm(self.pA[0:1, lo:lo + SL], inv,
                     self.sB[:, i * T + lo:i * T + lo + SL],
                     start=(i == 0), stop=(i == NDT - 1))
        for tc in range(NTC):
            lo = tc * SL
            t1 = self.sF[0:1, 0:SL]
            S.act(t1, self.pA[0:1, lo:lo + SL], AF.Ln, bias=self.k_misc[0:1, 1:2])
            S.act(t1, t1, AF.Exp, scale=-0.5)
            self.bcast_row(self.pB[:, 0:SL], t1)
            for i in range(NDT):
                S.tt(self.sC[:, 0:SL], self.xt(i, lo, lo + SL),
                     self.pB[:, 0:SL], ALU.mult)
                S.ts(self.xnt(i, lo, lo + SL), self.sC[:, 0:SL],
                     self.sR[:, 48 + i:49 + i], None, ALU.mult)

    def _attention(self, layer):
        S = self.S
        l0 = (layer == 0)
        cosd = self.inp("c_cos_l0" if l0 else "c_cos_l3", [128, T], BF16)
        sind = self.inp("c_sin_l0" if l0 else "c_sin_l3", [128, T], BF16)
        S.dma(self.sB[:, 4096:6144], cosd[:])
        S.dma(self.sB[:, 6144:8192], sind[:])
        cos = self.sB[:, 4096:6144]
        sin = self.sB[:, 6144:8192]
        EOFF = 8192
        if l0:
            esh = self.inp("d_eshift_l0", [16, 128, 1536])
            tau = self.inp("d_tau_bc", [128, 16])
            rr = self.inp("c_rowrecip", [128, 16])
            S.dma(self.sR[:, 24:40], tau[:])
            S.dma(self.sR[:, 40:56], rr[:])
        else:
            e3 = self.inp("c_e3shift_bf", [128, 640], BF16)
            S.dma(self.sB[:, EOFF:EOFF + 640], e3[:])
        if l0:
            wq = self.inp("d_L0_WqT", [D, D])
            wk = self.inp("d_L0_WkT", [D, D])
            wv = self.inp("d_L0_WvT", [D, D])
            wo = self.inp("d_L0_WoT", [D, D])
        else:
            wq = self.inp("L3_attn_Wq", [D, D])
            wk = self.inp("L3_attn_Wk", [D, D])
            wv = self.inp("L3_attn_Wv", [D, D])
            wo = self.inp("L3_attn_Wo", [D, D])
        QOF, KOF, VOF = 16384, 18432, 20480
        for hp in range(8):
            if l0:
                for hh in range(2):
                    S.dma(self.sB[:, EOFF + hh * 1536:EOFF + (hh + 1) * 1536],
                          esh[2 * hp + hh], q="gpsimd")
            for (wap, dst) in ((wq, QOF), (wk, KOF)):
                self.load_wchunks(self.sW[:, 0:1024], wap, D, 128,
                                  col0=hp * 128, ncols=128)
                ov = self.sW[:, 1024:2048].rearrange("p (k h f c) -> p k h f c", k=8, h=2, f=2)
                iv = self.sW[:, 0:1024].rearrange("p (k h f c) -> p k h f c", k=8, h=2, f=2)
                S.act(ov[:, :, :, 0, :], iv[:, :, :, 1, :], AF.Copy, scale=-1.0)
                S.copy(ov[:, :, :, 1, :], iv[:, :, :, 0, :])
                for tc in range(NTC):
                    lo = tc * SL
                    for c in range(NDT):
                        S.mm(self.pA[:, lo:lo + SL],
                             self.sW[:, c * 128:(c + 1) * 128],
                             self.xnt(c, lo, lo + SL),
                             start=(c == 0), stop=(c == NDT - 1))
                    for c in range(NDT):
                        S.mm(self.pB[:, 0:SL],
                             self.sW[:, 1024 + c * 128:1024 + (c + 1) * 128],
                             self.xnt(c, lo, lo + SL),
                             start=(c == 0), stop=(c == NDT - 1))
                    S.tt(self.sC[:, 0:SL], self.pA[:, lo:lo + SL],
                         cos[:, lo:lo + SL], ALU.mult)
                    S.tt(self.sC[:, SL:2 * SL], self.pB[:, 0:SL],
                         sin[:, lo:lo + SL], ALU.mult)
                    S.tt(self.sA[:, dst + lo:dst + lo + SL],
                         self.sC[:, 0:SL], self.sC[:, SL:2 * SL], ALU.add)
            self.load_wchunks(self.sW[:, 0:1024], wv, D, 128,
                              col0=hp * 128, ncols=128)
            for tt in range(16):
                t0 = tt * 128
                for c in range(NDT):
                    S.mm(self.pC[:, 0:128], self.xnt(c, t0, t0 + 128),
                         self.sW[:, c * 128:(c + 1) * 128],
                         start=(c == 0), stop=(c == NDT - 1))
                S.copy(self.sA[:, VOF + t0:VOF + t0 + 128], self.pC[:, 0:128])
            if hp == 0:
                self.tap("q0", self.sA[:, QOF:QOF + T])
                self.tap("k0", self.sA[:, KOF:KOF + T])
                self.tap("v0", self.sA[:, VOF:VOF + T])
            for hh in range(2):
                hb = hh * 64
                hglob = 2 * hp + hh
                for qt in range(16):
                    q0 = qt * 128
                    KF = q0 + 128
                    nch = (KF + SL - 1) // SL
                    for cc in range(nch):
                        k0 = cc * SL
                        N = min(SL, KF - k0)
                        pre = (q0 - (k0 + N - 1) <= 512) if l0 else (k0 + N == KF)
                        if pre:
                            if l0:
                                off = 1023 + k0 - q0 + hh * 1536
                            else:
                                off = 511 + k0 - q0
                            S.mm(self.pA[:, k0:k0 + N], self.k_id_bf[:],
                                 self.sB[:, EOFF + off:EOFF + off + N],
                                 start=True, stop=False)
                        S.mm(self.pA[:, k0:k0 + N],
                             self.sA[hb:hb + 64, QOF + q0:QOF + q0 + 128],
                             self.sA[hb:hb + 64, KOF + k0:KOF + k0 + N],
                             start=not pre, stop=True)
                    den = self.sR[:, 56:57]
                    rd = self.sR[:, 57:58]
                    if hp == 0 and hh == 0 and qt == 5 and DEBUG_TAPS:
                        S.copy(self.sC[:, 0:KF], self.pA[:, 0:KF])
                        self.tap("sc5", self.sC[:, 0:KF])
                    S.act(self.sB[:, 0:KF], self.pA[:, 0:KF], AF.Exp,
                          accum_out=den)
                    if hp == 0 and hh == 0 and qt == 5:
                        self.tap("exp5", self.sB[:, 0:KF])
                        self.tap("den5", den)
                    S.recip(rd, den)
                    if l0:
                        c2 = self.sR[:, 58:59]
                        c2d = self.sR[:, 59:60]
                        S.tt(c2, self.sR[:, 24 + hglob:25 + hglob],
                             self.sR[:, 40 + qt:41 + qt], ALU.mult)
                        S.tt(c2d, c2, den, ALU.mult)
                        S.ts(self.sB[:, 0:KF], self.sB[:, 0:KF], c2d, 0.0,
                             ALU.add, ALU.max)
                    S.ts(self.sB[:, 0:KF], self.sB[:, 0:KF], rd, None, ALU.mult)
                    nkc = KF // 128
                    for g in range((nkc + 7) // 8):
                        b0 = g * 8
                        nb = min(8, nkc - b0)
                        for j in range(nb):
                            S.trans(self.pT[:, j * 128:(j + 1) * 128],
                                    self.sB[:, (b0 + j) * 128:(b0 + j + 1) * 128],
                                    self.k_id_bf[:])
                        S.copy(self.sB[:, 2048 + b0 * 128:2048 + (b0 + nb) * 128],
                               self.pT[:, 0:nb * 128])
                    for kc in range(nkc):
                        S.mm(self.pC[hb:hb + 64, 0:128],
                             self.sA[:, VOF + kc * 128 + hb:VOF + kc * 128 + hb + 64],
                             self.sB[:, 2048 + kc * 128:2048 + (kc + 1) * 128],
                             start=(kc == 0), stop=(kc == nkc - 1))
                    S.copy(self.sA[hb:hb + 64, hp * T + q0:hp * T + q0 + 128],
                           self.pC[hb:hb + 64, 0:128])
                    if hp == 0 and hh == 0 and qt == 5:
                        self.tap("eT5", self.sB[:, 2048:2048 + KF])
                        self.tap("av5", self.sA[0:64, q0:q0 + 128])
        for i in range(NDT):
            self.load_wchunks(self.sW[:, 0:1024], wo, D, 128,
                              col0=i * 128, ncols=128)
            for tc in range(NTC):
                lo = tc * SL
                for c in range(NDT):
                    S.mm(self.pA[:, lo:lo + SL], self.sW[:, c * 128:(c + 1) * 128],
                         self.sA[:, c * T + lo:c * T + lo + SL],
                         start=(c == 0), stop=(c == NDT - 1))
                xi = self.xt(i, lo, lo + SL)
                S.tt(xi, xi, self.pA[:, lo:lo + SL], ALU.add)
        self.postln()

    def l0_attn(self):
        self._attention(0)

    def l3_attn(self):
        self._rmsnorm_l3()
        self._attention(3)

    def l0_ffn(self):
        S = self.S
        wr = self.inp("d_L0f_wr_cols", [128, 8])
        b1c = self.inp("d_L0f_b1_cols", [128, 32])
        b2c = self.inp("d_L0f_b2_cols", [128, 8])
        W1 = self.inp("L0_ffn_W1", [D, 4096])
        W2 = self.inp("L0_ffn_W2", [4096, D])
        S.dma(self.sR[:, 96:104], wr[:])
        S.dma(self.sR[:, 184:216], b1c[:])
        S.dma(self.sR[:, 216:224], b2c[:])
        S.act(self.sR[:, 224:232], self.sR[:, 216:224], AF.Copy, scale=2.0)
        for tc in range(NTC):
            lo = tc * SL
            for c in range(NDT):
                S.mm(self.pA[0:1, lo:lo + SL], self.sR[:, 96 + c:97 + c],
                     self.xt(c, lo, lo + SL), start=(c == 0), stop=(c == NDT - 1))
        srow = self.sF[0:1, 0:T]
        S.acopy(srow, self.pA[0:1, 0:T])
        scr_s = self.scratch([T], F32)
        S.dma(scr_s, srow)
        for j in range(16):
            S.trans(self.pB[:, j:j + 1], srow[0:1, j * 128:(j + 1) * 128],
                    self.k_id_f32[0:1, 0:1])
        S.copy(self.sR[:, 104:120], self.pB[:, 0:16])
        sbrd = self.sC[:, 0:T]
        S.dma(sbrd, bass.AP(tensor=scr_s.tensor, offset=0, ap=[[0, 128], [1, T]]))
        self.tap("sbrd", sbrd)
        for j in range(16):
            S.ts(self.sB[:, 0:T], sbrd, self.sR[:, 104 + j:105 + j], 0.0,
                 ALU.is_gt, ALU.add, accum_out=self.sR[:, 120 + j:121 + j])
            if j == 0:
                self.tap("cmp0", self.sB[:, 0:T])
        S.tss(self.sR[:, 136:152], self.sR[:, 120:136], 1024.0, ALU.is_lt)
        S.act(self.sR[:, 152:168], self.sR[:, 104:120], AF.Sigmoid)
        S.tt(self.sR[:, 168:184], self.sR[:, 152:168], self.sR[:, 136:152],
             ALU.mult)
        S.act(self.sR[:, 168:184], self.sR[:, 168:184], AF.Copy, scale=0.5)
        scr_w = self.scratch([T], F32)
        S.trans(self.pB[0:16, 0:128], self.sR[:, 168:184], self.k_id_f32[:])
        S.copy(self.sF[0:16, 0:128], self.pB[0:16, 0:128])
        S.dma(scr_w.rearrange("(j p) -> j p", p=128), self.sF[0:16, 0:128])
        wtb = self.sB[:, 0:T]
        S.dma(wtb, bass.AP(tensor=scr_w.tensor, offset=0, ap=[[0, 128], [1, T]]),
              q="gpsimd")
        self.tap("srow", srow)
        self.tap("scol", self.sR[:, 104:120])
        self.tap("rank", self.sR[:, 120:136])
        self.tap("wt", self.sR[:, 168:184])
        self.tap("wtb", wtb)
        A0, B0 = 0.7978845608028654, 0.0356774081363001

        def h1sl(ft, lo, hi):
            if ft < 24:
                return self.sA[:, ft * 1024 + lo:ft * 1024 + hi]
            return self.sB[:, 2048 + (ft - 24) * 1024 + lo:2048 + (ft - 24) * 1024 + hi]

        for th in range(2):
            t0 = th * 1024
            for ft in range(32):
                self.load_wchunks(self.sW[:, 0:1024], W1, D, 128,
                                  col0=ft * 128, ncols=128)
                for tcc in range(2):
                    lo = t0 + tcc * SL
                    for c in range(NDT):
                        S.mm(self.pC[:, 0:SL], self.sW[:, c * 128:(c + 1) * 128],
                             self.xnt(c, lo, lo + SL),
                             start=(c == 0), stop=(c == NDT - 1))
                    b1 = self.sR[:, 184 + ft:185 + ft]
                    uf = self.sW[:, 1024:1536]
                    s2 = self.sW[:, 1536:2048]
                    vv = self.sW[:, 2048:2560]
                    ww = self.sW[:, 2560:3072]
                    S.ts(uf, self.pC[:, 0:SL], b1, None, ALU.add)
                    S.act(s2, self.pC[:, 0:SL], AF.Square, bias=b1)
                    S.ts(vv, s2, B0, A0, ALU.mult, ALU.add)
                    S.tt(ww, vv, uf, ALU.mult)
                    S.act(vv, ww, AF.Tanh)
                    S.stt(h1sl(ft, tcc * SL, tcc * SL + SL), vv, 1.0, uf,
                          ALU.add, ALU.mult)
            if th == 0:
                self.tap("h1_f0", h1sl(0, 0, 1024))
            for dt in range(NDT):
                for kh in range(2):
                    src = W2.rearrange("(c p) o -> p c o", p=128)
                    S.dma(self.sW[:, 0:2048],
                          src[:, kh * 16:(kh + 1) * 16, dt * 128:(dt + 1) * 128],
                          q="gpsimd")
                    for tcc in range(2):
                        ptile = self.pC[:, 0:SL] if tcc == 0 else self.pB[:, 0:SL]
                        for c in range(16):
                            ft = kh * 16 + c
                            S.mm(ptile, self.sW[:, c * 128:(c + 1) * 128],
                                 h1sl(ft, tcc * SL, tcc * SL + SL),
                                 start=(kh == 0 and c == 0),
                                 stop=(kh == 1 and c == 15))
                for tcc in range(2):
                    lo = t0 + tcc * SL
                    ptile = self.pC[:, 0:SL] if tcc == 0 else self.pB[:, 0:SL]
                    tv = self.sC[:, 0:SL]
                    t2 = self.sC[:, SL:2 * SL]
                    S.tt(tv, ptile, wtb[:, lo:lo + SL], ALU.mult)
                    S.stt(t2, wtb[:, lo:lo + SL], self.sR[:, 224 + dt:225 + dt],
                          tv, ALU.mult, ALU.add)
                    xi = self.xt(dt, lo, lo + SL)
                    S.stt(xi, xi, 2.0, t2, ALU.mult, ALU.add)
        self.postln()

    def l3_ffn(self):
        S = self.S
        rw = self.inp("L3_ffn_router_w", [D, 8])
        rb = self.inp("L3_ffn_router_b", [8])
        W1 = self.inp("L3_ffn_W1", [8, D, 512])
        W2 = self.inp("L3_ffn_W2", [8, 512, D])
        b1c = self.inp("d_L3f_b1_cols", [8, 128, 4])
        b2 = self.inp("L3_ffn_b2", [8, D])
        self.load_wchunks(self.sW[:, 0:64], rw, D, 8, col0=0, ncols=8)
        S.dma(self.sR[0:8, 0:1], rb.rearrange("(e o) -> e o", o=1))
        for tc in range(NTC):
            lo = tc * SL
            for c in range(NDT):
                S.mm(self.pB[0:8, 0:SL], self.sW[:, c * 8:(c + 1) * 8],
                     self.xnt(c, lo, lo + SL), start=(c == 0), stop=(c == NDT - 1))
            S.ts(self.sF[0:8, lo:lo + SL], self.pB[0:8, 0:SL],
                 self.sR[0:8, 0:1], None, ALU.add)
        for j in range(16):
            S.trans(self.pA[:, j * 8:(j + 1) * 8],
                    self.sF[0:8, j * 128:(j + 1) * 128], self.k_id_f32[0:8, 0:8])
        lgT = self.sC[:, 0:128]
        S.copy(lgT, self.pA[:, 0:128])
        v3 = lambda ap: ap.rearrange("p (j e) -> p j e", e=8)
        m1 = self.sC[:, 384:400]
        m2 = self.sC[:, 400:416]
        s12 = self.sC[:, 416:432]
        w1 = self.sC[:, 128:256]
        w2 = self.sC[:, 256:384]
        gm = self.sC[:, 432:560]
        rmt = self.sC[:, 560:688]
        S.red(m1, v3(lgT), ALU.max)
        m1b = m1.unsqueeze(2).broadcast_to((128, 16, 8))
        S.tt(v3(w1), v3(lgT), m1b, ALU.subtract)
        S.act(gm, w1, AF.Exp)
        S.tt(v3(w1), v3(lgT), m1b, ALU.is_equal)
        S.stt(w2, w1, -1.0e30, lgT, ALU.mult, ALU.add)
        S.red(m2, v3(w2), ALU.max)
        m2b = m2.unsqueeze(2).broadcast_to((128, 16, 8))
        S.tt(v3(w2), v3(w2), m2b, ALU.is_equal)
        S.tt(w1, w1, w2, ALU.add)
        S.tt(gm, gm, w1, ALU.mult)
        S.red(s12, v3(gm), ALU.add)
        S.recip(s12, s12)
        s12b = s12.unsqueeze(2).broadcast_to((128, 16, 8))
        S.tt(v3(rmt), v3(gm), s12b, ALU.mult)
        for j in range(16):
            S.trans(self.pA[0:8, j * 128:(j + 1) * 128],
                    rmt[:, j * 8:(j + 1) * 8], self.k_id_f32[:])
        S.copy(self.sB[0:8, 0:T], self.pA[0:8, 0:T])
        scr_rm = self.scratch([8, T], BF16)
        S.dma(scr_rm, self.sB[0:8, 0:T])
        S.dma(self.sB[0:8, T:T + D], b2, q="gpsimd")
        for e in range(8):
            S.dma(self.sR[:, 64 + e * 4:68 + e * 4], b1c[e])
        for th in range(2):
            t0 = th * 1024
            for e in range(8):
                rbc = self.sB[:, 2048 + 8 * 1024:2048 + 9 * 1024]
                S.dma(rbc, bass.AP(tensor=scr_rm.tensor, offset=e * T + t0,
                                   ap=[[0, 128], [1, 1024]]))
                for fq in range(4):
                    src = W1.rearrange("e (c p) f -> e p c f", p=128)
                    S.dma(self.sW[:, 0:1024],
                          src[e][:, :, fq * 128:(fq + 1) * 128], q="gpsimd")
                    for tcc in range(2):
                        lo = t0 + tcc * SL
                        for c in range(NDT):
                            S.mm(self.pC[:, 0:SL], self.sW[:, c * 128:(c + 1) * 128],
                                 self.xnt(c, lo, lo + SL),
                                 start=(c == 0), stop=(c == NDT - 1))
                        tmp = self.sW[:, 1024:1536]
                        S.act(tmp, self.pC[:, 0:SL], AF.Gelu,
                              bias=self.sR[:, 64 + e * 4 + fq:65 + e * 4 + fq])
                        S.tt(self.sA[:, fq * 1024 + tcc * SL:fq * 1024 + tcc * SL + SL],
                             tmp, rbc[:, tcc * SL:tcc * SL + SL], ALU.mult)
                for dtp in range(4):
                    src2 = W2.rearrange("e (c p) o -> e p c o", p=128)
                    S.dma(self.sW[:, 0:1024],
                          src2[e][:, :, dtp * 256:(dtp + 1) * 256], q="gpsimd")
                    for dh in range(2):
                        dt = dtp * 2 + dh
                        for tcc in range(2):
                            lo = t0 + tcc * SL
                            for c in range(4):
                                S.mm(self.pC[:, 0:SL],
                                     self.sW[:, c * 256 + dh * 128:c * 256 + dh * 128 + 128],
                                     self.sA[:, c * 1024 + tcc * SL:c * 1024 + tcc * SL + SL],
                                     start=(c == 0), stop=(c == 3))
                            xi = self.xt(dt, lo, lo + SL)
                            S.tt(xi, xi, self.pC[:, 0:SL], ALU.add)
            for dt in range(NDT):
                for tcc in range(2):
                    lo = t0 + tcc * SL
                    S.mm(self.pC[:, 0:SL],
                         self.sB[0:8, T + dt * 128:T + (dt + 1) * 128],
                         self.sB[0:8, lo:lo + SL], start=True, stop=True)
                    xi = self.xt(dt, lo, lo + SL)
                    S.tt(xi, xi, self.pC[:, 0:SL], ALU.add)
        self.postln()


    def _h1sl(self, ft, lo, hi):
        if ft < 24:
            return self.sA[:, ft * 1024 + lo:ft * 1024 + hi]
        return self.sB[:, (ft - 24) * 1024 + lo:(ft - 24) * 1024 + hi]

    def l1_ffn(self):
        S = self.S
        Wexp = self.inp("L1_ffn_W_exp", [D, 4096])
        bexp = self.inp("d_bexp_cols", [128, 32])
        upb = self.inp("d_upbiasP_cols", [128, 32])
        WconP = self.inp("d_W_conP", [4096, D])
        bcon = self.inp("d_bcon_cols", [128, 8])
        dbias = self.inp("d_dbias_cols", [128, 8])
        S.dma(self.sR[:, 64:96], bexp[:])
        S.dma(self.sR[:, 96:128], upb[:])
        S.dma(self.sR[:, 128:136], bcon[:])
        S.dma(self.sR[:, 136:144], dbias[:])
        S.tt(self.sR[:, 128:136], self.sR[:, 128:136], self.sR[:, 136:144], ALU.add)
        CTOF = {"up1": 8192, "up2": 10240, "dn1": 12288, "dn2": 14336}
        for gname in ("up1", "up2", "dn1", "dn2"):
            fac = self.inp(f"d_bf_{gname}", [128, 12288])
            S.dma(self.sA[:, 0:12288], fac[:], q="gpsimd")
            f = lambda s: self.sA[:, s * 2048:(s + 1) * 2048]
            pong = self.sA[:, 12288:14336]
            ping = self.sW[:, 0:2048]
            S.tt(pong, f(0), f(1), ALU.mult)
            S.tt(ping, pong, f(2), ALU.mult)
            S.tt(pong, ping, f(3), ALU.mult)
            S.tt(ping, pong, f(4), ALU.mult)
            S.tt(self.sB[:, CTOF[gname]:CTOF[gname] + 2048], ping, f(5), ALU.mult)

        def apply_group(gname, t0, gelu=False):
            off = CTOF[gname]
            for m in range(32):
                for tcc in range(2):
                    lo = t0 + tcc * SL
                    hl = self._h1sl(m, (lo - t0), (lo - t0) + SL)
                    for kb in range(2):
                        S.mm(self.pC[kb * 64:kb * 64 + 64, 0:SL],
                             self.sB[kb * 64:kb * 64 + 64, off + m * 64:off + (m + 1) * 64],
                             self._h1sl(m, (lo - t0), (lo - t0) + SL)[kb * 64:kb * 64 + 64, :],
                             start=True, stop=True)
                    if gelu:
                        S.act(hl, self.pC[:, 0:SL], AF.Gelu,
                              bias=self.sR[:, 96 + m:97 + m])
                    else:
                        S.copy(hl, self.pC[:, 0:SL])

        scr_h = self.scratch([4096, 1024], BF16)

        def permute(t0):
            S.dma(scr_h.rearrange("(i p) t -> p i t", p=128)[:, 0:24, :],
                  self.sA[:, 0:24576])
            S.dma(scr_h.rearrange("(i p) t -> p i t", p=128)[:, 24:32, :],
                  self.sB[:, 0:8192])
            for p1 in range(2):
                S.dma(self.sA[p1 * 64:(p1 + 1) * 64, 0:24576]
                      .rearrange("p (i t) -> p i t", t=1024),
                      bass.AP(tensor=scr_h.tensor, offset=p1 * 1024,
                              ap=[[65536, 64], [2048, 24], [1, 1024]]))
                S.dma(self.sB[p1 * 64:(p1 + 1) * 64, 0:8192]
                      .rearrange("p (i t) -> p i t", t=1024),
                      bass.AP(tensor=scr_h.tensor, offset=p1 * 1024 + 48 * 1024,
                              ap=[[65536, 64], [2048, 8], [1, 1024]]))

        for th in range(2):
            t0 = th * 1024
            for ft in range(32):
                self.load_wchunks(self.sW[:, 0:1024], Wexp, D, 128,
                                  col0=ft * 128, ncols=128)
                for tcc in range(2):
                    lo = t0 + tcc * SL
                    for c in range(NDT):
                        S.mm(self.pC[:, 0:SL], self.sW[:, c * 128:(c + 1) * 128],
                             self.xnt(c, lo, lo + SL),
                             start=(c == 0), stop=(c == NDT - 1))
                    S.ts(self._h1sl(ft, tcc * SL, tcc * SL + SL), self.pC[:, 0:SL],
                         self.sR[:, 64 + ft:65 + ft], None, ALU.add)
            apply_group("up1", t0)
            permute(t0)
            apply_group("up2", t0, gelu=True)
            permute(t0)
            apply_group("dn1", t0)
            permute(t0)
            apply_group("dn2", t0)
            for dt in range(NDT):
                for kh in range(2):
                    src = WconP.rearrange("(c p) o -> p c o", p=128)
                    S.dma(self.sW[:, 0:2048],
                          src[:, kh * 16:(kh + 1) * 16, dt * 128:(dt + 1) * 128],
                          q="gpsimd")
                    for tcc in range(2):
                        ptile = self.pC[:, 0:SL] if tcc == 0 else self.pB[:, 0:SL]
                        for c in range(16):
                            S.mm(ptile, self.sW[:, c * 128:(c + 1) * 128],
                                 self._h1sl(kh * 16 + c, tcc * SL, tcc * SL + SL),
                                 start=(kh == 0 and c == 0),
                                 stop=(kh == 1 and c == 15))
                for tcc in range(2):
                    lo = t0 + tcc * SL
                    ptile = self.pC[:, 0:SL] if tcc == 0 else self.pB[:, 0:SL]
                    S.ts(self.sC[:, 0:SL], ptile, self.sR[:, 128 + dt:129 + dt],
                         None, ALU.add)
                    xi = self.xt(dt, lo, lo + SL)
                    S.tt(xi, xi, self.sC[:, 0:SL], ALU.add)
        self.postln()

    def _quant(self, dst, src, width, btmp):
        S = self.S
        nb = width // 16
        mx = self.sR[:, 320:320 + nb]
        sc = self.sR[:, 408:408 + nb]
        B1, B2, B3 = btmp
        v = self.sF[:, 0:width]
        ftmp = self.sF[:, width:2 * width]
        vview = lambda ap: ap.rearrange("p (b g) -> p b g", g=16)
        S.red(mx[:, 0:nb], vview(src), ALU.max, absval=True)
        S.ts(sc[:, 0:nb], mx[:, 0:nb], 1.0 / 6.0, 1e-12, ALU.mult, ALU.max)
        S.recip(sc[:, 0:nb], sc[:, 0:nb])
        S.act(v, src, AF.Abs)
        rcb = sc[:, 0:nb].unsqueeze(2).broadcast_to((128, nb, 16))
        S.tt(vview(v), vview(v), rcb, ALU.mult)
        S.ts(ftmp, v, 2.25, 2.0, ALU.min, ALU.mult)
        S.ts(B2, ftmp, MAGIC, MAGIC, ALU.add, ALU.subtract)
        S.ts(ftmp, v, 2.25, 5.0, ALU.max, ALU.min)
        S.ts(B3, ftmp, MAGIC, MAGIC, ALU.add, ALU.subtract)
        S.ts(B1, B3, 4.0, -2.0, ALU.min, ALU.add)
        S.stt(B3, B2, 0.5, B1, ALU.mult, ALU.add)
        S.ts(B1, v, 5.0, 2.0, ALU.is_gt, ALU.mult)
        S.tt(B2, B3, B1, ALU.add)
        S.act(B1, src, AF.Sign)
        S.tt(B3, B2, B1, ALU.mult)
        S.recip(sc[:, 0:nb], sc[:, 0:nb])
        scb = sc[:, 0:nb].unsqueeze(2).broadcast_to((128, nb, 16))
        S.tt(vview(dst), vview(B3), scb, ALU.mult)

    def l2_ffn(self):
        S = self.S
        wg = self.inp("L2_ffn_w_gate", [2816, D])
        wu = self.inp("L2_ffn_w_up", [2816, D])
        wd = self.inp("L2_ffn_w_down", [D, 2816])
        scr_h = self.scratch([2816, T], BF16)
        scr_hq = self.scratch([T, 2816], BF16)
        scr_wdT = self.scratch([2816, D], BF16)
        btmp1k = (self.sB[:, 1024:2048], self.sB[:, 2048:3072], self.sB[:, 3072:4096])
        for tt in range(16):
            t0 = tt * 128
            for i in range(NDT):
                S.trans(self.pA[:, i * 128:(i + 1) * 128],
                        self.xt(i, t0, t0 + 128), self.k_id_f32[:])
            S.copy(self.sC[:, 0:1024], self.pA[:, 0:1024])
            self._quant(self.sB[:, 0:1024], self.sC[:, 0:1024], 1024, btmp1k)
            for i in range(NDT):
                S.trans(self.pT[:, i * 128:(i + 1) * 128],
                        self.sB[:, i * 128:(i + 1) * 128], self.k_id_bf[:])
            for i in range(NDT):
                S.copy(self.xnt(i, t0, t0 + 128), self.pT[:, i * 128:(i + 1) * 128])
        for rt in range(22):
            r0 = rt * 128
            for (wap, slot) in ((wg, 4096), (wu, 5120)):
                S.dma(self.sC[:, 0:1024], wap[r0:r0 + 128, :])
                self._quant(self.sB[:, 0:1024], self.sC[:, 0:1024], 1024, btmp1k)
                for i in range(NDT):
                    S.trans(self.pT[:, i * 128:(i + 1) * 128],
                            self.sB[:, i * 128:(i + 1) * 128], self.k_id_bf[:])
                S.copy(self.sB[:, slot:slot + 1024], self.pT[:, 0:1024])
            for tc in range(NTC):
                lo = tc * SL
                for c in range(NDT):
                    S.mm(self.pC[:, 0:SL], self.sB[:, 4096 + c * 128:4096 + (c + 1) * 128],
                         self.xnt(c, lo, lo + SL), start=(c == 0), stop=(c == NDT - 1))
                for c in range(NDT):
                    S.mm(self.pB[:, 0:SL], self.sB[:, 5120 + c * 128:5120 + (c + 1) * 128],
                         self.xnt(c, lo, lo + SL), start=(c == 0), stop=(c == NDT - 1))
                S.act(self.sW[:, 0:SL], self.pC[:, 0:SL], AF.Silu)
                S.tt(self.sW[:, SL:2 * SL], self.sW[:, 0:SL], self.pB[:, 0:SL],
                     ALU.mult)
                S.dma(scr_h[r0:r0 + 128, lo:lo + SL], self.sW[:, SL:2 * SL])
        wdv = scr_wdT.rearrange("(kb p) o -> p kb o", p=128)
        btmp14 = (self.sB[:, 1408:2816], self.sB[:, 2816:4224], self.sB[:, 4224:5632])
        for rt in range(NDT):
            for ch in range(2):
                wcols = 1408
                S.dma(self.sC[:, 0:wcols], wd[rt * 128:(rt + 1) * 128,
                                              ch * wcols:(ch + 1) * wcols])
                self._quant(self.sB[:, 0:wcols], self.sC[:, 0:wcols], wcols, btmp14)
                for g in range(2):
                    nblk = 8 if g == 0 else 3
                    for j in range(nblk):
                        blk = g * 8 + j
                        S.trans(self.pT[:, j * 128:(j + 1) * 128],
                                self.sB[:, blk * 128:(blk + 1) * 128], self.k_id_bf[:])
                    S.copy(self.sB[:, 5632 + g * 1024:5632 + g * 1024 + nblk * 128],
                           self.pT[:, 0:nblk * 128])
                S.dma(wdv[:, ch * 11:(ch + 1) * 11, rt * 128:(rt + 1) * 128],
                      self.sB[:, 5632:5632 + 1408])
        for tt in range(16):
            t0 = tt * 128
            S.dmaT(self.sW[:, 0:2816], scr_h[:, t0:t0 + 128])
            self._quant(self.sB[:, 0:1408], self.sW[:, 0:1408], 1408, btmp14)
            self._quant(self.sB[:, 1408:2816] if False else self.sB[:, 5632:7040],
                        self.sW[:, 1408:2816], 1408, btmp14)
            S.dma(scr_hq[t0:t0 + 128, 0:1408], self.sB[:, 0:1408])
            S.dma(scr_hq[t0:t0 + 128, 1408:2816], self.sB[:, 5632:7040])
        wdl = scr_wdT.rearrange("(c p) o -> p c o", p=128)
        for kh in range(2):
            for j in range(11):
                kc = kh * 11 + j
                S.dmaT(self.sA[:, j * T:(j + 1) * T], scr_hq[:, kc * 128:(kc + 1) * 128])
            for dt in range(NDT):
                S.dma(self.sW[:, 0:1408],
                      wdl[:, kh * 11:(kh + 1) * 11, dt * 128:(dt + 1) * 128],
                      q="sync")
                for tc in range(NTC):
                    lo = tc * SL
                    for c in range(11):
                        S.mm(self.pC[:, 0:SL], self.sW[:, c * 128:(c + 1) * 128],
                             self.sA[:, c * T + lo:c * T + lo + SL],
                             start=(c == 0), stop=(c == 10))
                    xi = self.xt(dt, lo, lo + SL)
                    S.tt(xi, xi, self.pC[:, 0:SL], ALU.add)
        self.postln()

    def _ut(self, slab, lo, hi):
        if slab < 12:
            return self.sA[:, slab * T + lo:slab * T + hi]
        return self.sB[:, (slab - 12) * T + lo:(slab - 12) * T + hi]

    def l1_attn(self, upto=9):
        S = self.S
        Win = self.inp("L1_attn_W_in", [D, 4096])
        Wx = self.inp("L1_attn_W_x", [2048, 96])
        Wdt = self.inp("L1_attn_W_dt", [64, 2048])
        Wout = self.inp("L1_attn_W_out", [2048, D])
        convc = self.inp("d_L1_conv_cols", [128, 64])
        convb = self.inp("d_L1_convb_cols", [128, 16])
        bdt = self.inp("d_L1_bdt_cols", [128, 16])
        Dcol = self.inp("d_L1_D_cols", [128, 16])
        alog = self.inp("d_L1_alog_cols", [128, 256])
        S.dma(self.sR[:, 0:64], convc[:])
        S.dma(self.sR[:, 64:80], convb[:])
        S.dma(self.sR[:, 80:96], bdt[:])
        S.dma(self.sR[:, 96:112], Dcol[:])
        S.dma(self.sR[:, 256:512], alog[:])
        S.act(self.sC[:, 0:256], self.sR[:, 256:512], AF.Exp)
        S.ts(self.sR[:, 256:512], self.sC[:, 0:256], -1.0, None, ALU.mult)
        scr_x = self.scratch([128, NDT * T], F32)
        scr_res = self.scratch([2048, T], BF16)
        scr_B = self.scratch([16, T], BF16)
        scr_C = self.scratch([16, T], BF16)
        scr_y2 = self.scratch([2048, T], BF16)
        S.dma(scr_x, self.x[:])
        dlt_all = self.x[:].bitcast(BF16)
        for ft in range(32):
            self.load_wchunks(self.sB[:, 8192:9216], Win, D, 128,
                              col0=ft * 128, ncols=128)
            for tc in range(NTC):
                lo = tc * SL
                for c in range(NDT):
                    S.mm(self.pA[:, lo:lo + SL],
                         self.sB[:, 8192 + c * 128:8192 + (c + 1) * 128],
                         self.xnt(c, lo, lo + SL),
                         start=(c == 0), stop=(c == NDT - 1))
            if ft < 16:
                S.memset(self.sW[:, 0:3], 0.0)
                S.copy(self.sW[:, 3:2051], self.pA[:, 0:T])
                wc = lambda i: self.sR[:, ft * 4 + i:ft * 4 + i + 1]
                S.ts(self.sC[:, 0:T], self.sW[:, 0:T], wc(0), None, ALU.mult)
                S.stt(self.sF[:, 0:2048], self.sW[:, 1:2049], wc(1),
                      self.sC[:, 0:T], ALU.mult, ALU.add)
                S.stt(self.sC[:, 0:T], self.sW[:, 2:2050], wc(2),
                      self.sF[:, 0:2048], ALU.mult, ALU.add)
                S.stt(self.sF[:, 0:2048], self.sW[:, 3:2051], wc(3),
                      self.sC[:, 0:T], ALU.mult, ALU.add)
                S.act(self._ut(ft, 0, T), self.sF[:, 0:2048], AF.Silu,
                      bias=self.sR[:, 64 + ft:65 + ft])
            else:
                S.copy(self.sB[:, 9216:11264], self.pA[:, 0:T])
                S.dma(scr_res[(ft - 16) * 128:(ft - 15) * 128, :],
                      self.sB[:, 9216:11264])
        if upto < 2:
            return
        self.load_wchunks(self.sB[:, 8192:9728], Wx, 2048, 96, col0=0, ncols=96)
        for tc in range(NTC):
            lo = tc * SL
            for c in range(16):
                S.mm(self.pB[0:96, 0:SL], self.sB[:, 8192 + c * 96:8192 + (c + 1) * 96],
                     self._ut(c, lo, lo + SL), start=(c == 0), stop=(c == 15))
            S.copy(self.sB[0:64, 11264 + lo:11264 + lo + SL], self.pB[0:64, 0:SL])
            S.copy(self.sB[64:96, 13312 + lo:13312 + lo + SL], self.pB[64:96, 0:SL])
        S.dma(scr_B, self.sB[64:80, 13312:15360])
        S.dma(scr_C, self.sB[80:96, 13312:15360])
        if upto < 3:
            return
        S.dma(self.sB[0:64, 13312:15360], Wdt, q="gpsimd")
        for slab in range(16):
            for tc in range(NTC):
                lo = tc * SL
                S.mm(self.pC[:, 0:SL],
                     self.sB[0:64, 13312 + slab * 128:13312 + (slab + 1) * 128],
                     self.sB[0:64, 11264 + lo:11264 + lo + SL],
                     start=True, stop=True)
                bc = self.sR[:, 80 + slab:81 + slab]
                if upto < 3.2:
                    continue
                S.act(self.sC[:, 0:SL], self.pC[:, 0:SL], AF.Abs, bias=bc)
                S.act(self.sC[:, SL:2 * SL], self.sC[:, 0:SL], AF.Exp, scale=-1.0)
                S.act(self.sC[:, 0:SL], self.sC[:, SL:2 * SL], AF.Ln, bias=1.0)
                if upto < 3.4:
                    continue
                S.ts(self.sF[:, 0:SL], self.pC[:, 0:SL], bc, 0.0, ALU.add, ALU.max)
                if upto < 3.6:
                    continue
                S.tt(dlt_all[:, slab * T + lo:slab * T + lo + SL],
                     self.sF[:, 0:SL], self.sC[:, 0:SL], ALU.add)
        if upto < 5:
            return
        sCbf = self.sC[:].bitcast(BF16)
        carry = self.sF[:, 0:256]
        S.memset(carry, 0.0)
        for te in range(4):
            t0 = te * SL
            S.dma(self.xn[:, 0:8192],
                  bass.AP(tensor=scr_B.tensor, offset=t0,
                          ap=[[0, 128], [T, 16], [1, SL]]))
            S.dma(self.xn[:, 8192:16384],
                  bass.AP(tensor=scr_C.tensor, offset=t0,
                          ap=[[0, 128], [T, 16], [1, SL]]))
            for slab in range(16):
                du = self.sW[:, 0:SL]
                S.tt(du, dlt_all[:, slab * T + t0:slab * T + t0 + SL],
                     self._ut(slab, t0, t0 + SL), ALU.mult)
                S.ts(self.pA[:, 0:SL], self._ut(slab, t0, t0 + SL),
                     self.sR[:, 96 + slab:97 + slab], None, ALU.mult)
                for n in range(16):
                    a = sCbf[:, 0:SL]
                    b = sCbf[:, SL:2 * SL]
                    h = sCbf[:, 2 * SL:3 * SL]
                    g = sCbf[:, 3 * SL:4 * SL]
                    S.act(a, dlt_all[:, slab * T + t0:slab * T + t0 + SL],
                          AF.Exp, scale=self.sR[:, 256 + slab * 16 + n:257 + slab * 16 + n])
                    S.tt(b, du, self.xn[:, n * SL:(n + 1) * SL], ALU.mult)
                    S.scan(h, a, b, carry[:, slab * 16 + n:slab * 16 + n + 1],
                           ALU.mult, ALU.add)
                    S.copy(carry[:, slab * 16 + n:slab * 16 + n + 1],
                           h[:, SL - 1:SL])
                    S.tt(g, h, self.xn[:, 8192 + n * SL:8192 + (n + 1) * SL],
                         ALU.mult)
                    S.tt(self.pA[:, 0:SL], self.pA[:, 0:SL], g, ALU.add)
                S.dma(sCbf[:, 0:SL],
                      scr_res[slab * 128:(slab + 1) * 128, t0:t0 + SL])
                S.act(sCbf[:, SL:2 * SL], sCbf[:, 0:SL], AF.Silu)
                S.tt(sCbf[:, 2 * SL:3 * SL], self.pA[:, 0:SL],
                     sCbf[:, SL:2 * SL], ALU.mult)
                S.dma(scr_y2[slab * 128:(slab + 1) * 128, t0:t0 + SL],
                      sCbf[:, 2 * SL:3 * SL])
        if upto < 6:
            return
        S.dma(self.x[:], scr_x)
        for slab in range(16):
            S.dma(self._ut(slab, 0, T), scr_y2[slab * 128:(slab + 1) * 128, :])
        for dt in range(NDT):
            self.load_wchunks(self.sW[:, 0:2048], Wout, 2048, 128,
                              col0=dt * 128, ncols=128)
            for tc in range(NTC):
                lo = tc * SL
                for c in range(16):
                    S.mm(self.pC[:, 0:SL], self.sW[:, c * 128:(c + 1) * 128],
                         self._ut(c, lo, lo + SL), start=(c == 0), stop=(c == 15))
                xi = self.xt(dt, lo, lo + SL)
                S.tt(xi, xi, self.pC[:, 0:SL], ALU.add)
        self.postln()



_SUBLAYER_FN = {0: "l0_attn", 1: "l0_ffn", 2: "l1_attn", 3: "l1_ffn",
                4: "l2_attn", 5: "l2_ffn", 6: "l3_attn", 7: "l3_ffn"}

_cache = {}


def _build_group(g):
    key = tuple(g)
    p = Prog(g)
    p.load_consts()
    p.prelude()
    for s in g:
        getattr(p, _SUBLAYER_FN[s])()
    p.epilogue()
    sem = p.nc.semaphore("gs").__enter__()
    p.S.build(sem)
    return p


def kernel(**inputs):
    consts = make_consts()
    derived = make_derived(inputs)
    pool = {}
    pool.update(consts)
    pool.update({k: np.asarray(v) for k, v in derived.items()})
    for k, v in inputs.items():
        pool[k] = np.asarray(v)
    x = np.asarray(inputs["L0_attn_x"], dtype=np.float32)
    xcur = [np.ascontiguousarray(x[c].T) for c in range(8)]
    for g in FUSE_GROUPS:
        p = _build_group(g)
        base = {name: pool[name] for name in p.used if name != "xin"}
        in_maps = [dict(base, xin=xcur[c]) for c in range(8)]
        res = run_bass_kernel_spmd(p.nc, in_maps, core_ids=list(range(8)))
        xcur = [np.asarray(res.results[c]["xout"]) for c in range(8)]
    return np.stack([xc.T for xc in xcur]).astype(np.float32)
